# version 19
# baseline (speedup 1.0000x reference)
import contextlib
import numpy as np

import concourse.bass as bass
import concourse.mybir as mybir
from concourse.bass_utils import run_bass_kernel_spmd

F32 = mybir.dt.float32
BF16 = mybir.dt.bfloat16
I32 = mybir.dt.int32
U8 = mybir.dt.uint8
AF = mybir.ActivationFunctionType
ALU = mybir.AluOpType
AX = mybir.AxisListType

T = 8192
DM = 1024
H = 16
CH = 64
CC = 8
TCH = 512
NCHUNK = T // TCH
NT1 = TCH // 128

ENGS = ("sync", "vector", "scalar", "tensor", "gpsimd")
DRIVE_IN_SCORES = True
DRAIN_STATS = []
TEST_STAGE = 3
QUAKE = 0x5F3759DF


class Rec:

    def __init__(self, nc, es):
        self.nc = nc
        self.items = {e: [] for e in ENGS}
        self.semh = {}
        self.semc = {}
        for name in ("sV", "sA", "sT", "sG", "sS", "sD", "sD2"):
            self.semh[name] = es.enter_context(nc.semaphore(name))
            self.semc[name] = 0
        self.eng_sem = {"vector": "sV", "scalar": "sA", "tensor": "sT",
                        "gpsimd": "sG", "sync": "sS"}

    def op(self, eng, fn):
        self.items[eng].append(["op", fn, None, None])

    def mark(self, eng, sem=None, n=1):
        sem = sem or self.eng_sem[eng]
        items = self.items[eng]
        for i in range(len(items) - 1, -1, -1):
            kind = items[i][0]
            if kind == "op":
                self.semc[sem] += n
                items[i] = ["opinc", items[i][1], self.semh[sem], n]
                return (sem, self.semc[sem])
            if kind == "opinc":
                raise RuntimeError(f"mark({eng}): last op already marked")
        raise RuntimeError(f"mark({eng}): no op to mark")

    def wait(self, eng, tok):
        if tok is None:
            return
        self.items[eng].append(["wait", None, self.semh[tok[0]], tok[1]])

    def waits(self, eng, toks):
        best = {}
        for t in toks:
            if t is None:
                continue
            if t[0] not in best or t[1] > best[t[0]]:
                best[t[0]] = t[1]
        for s, v in best.items():
            self.items[eng].append(["wait", None, self.semh[s], v])

    def chain(self, eng):
        tok = self.mark(eng)
        self.wait(eng, tok)
        return tok

    def emit(self, block):
        for ename in ENGS:
            sec = getattr(block, ename)
            items = self.items[ename]

            def body(e, items=items):
                for kind, fn, sem, val in items:
                    if kind == "op":
                        fn(e)
                    elif kind == "opinc":
                        fn(e).then_inc(sem, val)
                    else:
                        e.wait_ge(sem, val)

            sec(body)


def build_program():
    nc = bass.Bass("TRN2", target_bir_lowering=False)

    x_in = nc.dram_tensor("x", [T, DM], F32, kind="ExternalInput").ap()
    out = nc.dram_tensor("out", [T, DM], F32, kind="ExternalOutput").ap()
    wsrcs = {}
    for name in ("wq", "wk", "wv", "wo", "wr", "wkf", "wvf"):
        wsrcs[name] = nc.dram_tensor(name, [DM, DM], F32, kind="ExternalInput").ap()
    bsrcs = {}
    for name in ("bq", "bk", "bv", "bo", "mur", "muk"):
        bsrcs[name] = nc.dram_tensor(name, [DM], F32, kind="ExternalInput").ap()

    wbf = nc.dram_tensor("wbf", [7, 128, CC * DM], BF16, kind="Internal").ap()
    W_RT, W_KT, W_VT, W_Q, W_K, W_V, W_O = range(7)

    es = contextlib.ExitStack()
    with es:
        sb = lambda name, shape, dt: es.enter_context(nc.sbuf_tensor(name, shape, dt))
        psal = lambda name, shape, dt: es.enter_context(nc.psum_tensor(name, shape, dt))

        slots = [sb(f"wslot{i}", [128, CC * DM], BF16) for i in range(4)]
        S_WO = 3

        big = sb("big", [128, 16384], U8)
        stgf = [big[:, 0:4096].bitcast(F32), big[:, 4096:8192].bitcast(F32)]
        stgbf = [big[:, 8192:10240].bitcast(BF16), big[:, 10240:12288].bitcast(BF16)]
        tstage = big[:, 12288:14336].bitcast(BF16)
        brow_f32 = big[:, 0:4096].bitcast(F32)
        s_tiles = [big[:, 0:8192].bitcast(BF16), big[:, 8192:16384].bitcast(BF16)]

        e_tiles = [sb("e_tile0", [128, H * 4 * CH], BF16),
                   sb("e_tile1", [128, H * 4 * CH], BF16)]
        ident = sb("ident", [128, 128], BF16)
        ones_col = sb("ones_col", [1, 128], BF16)
        biasrow = sb("biasrow", [1, 4 * DM], BF16)
        murT = sb("murT", [128, CC], F32)
        mukT = sb("mukT", [128, CC], F32)
        onemur = sb("onemur", [128, CC], F32)
        onemuk = sb("onemuk", [128, CC], F32)
        xbuf = [sb(f"xbuf{i}", [128, NT1, DM], F32) for i in range(2)]
        lnTs = [sb(f"lnT{i}", [128, CC, 513], BF16) for i in range(2)]
        LNT_A, LNT_F = 0, 1
        ln_t3 = [sb(f"ln_t{i}", [128, DM], BF16) for i in range(2)]
        xrT2 = [sb(f"xrT{i}", [128, DM], BF16) for i in range(2)]
        xkT2 = [sb(f"xkT{i}", [128, DM], BF16) for i in range(2)]
        kact2 = [sb(f"kact{i}", [128, DM], BF16) for i in range(2)]
        kaT2 = [sb(f"kaT{i}", [128, DM], BF16) for i in range(2)]
        carry1 = sb("carry1", [128, CC], BF16)
        carry3 = sb("carry3", [128, CC], BF16)
        qts = [sb(f"qt{i}", [128, DM], BF16) for i in range(4)]
        kts = [sb(f"kt{i}", [128, DM], BF16) for i in range(4)]
        vts = [sb(f"vt{i}", [128, DM], BF16) for i in range(4)]
        d_tile = sb("d_tile", [128, DM], F32)
        n_tile = sb("n_tile", [128, DM], F32)
        bn6 = sb("bn6", [128, 12], F32)
        ag2 = sb("ag2", [128, 2], F32)
        v1 = bn6[:, 0:1]
        y1 = bn6[:, 1:2]
        t1b = bn6[:, 2:3]

        tmpf = sb("tmpf", [128, 512], F32)
        mixtv = sb("mixtv", [128, 512], BF16)
        pm = [psal(f"pm{i}", [128, 512], F32) for i in range(6)]
        ptrs = [psal("ptr0", [128, 1024], BF16), psal("ptr1", [128, 1024], BF16)]

        r = Rec(nc, es)

        st = {
            "ptr_free": [None, None],
            "ptr_i": [0],
            "x_last": None,
            "qkv_free": [None, None, None, None],
            "S_free": [None, None],
            "E_free": [None, None],
            "pm_free": {i: None for i in range(6)},
            "lnT_users": [None, None],
            "lnT_evac": [[None] * NT1, [None] * NT1],
            "lnT_carry": [None, None],
            "mixtv_free": None,
        }

        def dma4(dst_tile, widx):
            tok = None
            for qtr in range(4):
                r.op("sync", lambda e, dst_tile=dst_tile, widx=widx, qtr=qtr:
                     e.dma_start(out=dst_tile[:, qtr * 2048:(qtr + 1) * 2048],
                                 in_=wbf[widx, :, qtr * 2048:(qtr + 1) * 2048]))
                tok = r.mark("sync", "sD", 16)
            return tok

        def dma(dst, src, **kw):
            r.op("sync", lambda e, dst=dst, src=src, kw=kw: e.dma_start(
                out=dst, in_=src, **kw))
            return r.mark("sync", "sD", 16)

        r.op("gpsimd", lambda e: e.iota(ident[:], [[1, 128]], base=0,
                                        channel_multiplier=-1,
                                        allow_small_or_imprecise_dtypes=True))
        t_gi = r.mark("gpsimd")
        r.wait("vector", t_gi)
        r.op("vector", lambda e: e.tensor_scalar(out=ident[:], in0=ident[:], scalar1=0.0,
                                                 scalar2=None, op0=ALU.is_equal))
        t_ident = r.mark("vector")
        r.op("vector", lambda e: e.memset(ones_col[:], 1.0))
        t_ones = r.mark("vector")
        r.op("vector", lambda e: e.memset(carry1[:], 0.0))
        r.op("vector", lambda e: e.memset(carry3[:], 0.0))
        r.mark("vector")

        t_d = dma(murT[:], bsrcs["mur"].rearrange("(cc p) -> p cc", p=128),
                  allow_slow_non_contiguous=True)
        r.wait("vector", t_d)
        t_d = dma(mukT[:], bsrcs["muk"].rearrange("(cc p) -> p cc", p=128),
                  allow_slow_non_contiguous=True)
        r.wait("vector", t_d)
        r.op("vector", lambda e: e.tensor_scalar(out=onemur[:], in0=murT[:], scalar1=-1.0,
                                                 scalar2=1.0, op0=ALU.mult, op1=ALU.add))
        r.op("vector", lambda e: e.tensor_scalar(out=onemuk[:], in0=mukT[:], scalar1=-1.0,
                                                 scalar2=1.0, op0=ALU.mult, op1=ALU.add))
        t_mu = r.chain("vector")

        t_bias = None
        for i, (bname, scale) in enumerate((("bq", 0.125), ("bk", 1.0),
                                            ("bv", 1.0), ("bo", 1.0))):
            t_d = dma(brow_f32[0:1, :], bsrcs[bname].rearrange("(a d) -> a d", a=1))
            r.waits("vector", [t_d])
            r.op("vector", lambda e, i=i, scale=scale: e.tensor_scalar(
                out=biasrow[0:1, i * DM:(i + 1) * DM], in0=brow_f32[0:1, :],
                scalar1=scale, scalar2=None, op0=ALU.mult))
            t_bias = r.mark("vector")
            r.wait("sync", t_bias)

        r.op("scalar", lambda e: e.preload_act_table(AF.Exp))
        r.mark("scalar")

        conv_specs = [
            (W_Q, "wq", 0.125, False), (W_K, "wk", 1.0, False),
            (W_V, "wv", 1.0, False), (W_O, "wo", 1.0, False),
            (W_RT, "wr", 1.0, True), (W_KT, "wkf", 1.0, True),
            (W_VT, "wvf", 1.0, True),
        ]
        def dma_g(dst, srcap):
            r.op("gpsimd", lambda e, dst=dst, srcap=srcap: e.dma_start(
                out=dst, in_=srcap))
            return r.mark("gpsimd", "sD2", 16)

        stg_tok = [t_bias, None]
        bf_tok = [None, None]
        t_tstage_free = None
        for widx, wname, scale, transp in conv_specs:
            wsrc = wsrcs[wname]
            for cc in range(CC):
                par = cc % 2
                r.wait("sync", stg_tok[par])
                t_ld = dma(stgf[par][:, :], wsrc[cc * 128:(cc + 1) * 128, :])
                r.waits("vector", [t_ld, bf_tok[par]])
                r.op("vector", lambda e, par=par, scale=scale: e.tensor_scalar(
                    out=stgbf[par][:, :], in0=stgf[par][:, :], scalar1=scale,
                    scalar2=None, op0=ALU.mult))
                t_cv = r.mark("vector")
                stg_tok[par] = t_cv
                if not transp:
                    r.wait("sync", t_cv)
                    bf_tok[par] = dma(wbf[widx, :, cc * DM:(cc + 1) * DM],
                                      stgbf[par][:, :])
                else:
                    pi = st["ptr_i"][0]; st["ptr_i"][0] ^= 1
                    ptr = ptrs[pi]
                    r.waits("tensor", [t_cv, t_ident, st["ptr_free"][pi]])
                    for j in range(8):
                        r.op("tensor", lambda e, par=par, j=j, ptr=ptr: e.transpose(
                            ptr[:, j * 128:(j + 1) * 128],
                            stgbf[par][:, j * 128:(j + 1) * 128], ident[:]))
                    t_tr = r.mark("tensor")
                    bf_tok[par] = t_tr
                    r.waits("scalar", [t_tr, t_tstage_free])
                    r.op("scalar", lambda e, ptr=ptr: e.activation(out=tstage[:, :],
                                                          in_=ptr[:, :], func=AF.Copy))
                    t_ev = r.mark("scalar")
                    st["ptr_free"][pi] = t_ev
                    r.wait("sync", t_ev)
                    t_tstage_free = dma(
                        wbf[widx][:, :].rearrange("p (j d) -> p j d", j=8)
                        [:, :, cc * 128:(cc + 1) * 128],
                        tstage[:, :].rearrange("p (j d) -> p j d", j=8))

        slot_tok = [None] * 4
        r.waits("sync", [("sD", r.semc["sD"]), ("sD2", r.semc["sD2"])])
        for si, widx in ((0, W_RT), (1, W_KT), (2, W_VT), (S_WO, W_O)):
            slot_tok[si] = dma4(slots[si], widx)

        x_tok = [None, None]
        x_free = [None, None]

        def load_x(c):
            par = c % 2
            r.wait("sync", x_free[par])
            x_tok[par] = dma(xbuf[par][:, :, :],
                             x_in[c * TCH:(c + 1) * TCH, :].rearrange(
                                 "(t2 p) d -> p t2 d", p=128))

        load_x(0)

        def emit_ln_v(par, t1, deps, lt, lt_free):
            xb = xbuf[par]
            r.waits("vector", deps + [lt_free])
            r.op("vector", lambda e: e.bn_stats(bn6[:, 0:6], xb[:, t1, 0:512]))
            r.op("vector", lambda e: e.bn_stats(bn6[:, 6:12], xb[:, t1, 512:1024]))
            r.chain("vector")
            r.op("vector", lambda e: e.bn_aggr(ag2[:], bn6[:]))
            r.chain("vector")
            r.op("vector", lambda e: e.tensor_scalar(
                out=v1, in0=ag2[:, 1:2], scalar1=1e-5, scalar2=None, op0=ALU.add))
            r.chain("vector")
            v1i = v1.bitcast(I32)
            y1i = y1.bitcast(I32)
            r.op("vector", lambda e: e.tensor_copy(out=t1b, in_=v1i))
            r.chain("vector")
            r.op("vector", lambda e: e.tensor_scalar(
                out=t1b, in0=t1b, scalar1=-0.5, scalar2=float(QUAKE),
                op0=ALU.mult, op1=ALU.add))
            r.chain("vector")
            r.op("vector", lambda e: e.tensor_copy(out=y1i, in_=t1b[:]))
            r.chain("vector")
            for _ in range(2):
                r.op("vector", lambda e: e.tensor_tensor(out=t1b, in0=y1,
                                                         in1=y1, op=ALU.mult))
                r.chain("vector")
                r.op("vector", lambda e: e.tensor_tensor(out=t1b, in0=t1b,
                                                         in1=v1, op=ALU.mult))
                r.chain("vector")
                r.op("vector", lambda e: e.tensor_scalar(
                    out=t1b, in0=t1b, scalar1=-0.5, scalar2=1.5,
                    op0=ALU.mult, op1=ALU.add))
                r.chain("vector")
                r.op("vector", lambda e: e.tensor_tensor(out=y1, in0=y1,
                                                         in1=t1b, op=ALU.mult))
                r.chain("vector")
            r.op("vector", lambda e: e.tensor_scalar(
                out=lt[:, :], in0=xb[:, t1, :], scalar1=ag2[:, 0:1],
                scalar2=y1, op0=ALU.subtract, op1=ALU.mult))
            return r.mark("vector")

        def emit_lnT(li, t1, t_ln, carry, save_carry, lt):
            pi = st["ptr_i"][0]
            st["ptr_i"][0] ^= 1
            ptr = ptrs[pi]
            r.waits("tensor", [t_ln, t_ident, st["ptr_free"][pi],
                               st["lnT_users"][li]])
            for j in range(8):
                r.op("tensor", lambda e, j=j, lt=lt, ptr=ptr: e.transpose(
                    ptr[:, j * 128:(j + 1) * 128],
                    lt[:, j * 128:(j + 1) * 128], ident[:]))
            t_tr = r.mark("tensor")
            r.wait("scalar", t_tr)
            r.op("scalar", lambda e, li=li, t1=t1, ptr=ptr: e.activation(
                out=lnTs[li][:, :, 1 + t1 * 128:1 + (t1 + 1) * 128],
                in_=ptr[:].rearrange("p (j t) -> p j t", j=8), func=AF.Copy))
            tk = r.mark("scalar")
            st["lnT_evac"][li][t1] = tk
            st["ptr_free"][pi] = tk
            if t1 == 0 and carry is not None:
                r.waits("vector", [st["lnT_users"][li]])
                r.op("vector", lambda e, li=li, carry=carry: e.tensor_copy(
                    out=lnTs[li][:, :, 0:1], in_=carry[:].unsqueeze(2)))
                st["lnT_carry"][li] = r.mark("vector")
            if t1 == NT1 - 1 and save_carry is not None:
                r.waits("vector", [tk])
                r.op("vector", lambda e, li=li, save_carry=save_carry: e.tensor_copy(
                    out=save_carry[:].unsqueeze(2), in_=lnTs[li][:, :, 512:513]))
                r.mark("vector")
            return tk, t_tr

        def ffn_pieces(par, li, carry, save_carry, x_dep, res_cb, name,
                       role, p2_need=None, on_mix_done=None):
            toks = {}
            lt, xr_b, xk_b, ka_b, kT_b, u_b = (role["lt"], role["xr"], role["xk"],
                                               role["ka"], role["kT"], role["u"])
            q = []

            def piece(need, fn):
                q.append((need, fn))

            def mk_P1(t1):
                def go():
                    toks[("ln", t1)] = emit_ln_v(par, t1, [x_dep(t1)],
                                                 lt, role["lt_free"])
                return go

            def mk_P2(t1):
                def go():
                    _, t_tr = emit_lnT(li, t1, toks[("ln", t1)], carry,
                                       save_carry, lt)
                    role["lt_free"] = t_tr
                return go

            def mk_P3(t1):
                def go():
                    deps = [st["lnT_evac"][li][t1], t_mu,
                            toks.get(("mm_r", t1 - 1)), toks.get(("mm_k", t1 - 1)),
                            toks.get(("v_done", t1 - 1))]
                    if t1 > 0:
                        deps.append(st["lnT_evac"][li][t1 - 1])
                    else:
                        deps.append(st["lnT_carry"][li])
                        deps.append(role.get("prev_mm_r"))
                        deps.append(role.get("prev_mm_k"))
                        deps.append(role.get("prev_v_done"))
                    r.waits("vector", deps)
                    xr = xr_b[:].rearrange("p (cc t) -> p cc t", cc=CC)
                    xk = xk_b[:].rearrange("p (cc t) -> p cc t", cc=CC)
                    lo = 1 + t1 * 128
                    tv = mixtv[:].rearrange("p (a t) -> p a t", a=4)
                    for dst, muT, onemu in ((xr, murT, onemur), (xk, mukT, onemuk)):
                        for g in range(2):
                            r.waits("scalar", [st["mixtv_free"]])
                            for cc in range(g * 4, g * 4 + 4):
                                r.op("scalar", lambda e, muT=muT, cc=cc, lo=lo,
                                     li=li, tv=tv: e.activation(
                                         out=tv[:, cc % 4, :],
                                         in_=lnTs[li][:, cc, lo:lo + 128],
                                         func=AF.Copy,
                                         scale=muT[:, cc:cc + 1]))
                            t_p1 = r.mark("scalar")
                            r.waits("vector", [t_p1])
                            for cc in range(g * 4, g * 4 + 4):
                                r.op("vector", lambda e, dst=dst, onemu=onemu, cc=cc,
                                     lo=lo, li=li, tv=tv: e.scalar_tensor_tensor(
                                         out=dst[:, cc, :],
                                         in0=lnTs[li][:, cc, lo - 1:lo + 127],
                                         scalar=onemu[:, cc:cc + 1],
                                         in1=tv[:, cc % 4, :], op0=ALU.mult,
                                         op1=ALU.add))
                            tok_mix = r.chain("vector")
                            st["mixtv_free"] = tok_mix
                    toks[("mix", t1)] = tok_mix
                    if t1 == NT1 - 1:
                        st["lnT_users"][li] = toks[("mix", t1)]
                        if on_mix_done is not None:
                            on_mix_done()
                return go

            def evac_r(t1, half, bank, t_mm):
                r.waits("scalar", [t_mm, role.get("u_free")])
                r.op("scalar", lambda e, bank=bank, half=half: e.activation(
                    out=u_b[:, half * 512:(half + 1) * 512], in_=bank[:, :],
                    func=AF.Tanh, scale=0.5))
                return r.mark("scalar")

            def evac_k(t1, half, bank, t_mm):
                r.waits("scalar", [t_mm, toks.get(("kaT_tr", t1 - 1)),
                                   role.get("prev_kaT_tr") if t1 == 0 else None])
                r.op("scalar", lambda e, bank=bank, half=half: e.activation(
                    out=ka_b[:, half * 512:(half + 1) * 512], in_=bank[:, :],
                    func=AF.Relu))
                t_relu = r.mark("scalar")
                r.waits("vector", [t_relu])
                r.op("vector", lambda e, half=half: e.tensor_tensor(
                    out=ka_b[:, half * 512:(half + 1) * 512],
                    in0=ka_b[:, half * 512:(half + 1) * 512],
                    in1=ka_b[:, half * 512:(half + 1) * 512], op=ALU.mult))
                return r.mark("vector")

            def evac_v(t1, half, bank, t_mm):
                r.waits("vector", [t_mm])
                r.op("vector", lambda e, bank=bank, half=half:
                     e.scalar_tensor_tensor(
                         out=tmpf[:, :], in0=u_b[:, half * 512:(half + 1) * 512],
                         scalar=1.0, in1=bank[:, :], op0=ALU.add, op1=ALU.mult))
                r.chain("vector")
                r.op("vector", lambda e, t1=t1, half=half: e.scalar_tensor_tensor(
                    out=xbuf[par][:, t1, half * 512:(half + 1) * 512],
                    in0=tmpf[:, :], scalar=0.25,
                    in1=xbuf[par][:, t1, half * 512:(half + 1) * 512],
                    op0=ALU.mult, op1=ALU.add))
                tk = r.mark("vector")
                if half == 1:
                    role["u_free"] = tk
                    st["x_last"] = tk
                    if res_cb is not None:
                        res_cb(t1, tk)
                return tk

            def mk_mm(t1, phase, stat_of, slot_i, evac):
                def go():
                    stat, t_stat = stat_of()
                    sv = stat[:].rearrange("p (cc t) -> p cc t", cc=CC)
                    r.waits("tensor", [t_stat, slot_tok[slot_i],
                                       st["pm_free"][2], st["pm_free"][3]])
                    for cc in range(CC):
                        for half in range(2):
                            r.op("tensor", lambda e, cc=cc, half=half, sv=sv,
                                 slot_i=slot_i: e.matmul(
                                     pm[2 + half][:, :], lhsT=sv[:, cc, :],
                                     rhs=slots[slot_i][:, cc * DM + half * 512:
                                                       cc * DM + (half + 1) * 512],
                                     start=(cc == 0), stop=(cc == CC - 1),
                                     skip_group_check=True))
                    t_mm = r.mark("tensor")
                    toks[("mm_" + phase, t1)] = t_mm
                    tks = [evac(t1, 0, pm[2], t_mm), evac(t1, 1, pm[3], t_mm)]
                    st["pm_free"][2] = tks[0]
                    st["pm_free"][3] = tks[1]
                    if phase == "v":
                        toks[("v_done", t1)] = tks[1]
                return go

            def mk_P6(t1):
                def go():
                    t_k = toks[("mm_k_done", t1)]
                    pi = st["ptr_i"][0]; st["ptr_i"][0] ^= 1
                    ptr = ptrs[pi]
                    r.waits("tensor", [t_k, st["ptr_free"][pi],
                                       toks.get(("mm_v", t1 - 1)),
                                       role.get("prev_mm_v") if t1 == 0 else None])
                    for j in range(8):
                        r.op("tensor", lambda e, j=j, ptr=ptr: e.transpose(
                            ptr[:, j * 128:(j + 1) * 128],
                            ka_b[:, j * 128:(j + 1) * 128], ident[:]))
                    t_tr = r.mark("tensor")
                    toks[("kaT_tr", t1)] = t_tr
                    r.wait("scalar", t_tr)
                    r.op("scalar", lambda e, ptr=ptr: e.activation(
                        out=kT_b[:, :], in_=ptr[:, :], func=AF.Copy))
                    tk = r.mark("scalar")
                    st["ptr_free"][pi] = tk
                    toks[("kaT", t1)] = tk
                return go

            for t1 in range(NT1):
                piece((name, "x", t1), mk_P1(t1))
                piece(p2_need, mk_P2(t1))
                piece(None, mk_P3(t1))
                piece((name, "slots"),
                      mk_mm(t1, "r", (lambda t1=t1: (xr_b, toks[("mix", t1)])),
                            0, evac_r))
                piece(None,
                      mk_mm(t1, "k", (lambda t1=t1: (xk_b, toks[("mix", t1)])),
                            1, evac_k))

                def fin_k(t1=t1):
                    toks[("mm_k_done", t1)] = ("sV", r.semc["sV"])
                piece(None, fin_k)
                piece(None, mk_P6(t1))
                piece(None,
                      mk_mm(t1, "v", (lambda t1=t1: (kT_b, toks[("kaT", t1)])),
                            2, evac_v))
            def fin_all():
                role["prev_mm_r"] = toks[("mm_r", NT1 - 1)]
                role["prev_mm_k"] = toks[("mm_k", NT1 - 1)]
                role["prev_mm_v"] = toks[("mm_v", NT1 - 1)]
                role["prev_kaT_tr"] = toks[("kaT_tr", NT1 - 1)]
                role["prev_v_done"] = toks[("v_done", NT1 - 1)]
            piece(None, fin_all)
            return q

        def ln_only_pieces(par, li, x_dep, out_toks, name, role):
            q = []
            toks = {}
            lt = role["lt"]

            def mk_P1(t1):
                def go():
                    toks[("ln", t1)] = emit_ln_v(par, t1, [x_dep(t1)],
                                                 lt, role["lt_free"])
                return go

            def mk_P2(t1):
                def go():
                    out_toks[t1], t_tr = emit_lnT(li, t1, toks[("ln", t1)],
                                                  None, None, lt)
                    role["lt_free"] = t_tr
                return go

            for t1 in range(NT1):
                q.append(([(name, "x", t1), (name, "lnok")], mk_P1(t1)))
                q.append((None, mk_P2(t1)))
            return q

        queues = []
        events = set()

        def drive(n=1):
            done = 0
            for _ in range(n):
                fired = False
                for qq in queues:
                    if not qq:
                        continue
                    need, fn = qq[0]
                    ok = (need is None or
                          (isinstance(need, list) and all(n in events for n in need))
                          or (not isinstance(need, list) and need in events))
                    if ok:
                        qq.pop(0)
                        fn()
                        fired = True
                        done += 1
                        break
                if not fired:
                    break
            return done

        def drain_all():
            import sys
            left = sum(len(qq) for qq in queues)
            if left:
                DRAIN_STATS.append(left)
            while True:
                if drive(1) == 0:
                    live = [qq[0][0] for qq in queues if qq]
                    assert not live, f"undrivable pieces: {live}"
                    break

        def attn(par, on_qkv_done, on_res):
            t_omm = {}
            t_aT_tr = {-1: None}

            def qkv_mms(t1):
                pp = t1
                toks_ev = {}
                t_mm = None
                for wi in range(3):
                    b0 = 2 if wi == 1 else 0
                    r.waits("tensor", [st["lnT_evac"][LNT_A][t1], slot_tok[wi],
                                       t_ones, t_bias,
                                       st["pm_free"][b0], st["pm_free"][b0 + 1]])
                    for cc in range(CC):
                        for half in range(2):
                            r.op("tensor", lambda e, t1=t1, cc=cc, half=half, wi=wi,
                                 b0=b0: e.matmul(
                                     pm[b0 + half][:, :],
                                     lhsT=lnTs[LNT_A][:, cc,
                                                      1 + t1 * 128:1 + (t1 + 1) * 128],
                                     rhs=slots[wi][:, cc * DM + half * 512:
                                                   cc * DM + (half + 1) * 512],
                                     start=(cc == 0), stop=False,
                                     skip_group_check=True))
                    for half in range(2):
                        r.op("tensor", lambda e, wi=wi, half=half, b0=b0: e.matmul(
                            pm[b0 + half][:, :], lhsT=ones_col[:, :],
                            rhs=biasrow[0:1, wi * DM + half * 512:
                                        wi * DM + (half + 1) * 512],
                            start=False, stop=True, skip_group_check=True))
                    t_mm = r.mark("tensor")
                    dst = (qts, kts, vts)[wi][pp]
                    r.waits("scalar", [t_mm, st["qkv_free"][pp]])

                    for half in range(2):
                        r.op("scalar", lambda e, dst=dst, half=half, b0=b0:
                             e.activation(
                                 out=dst[:, half * 512:(half + 1) * 512],
                                 in_=pm[b0 + half][:, :], func=AF.Copy))
                    t_ev = r.mark("scalar")
                    toks_ev[wi] = t_ev
                    st["pm_free"][b0] = t_ev
                    st["pm_free"][b0 + 1] = t_ev
                return (toks_ev[1], toks_ev[2]), t_mm

            def scores(t1, t_qkv):
                t_qk, t_v = t_qkv
                pp = t1
                qt, kt, vt = qts[pp], kts[pp], vts[pp]
                q_h = qt[:].rearrange("p (h q) -> p h q", h=H)
                k_h = kt[:].rearrange("p (h k) -> p h k", h=H)
                v_h = vt[:].rearrange("p (h k) -> p h k", h=H)
                d_h = d_tile[:].rearrange("p (h q) -> p h q", h=H)
                n_h = n_tile[:].rearrange("p (h q) -> p h q", h=H)

                r.waits("vector", [st["E_free"][0], st["E_free"][1]])
                r.op("vector", lambda e: e.memset(e_tiles[0][:].bitcast(I32), 0))
                r.op("vector", lambda e: e.memset(e_tiles[1][:].bitcast(I32), 0))
                t_ez = r.mark("vector")
                st["E_free"][0] = t_ez
                st["E_free"][1] = t_ez

                def s_band(b):
                    kmax = 4 * b + 4
                    sv = s_tiles[b % 2][:, 0:H * 4 * CH].rearrange(
                        "p (h j k) -> p h j k", h=H, j=4)
                    r.waits("vector", [t_qk, st["S_free"][b % 2]])
                    r.op("vector", lambda e, b=b, kmax=kmax, sv=sv, q_h=q_h, k_h=k_h:
                         e.tensor_tensor(
                             out=sv[:, :, :, 0:kmax],
                             in0=q_h[:, :, 4 * b:4 * b + 4].unsqueeze(3).broadcast_to(
                                 [128, H, 4, kmax]),
                             in1=k_h[:, :, 0:kmax].unsqueeze(2).broadcast_to(
                                 [128, H, 4, kmax]), op=ALU.mult))
                    return r.mark("vector")

                t_Sb = {0: s_band(0)}
                t_EN = None
                r.wait("vector", t_v)
                for b in range(16):
                    if DRIVE_IN_SCORES:
                        drive(1)
                    if b + 1 < 16:
                        t_Sb[b + 1] = s_band(b + 1)
                    kmax = 4 * b + 4
                    half = kmax // 2
                    ep = e_tiles[b % 2]
                    ev = ep[:].rearrange("p (h j k) -> p h j k", h=H, j=4)
                    sv = s_tiles[b % 2][:, 0:H * 4 * CH].rearrange(
                        "p (h j k) -> p h j k", h=H, j=4)
                    fo = s_tiles[b % 2][:, 0:H * 4 * half].rearrange(
                        "p (h j k) -> p h j k", h=H, j=4)
                    r.waits("scalar", [t_Sb[b], st["E_free"][b % 2]])
                    for j in range(4):
                        qq = 4 * b + j
                        r.op("scalar", lambda e, j=j, qq=qq, sv=sv, ev=ev: e.activation(
                            out=ev[:, :, j, 0:qq + 1], in_=sv[:, :, j, 0:qq + 1],
                            func=AF.Exp))
                    t_E = r.mark("scalar")
                    st["S_free"][b % 2] = t_E
                    r.wait("vector", t_E)
                    r.op("vector", lambda e, ev=ev, fo=fo, half=half, kmax=kmax:
                         e.tensor_tensor(out=fo[:, :, :, 0:half],
                                         in0=ev[:, :, :, 0:half],
                                         in1=ev[:, :, :, half:kmax], op=ALU.add))
                    r.op("vector", lambda e, b=b, fo=fo, half=half, d_h=d_h:
                         e.tensor_reduce(
                             d_h[:, :, 4 * b:4 * b + 4].unsqueeze(3),
                             fo[:, :, :, 0:half], axis=AX.X, op=ALU.add))
                    r.op("vector", lambda e, ev=ev, v_h=v_h, kmax=kmax:
                         e.tensor_tensor(
                             out=ev[:, :, :, 0:kmax], in0=ev[:, :, :, 0:kmax],
                             in1=v_h[:, :, 0:kmax].unsqueeze(2).broadcast_to(
                                 [128, H, 4, kmax]), op=ALU.mult))
                    r.op("vector", lambda e, ev=ev, fo=fo, half=half, kmax=kmax:
                         e.tensor_tensor(out=fo[:, :, :, 0:half],
                                         in0=ev[:, :, :, 0:half],
                                         in1=ev[:, :, :, half:kmax], op=ALU.add))
                    r.op("vector", lambda e, b=b, fo=fo, half=half, n_h=n_h:
                         e.tensor_reduce(
                             n_h[:, :, 4 * b:4 * b + 4].unsqueeze(3),
                             fo[:, :, :, 0:half], axis=AX.X, op=ALU.add))
                    t_EN = r.mark("vector")
                    st["E_free"][b % 2] = t_EN
                r.wait("vector", t_EN)
                d_rcp = big[:, 0:4096].bitcast(F32)
                r.op("vector", lambda e, d_rcp=d_rcp: e.reciprocal(out=d_rcp[:, :],
                                                                   in_=d_tile[:]))
                r.chain("vector")
                r.op("vector", lambda e, d_rcp=d_rcp, qt=qt: e.tensor_tensor(
                    out=qt[:, :], in0=n_tile[:], in1=d_rcp[:, :], op=ALU.mult))
                t_at = r.mark("vector")
                st["S_free"][0] = t_at
                return t_at

            def o_T(t1, t_at):
                pi = st["ptr_i"][0]; st["ptr_i"][0] ^= 1
                ptr = ptrs[pi]
                r.waits("tensor", [t_at, st["ptr_free"][pi]])
                for j in range(8):
                    r.op("tensor", lambda e, j=j, t1=t1, ptr=ptr: e.transpose(
                        ptr[:, j * 128:(j + 1) * 128],
                        qts[t1][:, j * 128:(j + 1) * 128], ident[:]))
                t_tr = r.mark("tensor")
                t_aT_tr[t1] = t_tr
                r.wait("scalar", t_tr)
                r.op("scalar", lambda e, t1=t1, ptr=ptr: e.activation(
                    out=kts[t1][:, :], in_=ptr[:, :], func=AF.Copy))
                t_aT = r.mark("scalar")
                st["ptr_free"][pi] = t_aT
                r.waits("tensor", [t_aT, slot_tok[S_WO],
                                   st["pm_free"][4], st["pm_free"][5]])
                aTv = kts[t1][:].rearrange("p (cc t) -> p cc t", cc=CC)
                for cc in range(CC):
                    for half in range(2):
                        r.op("tensor", lambda e, cc=cc, half=half: e.matmul(
                            pm[4 + half][:, :], lhsT=aTv[:, cc, :],
                            rhs=slots[S_WO][:, cc * DM + half * 512:
                                            cc * DM + (half + 1) * 512],
                            start=(cc == 0), stop=False, skip_group_check=True))
                for half in range(2):
                    r.op("tensor", lambda e, half=half: e.matmul(
                        pm[4 + half][:, :], lhsT=ones_col[:, :],
                        rhs=biasrow[0:1, 3 * DM + half * 512:
                                    3 * DM + (half + 1) * 512],
                        start=False, stop=True, skip_group_check=True))
                t_omm[t1] = r.mark("tensor")

            def o_V(t1):
                r.wait("vector", t_omm[t1])
                r.op("vector", lambda e, t1=t1: e.tensor_tensor(
                    out=xbuf[par][:, t1, 0:512], in0=xbuf[par][:, t1, 0:512],
                    in1=pm[4][:, :], op=ALU.add))
                r.op("vector", lambda e, t1=t1: e.tensor_tensor(
                    out=xbuf[par][:, t1, 512:1024], in0=xbuf[par][:, t1, 512:1024],
                    in1=pm[5][:, :], op=ALU.add))
                tk = r.mark("vector")
                st["x_last"] = tk
                st["pm_free"][4] = tk
                st["pm_free"][5] = tk
                st["qkv_free"][t1] = tk
                on_res(t1, tk)

            t_qkv = {}
            for t1 in range(NT1):
                t_qkv[t1], mm = qkv_mms(t1)
                if t1 == NT1 - 1:
                    on_qkv_done(mm)
            for t1 in range(NT1):
                t_at = scores(t1, t_qkv[t1])
                o_T(t1, t_at)
                if DRIVE_IN_SCORES:
                    drive(1)
                o_V(t1)

        role_f1 = {"lt": ln_t3[0], "xr": xrT2[0], "xk": xkT2[0], "ka": kact2[0],
                   "kT": kaT2[0], "u": xrT2[0], "lt_free": None, "u_free": None}
        role_f2 = {"lt": ln_t3[1], "xr": xrT2[1], "xk": xkT2[1], "ka": kact2[1],
                   "kT": kaT2[1], "u": xrT2[1], "lt_free": None, "u_free": None}
        role_l2 = role_f1

        f1res_prev = [None] * NT1

        def res_store(lst, evset, nm):
            def cb(t1, tok):
                lst[t1] = tok
                evset.add((nm, "x", t1))
            return cb

        events.add(("f1", "slots"))
        events.add(("f1", "lnt"))
        for t1 in range(NT1):
            events.add(("f1", "x", t1))
        q1 = ffn_pieces(0, LNT_A, carry1, carry1, lambda t1: x_tok[0],
                        res_store(f1res_prev, events, "l2"), "f1", role_f1,
                        p2_need=("f1", "lnt"),
                        on_mix_done=lambda: events.add(("l2", "lnok")))
        queues.append(q1)
        ln2_out = [None] * NT1
        q2 = ln_only_pieces(0, LNT_A, lambda t1: f1res_prev[t1], ln2_out, "l2",
                            role_l2)
        queues.append(q2)
        drain_all()
        tok_T0 = ("sT", r.semc["sT"])
        for si, widx in ((0, W_Q), (1, W_K), (2, W_V)):
            r.wait("sync", tok_T0)
            slot_tok[si] = dma4(slots[si], widx)

        for c in range(NCHUNK):
            par = c % 2
            events.clear()
            queues.clear()

            nm_f1 = "f1n"
            nm_f2 = "f2c"
            nm_l2 = "l2n"
            res2 = [None] * NT1
            f1res = [None] * NT1

            if c + 1 < NCHUNK:
                load_x(c + 1)
                for t1 in range(NT1):
                    events.add((nm_f1, "x", t1))
                queues.append(ffn_pieces((c + 1) % 2, LNT_A, carry1, carry1,
                                         lambda t1, p=(c + 1) % 2: x_tok[p],
                                         res_store(f1res, events, nm_l2), nm_f1,
                                         role_f1, p2_need=(nm_f1, "lnt"),
                                         on_mix_done=lambda: events.add(
                                             (nm_l2, "lnok"))))
            f2q = ffn_pieces(par, LNT_F, carry3, carry3,
                             lambda t1: res2[t1], None, nm_f2, role_f2)
            queues.append(f2q)
            ln2n = [None] * NT1
            if c + 1 < NCHUNK:
                queues.append(ln_only_pieces((c + 1) % 2, LNT_A,
                                             lambda t1: f1res[t1], ln2n, nm_l2,
                                             role_l2))

            def on_qkv_done(last_mm, c=c):
                st["lnT_users"][LNT_A] = last_mm
                for si, widx in ((0, W_RT), (1, W_KT), (2, W_VT)):
                    r.wait("sync", last_mm)
                    slot_tok[si] = dma4(slots[si], widx)
                events.add((nm_f1, "slots"))
                events.add((nm_f1, "lnt"))
                events.add((nm_f2, "slots"))

            def on_res(t1, tok):
                res2[t1] = tok
                events.add((nm_f2, "x", t1))

            if TEST_STAGE >= 2:
                attn(par, on_qkv_done, on_res)
            else:
                events.add((nm_f2, "slots"))
                for t1 in range(NT1):
                    events.add((nm_f2, "x", t1))
                    res2[t1] = st["x_last"]
            if TEST_STAGE < 3:
                queues[:] = [q for q in queues if q is not f2q]
            drain_all()

            if c + 1 < NCHUNK:
                slot_deps = ((role_f2["prev_mm_r"], role_f1.get("prev_mm_r")),
                             (role_f2["prev_mm_k"], role_f1.get("prev_mm_k")),
                             (role_f2["prev_mm_v"], role_f1.get("prev_mm_v")))
                for si, widx in ((0, W_Q), (1, W_K), (2, W_V)):
                    r.waits("sync", list(slot_deps[si]))
                    slot_tok[si] = dma4(slots[si], widx)

            r.wait("sync", st["x_last"])
            x_free[par] = dma(out[c * TCH:(c + 1) * TCH, :].rearrange(
                "(t2 p) d -> p t2 d", p=128), xbuf[par][:, :, :])

        r.op("sync", lambda e: e.nop())
        r.wait("sync", ("sD", r.semc["sD"]))
        r.wait("sync", ("sD2", r.semc["sD2"]))

        block = es.enter_context(nc.Block())
        r.emit(block)

    return nc


_cache = {}


def kernel(L0_attn_x, L0_attn_Wq, L0_attn_bq, L0_attn_Wk, L0_attn_bk, L0_attn_Wv,
           L0_attn_bv, L0_attn_Wo, L0_attn_bo, L0_ffn_W_r, L0_ffn_W_k, L0_ffn_W_v,
           L0_ffn_mu_r, L0_ffn_mu_k):
    x = np.ascontiguousarray(np.asarray(L0_attn_x, dtype=np.float32))
    ws = {
        "wq": L0_attn_Wq, "bq": L0_attn_bq, "wk": L0_attn_Wk, "bk": L0_attn_bk,
        "wv": L0_attn_Wv, "bv": L0_attn_bv, "wo": L0_attn_Wo, "bo": L0_attn_bo,
        "wr": L0_ffn_W_r, "wkf": L0_ffn_W_k, "wvf": L0_ffn_W_v,
        "mur": L0_ffn_mu_r, "muk": L0_ffn_mu_k,
    }
    ws = {k: np.ascontiguousarray(np.asarray(v, dtype=np.float32)) for k, v in ws.items()}
    if "nc" not in _cache:
        _cache["nc"] = build_program()
    nc = _cache["nc"]
    in_maps = [dict(ws, x=x[i]) for i in range(8)]
    res = run_bass_kernel_spmd(nc, in_maps, core_ids=list(range(8)))
    return np.stack([np.asarray(res.results[i]["out"]) for i in range(8)], axis=0)


# version 20
# speedup vs baseline: 1.0330x; 1.0330x over previous
import contextlib
import numpy as np

import concourse.bass as bass
import concourse.mybir as mybir
from concourse.bass_utils import run_bass_kernel_spmd

F32 = mybir.dt.float32
BF16 = mybir.dt.bfloat16
I32 = mybir.dt.int32
U8 = mybir.dt.uint8
AF = mybir.ActivationFunctionType
ALU = mybir.AluOpType
AX = mybir.AxisListType

T = 8192
DM = 1024
H = 16
CH = 64
CC = 8
TCH = 512
NCHUNK = T // TCH
NT1 = TCH // 128

ENGS = ("sync", "vector", "scalar", "tensor", "gpsimd")
DRIVE_IN_SCORES = True
DRAIN_STATS = []
TEST_STAGE = 3
QUAKE = 0x5F3759DF


class Rec:

    def __init__(self, nc, es):
        self.nc = nc
        self.items = {e: [] for e in ENGS}
        self.semh = {}
        self.semc = {}
        for name in ("sV", "sA", "sT", "sG", "sS", "sD", "sD2"):
            self.semh[name] = es.enter_context(nc.semaphore(name))
            self.semc[name] = 0
        self.eng_sem = {"vector": "sV", "scalar": "sA", "tensor": "sT",
                        "gpsimd": "sG", "sync": "sS"}

    def op(self, eng, fn):
        self.items[eng].append(["op", fn, None, None])

    def mark(self, eng, sem=None, n=1):
        sem = sem or self.eng_sem[eng]
        items = self.items[eng]
        for i in range(len(items) - 1, -1, -1):
            kind = items[i][0]
            if kind == "op":
                self.semc[sem] += n
                items[i] = ["opinc", items[i][1], self.semh[sem], n]
                return (sem, self.semc[sem])
            if kind == "opinc":
                raise RuntimeError(f"mark({eng}): last op already marked")
        raise RuntimeError(f"mark({eng}): no op to mark")

    def wait(self, eng, tok):
        if tok is None:
            return
        self.items[eng].append(["wait", None, self.semh[tok[0]], tok[1]])

    def waits(self, eng, toks):
        best = {}
        for t in toks:
            if t is None:
                continue
            if t[0] not in best or t[1] > best[t[0]]:
                best[t[0]] = t[1]
        for s, v in best.items():
            self.items[eng].append(["wait", None, self.semh[s], v])

    def chain(self, eng):
        tok = self.mark(eng)
        self.wait(eng, tok)
        return tok

    def emit(self, block):
        for ename in ENGS:
            sec = getattr(block, ename)
            items = self.items[ename]

            def body(e, items=items):
                for kind, fn, sem, val in items:
                    if kind == "op":
                        fn(e)
                    elif kind == "opinc":
                        fn(e).then_inc(sem, val)
                    else:
                        e.wait_ge(sem, val)

            sec(body)


def build_program():
    nc = bass.Bass("TRN2", target_bir_lowering=False)

    x_in = nc.dram_tensor("x", [T, DM], F32, kind="ExternalInput").ap()
    out = nc.dram_tensor("out", [T, DM], F32, kind="ExternalOutput").ap()
    wsrcs = {}
    for name in ("wq", "wk", "wv", "wo", "wr", "wkf", "wvf"):
        wsrcs[name] = nc.dram_tensor(name, [DM, DM], F32, kind="ExternalInput").ap()
    bsrcs = {}
    for name in ("bq", "bk", "bv", "bo", "mur", "muk"):
        bsrcs[name] = nc.dram_tensor(name, [DM], F32, kind="ExternalInput").ap()

    wbf = nc.dram_tensor("wbf", [7, 128, CC * DM], BF16, kind="Internal").ap()
    W_RT, W_KT, W_VT, W_Q, W_K, W_V, W_O = range(7)

    es = contextlib.ExitStack()
    with es:
        sb = lambda name, shape, dt: es.enter_context(nc.sbuf_tensor(name, shape, dt))
        psal = lambda name, shape, dt: es.enter_context(nc.psum_tensor(name, shape, dt))

        slots = [sb(f"wslot{i}", [128, CC * DM], BF16) for i in range(4)]
        S_WO = 3

        big = sb("big", [128, 16384], U8)
        stgf = [big[:, 0:4096].bitcast(F32), big[:, 4096:8192].bitcast(F32)]
        stgbf = [big[:, 8192:10240].bitcast(BF16), big[:, 10240:12288].bitcast(BF16)]
        tstage = big[:, 12288:14336].bitcast(BF16)
        brow_f32 = big[:, 0:4096].bitcast(F32)
        s_tiles = [big[:, 0:8192].bitcast(BF16), big[:, 8192:16384].bitcast(BF16)]

        e_tiles = [sb("e_tile0", [128, H * 4 * CH], BF16),
                   sb("e_tile1", [128, H * 4 * CH], BF16)]
        ident = sb("ident", [128, 128], BF16)
        ones_col = sb("ones_col", [1, 128], BF16)
        biasrow = sb("biasrow", [1, 4 * DM], BF16)
        murT = sb("murT", [128, CC], F32)
        mukT = sb("mukT", [128, CC], F32)
        onemur = sb("onemur", [128, CC], F32)
        onemuk = sb("onemuk", [128, CC], F32)
        xbuf = [sb(f"xbuf{i}", [128, NT1, DM], F32) for i in range(2)]
        lnTs = [sb(f"lnT{i}", [128, CC, 513], BF16) for i in range(2)]
        LNT_A, LNT_F = 0, 1
        ln_t3 = [sb(f"ln_t{i}", [128, DM], BF16) for i in range(2)]
        xrT2 = [sb(f"xrT{i}", [128, DM], BF16) for i in range(2)]
        xkT2 = [sb(f"xkT{i}", [128, DM], BF16) for i in range(2)]
        kact2 = [sb(f"kact{i}", [128, DM], BF16) for i in range(2)]
        kaT2 = [sb(f"kaT{i}", [128, DM], BF16) for i in range(2)]
        carry1 = sb("carry1", [128, CC], BF16)
        carry3 = sb("carry3", [128, CC], BF16)
        qts = [sb(f"qt{i}", [128, DM], BF16) for i in range(4)]
        kts = [sb(f"kt{i}", [128, DM], BF16) for i in range(4)]
        vts = [sb(f"vt{i}", [128, DM], BF16) for i in range(4)]
        d_tile = sb("d_tile", [128, DM], F32)
        n_tile = sb("n_tile", [128, DM], F32)
        bn6 = sb("bn6", [128, 12], F32)
        ag2 = sb("ag2", [128, 2], F32)
        v1 = bn6[:, 0:1]
        y1 = bn6[:, 1:2]
        t1b = bn6[:, 2:3]

        tmpf = sb("tmpf", [128, 512], F32)
        mixtv = sb("mixtv", [128, 512], BF16)
        pm = [psal(f"pm{i}", [128, 512], F32) for i in range(6)]
        ptrs = [psal("ptr0", [128, 1024], BF16), psal("ptr1", [128, 1024], BF16)]

        r = Rec(nc, es)

        st = {
            "ptr_free": [None, None],
            "ptr_i": [0],
            "x_last": None,
            "qkv_free": [None, None, None, None],
            "S_free": [None, None],
            "E_free": [None, None],
            "pm_free": {i: None for i in range(6)},
            "lnT_users": [None, None],
            "lnT_evac": [[None] * NT1, [None] * NT1],
            "lnT_carry": [None, None],
            "mixtv_free": None,
        }

        def dma4(dst_tile, widx):
            tok = None
            for qtr in range(4):
                r.op("sync", lambda e, dst_tile=dst_tile, widx=widx, qtr=qtr:
                     e.dma_start(out=dst_tile[:, qtr * 2048:(qtr + 1) * 2048],
                                 in_=wbf[widx, :, qtr * 2048:(qtr + 1) * 2048]))
                tok = r.mark("sync", "sD", 16)
            return tok

        def dma(dst, src, **kw):
            r.op("sync", lambda e, dst=dst, src=src, kw=kw: e.dma_start(
                out=dst, in_=src, **kw))
            return r.mark("sync", "sD", 16)

        r.op("gpsimd", lambda e: e.iota(ident[:], [[1, 128]], base=0,
                                        channel_multiplier=-1,
                                        allow_small_or_imprecise_dtypes=True))
        t_gi = r.mark("gpsimd")
        r.wait("vector", t_gi)
        r.op("vector", lambda e: e.tensor_scalar(out=ident[:], in0=ident[:], scalar1=0.0,
                                                 scalar2=None, op0=ALU.is_equal))
        t_ident = r.mark("vector")
        r.op("vector", lambda e: e.memset(ones_col[:], 1.0))
        t_ones = r.mark("vector")
        r.op("vector", lambda e: e.memset(carry1[:], 0.0))
        r.op("vector", lambda e: e.memset(carry3[:], 0.0))
        r.mark("vector")

        t_d = dma(murT[:], bsrcs["mur"].rearrange("(cc p) -> p cc", p=128),
                  allow_slow_non_contiguous=True)
        r.wait("vector", t_d)
        t_d = dma(mukT[:], bsrcs["muk"].rearrange("(cc p) -> p cc", p=128),
                  allow_slow_non_contiguous=True)
        r.wait("vector", t_d)
        r.op("vector", lambda e: e.tensor_scalar(out=onemur[:], in0=murT[:], scalar1=-1.0,
                                                 scalar2=1.0, op0=ALU.mult, op1=ALU.add))
        r.op("vector", lambda e: e.tensor_scalar(out=onemuk[:], in0=mukT[:], scalar1=-1.0,
                                                 scalar2=1.0, op0=ALU.mult, op1=ALU.add))
        t_mu = r.chain("vector")

        t_bias = None
        for i, (bname, scale) in enumerate((("bq", 0.125), ("bk", 1.0),
                                            ("bv", 1.0), ("bo", 1.0))):
            t_d = dma(brow_f32[0:1, :], bsrcs[bname].rearrange("(a d) -> a d", a=1))
            r.waits("vector", [t_d])
            r.op("vector", lambda e, i=i, scale=scale: e.tensor_scalar(
                out=biasrow[0:1, i * DM:(i + 1) * DM], in0=brow_f32[0:1, :],
                scalar1=scale, scalar2=None, op0=ALU.mult))
            t_bias = r.mark("vector")
            r.wait("sync", t_bias)

        r.op("scalar", lambda e: e.preload_act_table(AF.Exp))
        r.mark("scalar")

        conv_specs = [
            (W_Q, "wq", 0.125, False), (W_K, "wk", 1.0, False),
            (W_V, "wv", 1.0, False), (W_O, "wo", 1.0, False),
            (W_RT, "wr", 1.0, True), (W_KT, "wkf", 1.0, True),
            (W_VT, "wvf", 1.0, True),
        ]
        def dma_g(dst, srcap):
            r.op("gpsimd", lambda e, dst=dst, srcap=srcap: e.dma_start(
                out=dst, in_=srcap))
            return r.mark("gpsimd", "sD2", 16)

        stg_tok = [t_bias, None]
        bf_tok = [None, None]
        t_tstage_free = None
        for widx, wname, scale, transp in conv_specs:
            wsrc = wsrcs[wname]
            for cc in range(CC):
                par = cc % 2
                r.wait("sync", stg_tok[par])
                t_ld = dma(stgf[par][:, :], wsrc[cc * 128:(cc + 1) * 128, :])
                r.waits("vector", [t_ld, bf_tok[par]])
                r.op("vector", lambda e, par=par, scale=scale: e.tensor_scalar(
                    out=stgbf[par][:, :], in0=stgf[par][:, :], scalar1=scale,
                    scalar2=None, op0=ALU.mult))
                t_cv = r.mark("vector")
                stg_tok[par] = t_cv
                if not transp:
                    r.wait("sync", t_cv)
                    bf_tok[par] = dma(wbf[widx, :, cc * DM:(cc + 1) * DM],
                                      stgbf[par][:, :])
                else:
                    pi = st["ptr_i"][0]; st["ptr_i"][0] ^= 1
                    ptr = ptrs[pi]
                    r.waits("tensor", [t_cv, t_ident, st["ptr_free"][pi]])
                    for j in range(8):
                        r.op("tensor", lambda e, par=par, j=j, ptr=ptr: e.transpose(
                            ptr[:, j * 128:(j + 1) * 128],
                            stgbf[par][:, j * 128:(j + 1) * 128], ident[:]))
                    t_tr = r.mark("tensor")
                    bf_tok[par] = t_tr
                    r.waits("scalar", [t_tr, t_tstage_free])
                    r.op("scalar", lambda e, ptr=ptr: e.activation(out=tstage[:, :],
                                                          in_=ptr[:, :], func=AF.Copy))
                    t_ev = r.mark("scalar")
                    st["ptr_free"][pi] = t_ev
                    r.wait("sync", t_ev)
                    t_tstage_free = dma(
                        wbf[widx][:, :].rearrange("p (j d) -> p j d", j=8)
                        [:, :, cc * 128:(cc + 1) * 128],
                        tstage[:, :].rearrange("p (j d) -> p j d", j=8))

        slot_tok = [None] * 4
        r.waits("sync", [("sD", r.semc["sD"]), ("sD2", r.semc["sD2"])])
        for si, widx in ((0, W_RT), (1, W_KT), (2, W_VT), (S_WO, W_O)):
            slot_tok[si] = dma4(slots[si], widx)

        x_tok = [None, None]
        x_free = [None, None]

        def load_x(c):
            par = c % 2
            r.wait("sync", x_free[par])
            x_tok[par] = dma(xbuf[par][:, :, :],
                             x_in[c * TCH:(c + 1) * TCH, :].rearrange(
                                 "(t2 p) d -> p t2 d", p=128))

        load_x(0)

        def emit_ln_v(par, t1, deps, lt, lt_free):
            xb = xbuf[par]
            r.waits("vector", deps + [lt_free])
            r.op("vector", lambda e: e.bn_stats(bn6[:, 0:6], xb[:, t1, 0:512]))
            r.op("vector", lambda e: e.bn_stats(bn6[:, 6:12], xb[:, t1, 512:1024]))
            r.chain("vector")
            r.op("vector", lambda e: e.bn_aggr(ag2[:], bn6[:]))
            r.chain("vector")
            r.op("vector", lambda e: e.tensor_scalar(
                out=v1, in0=ag2[:, 1:2], scalar1=1e-5, scalar2=None, op0=ALU.add))
            r.chain("vector")
            v1i = v1.bitcast(I32)
            y1i = y1.bitcast(I32)
            r.op("vector", lambda e: e.tensor_copy(out=t1b, in_=v1i))
            r.chain("vector")
            r.op("vector", lambda e: e.tensor_scalar(
                out=t1b, in0=t1b, scalar1=-0.5, scalar2=float(QUAKE),
                op0=ALU.mult, op1=ALU.add))
            r.chain("vector")
            r.op("vector", lambda e: e.tensor_copy(out=y1i, in_=t1b[:]))
            r.chain("vector")
            for _ in range(2):
                r.op("vector", lambda e: e.tensor_tensor(out=t1b, in0=y1,
                                                         in1=y1, op=ALU.mult))
                r.chain("vector")
                r.op("vector", lambda e: e.tensor_tensor(out=t1b, in0=t1b,
                                                         in1=v1, op=ALU.mult))
                r.chain("vector")
                r.op("vector", lambda e: e.tensor_scalar(
                    out=t1b, in0=t1b, scalar1=-0.5, scalar2=1.5,
                    op0=ALU.mult, op1=ALU.add))
                r.chain("vector")
                r.op("vector", lambda e: e.tensor_tensor(out=y1, in0=y1,
                                                         in1=t1b, op=ALU.mult))
                r.chain("vector")
            r.op("vector", lambda e: e.tensor_scalar(
                out=lt[:, :], in0=xb[:, t1, :], scalar1=ag2[:, 0:1],
                scalar2=y1, op0=ALU.subtract, op1=ALU.mult))
            return r.mark("vector")

        def emit_lnT(li, t1, t_ln, carry, save_carry, lt):
            pi = st["ptr_i"][0]
            st["ptr_i"][0] ^= 1
            ptr = ptrs[pi]
            r.waits("tensor", [t_ln, t_ident, st["ptr_free"][pi],
                               st["lnT_users"][li]])
            for j in range(8):
                r.op("tensor", lambda e, j=j, lt=lt, ptr=ptr: e.transpose(
                    ptr[:, j * 128:(j + 1) * 128],
                    lt[:, j * 128:(j + 1) * 128], ident[:]))
            t_tr = r.mark("tensor")
            r.wait("scalar", t_tr)
            r.op("scalar", lambda e, li=li, t1=t1, ptr=ptr: e.activation(
                out=lnTs[li][:, :, 1 + t1 * 128:1 + (t1 + 1) * 128],
                in_=ptr[:].rearrange("p (j t) -> p j t", j=8), func=AF.Copy))
            tk = r.mark("scalar")
            st["lnT_evac"][li][t1] = tk
            st["ptr_free"][pi] = tk
            if t1 == 0 and carry is not None:
                r.waits("vector", [st["lnT_users"][li]])
                r.op("vector", lambda e, li=li, carry=carry: e.tensor_copy(
                    out=lnTs[li][:, :, 0:1], in_=carry[:].unsqueeze(2)))
                st["lnT_carry"][li] = r.mark("vector")
            if t1 == NT1 - 1 and save_carry is not None:
                r.waits("vector", [tk])
                r.op("vector", lambda e, li=li, save_carry=save_carry: e.tensor_copy(
                    out=save_carry[:].unsqueeze(2), in_=lnTs[li][:, :, 512:513]))
                r.mark("vector")
            return tk, t_tr

        def ffn_pieces(par, li, carry, save_carry, x_dep, res_cb, name,
                       role, p2_need=None, on_mix_done=None):
            toks = {}
            lt, xr_b, xk_b, ka_b, kT_b, u_b = (role["lt"], role["xr"], role["xk"],
                                               role["ka"], role["kT"], role["u"])
            q = []

            def piece(need, fn):
                q.append((need, fn))

            def mk_P1(t1):
                def go():
                    toks[("ln", t1)] = emit_ln_v(par, t1, [x_dep(t1)],
                                                 lt, role["lt_free"])
                return go

            def mk_P2(t1):
                def go():
                    _, t_tr = emit_lnT(li, t1, toks[("ln", t1)], carry,
                                       save_carry, lt)
                    role["lt_free"] = t_tr
                return go

            def mk_P3(t1):
                def go():
                    deps = [st["lnT_evac"][li][t1], t_mu,
                            toks.get(("mm_r", t1 - 1)), toks.get(("mm_k", t1 - 1)),
                            toks.get(("v_done", t1 - 1))]
                    if t1 > 0:
                        deps.append(st["lnT_evac"][li][t1 - 1])
                    else:
                        deps.append(st["lnT_carry"][li])
                        deps.append(role.get("prev_mm_r"))
                        deps.append(role.get("prev_mm_k"))
                        deps.append(role.get("prev_v_done"))
                    r.waits("vector", deps)
                    xr = xr_b[:].rearrange("p (cc t) -> p cc t", cc=CC)
                    xk = xk_b[:].rearrange("p (cc t) -> p cc t", cc=CC)
                    lo = 1 + t1 * 128
                    tv = mixtv[:].rearrange("p (a t) -> p a t", a=4)
                    for dst, muT, onemu in ((xr, murT, onemur), (xk, mukT, onemuk)):
                        for g in range(2):
                            r.waits("scalar", [st["mixtv_free"]])
                            for cc in range(g * 4, g * 4 + 4):
                                r.op("scalar", lambda e, muT=muT, cc=cc, lo=lo,
                                     li=li, tv=tv: e.activation(
                                         out=tv[:, cc % 4, :],
                                         in_=lnTs[li][:, cc, lo:lo + 128],
                                         func=AF.Copy,
                                         scale=muT[:, cc:cc + 1]))
                            t_p1 = r.mark("scalar")
                            r.waits("vector", [t_p1])
                            for cc in range(g * 4, g * 4 + 4):
                                r.op("vector", lambda e, dst=dst, onemu=onemu, cc=cc,
                                     lo=lo, li=li, tv=tv: e.scalar_tensor_tensor(
                                         out=dst[:, cc, :],
                                         in0=lnTs[li][:, cc, lo - 1:lo + 127],
                                         scalar=onemu[:, cc:cc + 1],
                                         in1=tv[:, cc % 4, :], op0=ALU.mult,
                                         op1=ALU.add))
                            tok_mix = r.chain("vector")
                            st["mixtv_free"] = tok_mix
                    toks[("mix", t1)] = tok_mix
                    if t1 == NT1 - 1:
                        st["lnT_users"][li] = toks[("mix", t1)]
                        if on_mix_done is not None:
                            on_mix_done()
                return go

            def evac_r(t1, half, bank, t_mm):
                r.waits("scalar", [t_mm, role.get("u_free")])
                r.op("scalar", lambda e, bank=bank, half=half: e.activation(
                    out=u_b[:, half * 512:(half + 1) * 512], in_=bank[:, :],
                    func=AF.Tanh, scale=0.5))
                return r.mark("scalar")

            def evac_k(t1, half, bank, t_mm):
                r.waits("scalar", [t_mm, toks.get(("kaT_tr", t1 - 1)),
                                   role.get("prev_kaT_tr") if t1 == 0 else None])
                r.op("scalar", lambda e, bank=bank, half=half: e.activation(
                    out=ka_b[:, half * 512:(half + 1) * 512], in_=bank[:, :],
                    func=AF.Relu))
                t_relu = r.mark("scalar")
                r.waits("vector", [t_relu])
                r.op("vector", lambda e, half=half: e.tensor_tensor(
                    out=ka_b[:, half * 512:(half + 1) * 512],
                    in0=ka_b[:, half * 512:(half + 1) * 512],
                    in1=ka_b[:, half * 512:(half + 1) * 512], op=ALU.mult))
                return r.mark("vector")

            def evac_v(t1, half, bank, t_mm):
                r.waits("vector", [t_mm])
                r.op("vector", lambda e, bank=bank, half=half:
                     e.scalar_tensor_tensor(
                         out=tmpf[:, :], in0=u_b[:, half * 512:(half + 1) * 512],
                         scalar=1.0, in1=bank[:, :], op0=ALU.add, op1=ALU.mult))
                r.chain("vector")
                r.op("vector", lambda e, t1=t1, half=half: e.scalar_tensor_tensor(
                    out=xbuf[par][:, t1, half * 512:(half + 1) * 512],
                    in0=tmpf[:, :], scalar=0.25,
                    in1=xbuf[par][:, t1, half * 512:(half + 1) * 512],
                    op0=ALU.mult, op1=ALU.add))
                tk = r.mark("vector")
                if half == 1:
                    role["u_free"] = tk
                    st["x_last"] = tk
                    if res_cb is not None:
                        res_cb(t1, tk)
                return tk

            def mk_mm(t1, phase, stat_of, slot_i, evac):
                def go():
                    stat, t_stat = stat_of()
                    sv = stat[:].rearrange("p (cc t) -> p cc t", cc=CC)
                    r.waits("tensor", [t_stat, slot_tok[slot_i],
                                       st["pm_free"][2], st["pm_free"][3]])
                    for cc in range(CC):
                        for half in range(2):
                            r.op("tensor", lambda e, cc=cc, half=half, sv=sv,
                                 slot_i=slot_i: e.matmul(
                                     pm[2 + half][:, :], lhsT=sv[:, cc, :],
                                     rhs=slots[slot_i][:, cc * DM + half * 512:
                                                       cc * DM + (half + 1) * 512],
                                     start=(cc == 0), stop=(cc == CC - 1),
                                     skip_group_check=True))
                    t_mm = r.mark("tensor")
                    toks[("mm_" + phase, t1)] = t_mm
                    tks = [evac(t1, 0, pm[2], t_mm), evac(t1, 1, pm[3], t_mm)]
                    st["pm_free"][2] = tks[0]
                    st["pm_free"][3] = tks[1]
                    if phase == "v":
                        toks[("v_done", t1)] = tks[1]
                return go

            def mk_P6(t1):
                def go():
                    t_k = toks[("mm_k_done", t1)]
                    pi = st["ptr_i"][0]; st["ptr_i"][0] ^= 1
                    ptr = ptrs[pi]
                    r.waits("tensor", [t_k, st["ptr_free"][pi],
                                       toks.get(("mm_v", t1 - 1)),
                                       role.get("prev_mm_v") if t1 == 0 else None])
                    for j in range(8):
                        r.op("tensor", lambda e, j=j, ptr=ptr: e.transpose(
                            ptr[:, j * 128:(j + 1) * 128],
                            ka_b[:, j * 128:(j + 1) * 128], ident[:]))
                    t_tr = r.mark("tensor")
                    toks[("kaT_tr", t1)] = t_tr
                    r.wait("scalar", t_tr)
                    r.op("scalar", lambda e, ptr=ptr: e.activation(
                        out=kT_b[:, :], in_=ptr[:, :], func=AF.Copy))
                    tk = r.mark("scalar")
                    st["ptr_free"][pi] = tk
                    toks[("kaT", t1)] = tk
                return go

            for t1 in range(NT1):
                piece((name, "x", t1), mk_P1(t1))
                piece(p2_need, mk_P2(t1))
                piece(None, mk_P3(t1))
                piece((name, "slots"),
                      mk_mm(t1, "r", (lambda t1=t1: (xr_b, toks[("mix", t1)])),
                            0, evac_r))
                piece(None,
                      mk_mm(t1, "k", (lambda t1=t1: (xk_b, toks[("mix", t1)])),
                            1, evac_k))

                def fin_k(t1=t1):
                    toks[("mm_k_done", t1)] = ("sV", r.semc["sV"])
                piece(None, fin_k)
                piece(None, mk_P6(t1))
                piece(None,
                      mk_mm(t1, "v", (lambda t1=t1: (kT_b, toks[("kaT", t1)])),
                            2, evac_v))
            def fin_all():
                role["prev_mm_r"] = toks[("mm_r", NT1 - 1)]
                role["prev_mm_k"] = toks[("mm_k", NT1 - 1)]
                role["prev_mm_v"] = toks[("mm_v", NT1 - 1)]
                role["prev_kaT_tr"] = toks[("kaT_tr", NT1 - 1)]
                role["prev_v_done"] = toks[("v_done", NT1 - 1)]
            piece(None, fin_all)
            return q

        def ln_only_pieces(par, li, x_dep, out_toks, name, role):
            q = []
            toks = {}
            lt = role["lt"]

            def mk_P1(t1):
                def go():
                    toks[("ln", t1)] = emit_ln_v(par, t1, [x_dep(t1)],
                                                 lt, role["lt_free"])
                return go

            def mk_P2(t1):
                def go():
                    out_toks[t1], t_tr = emit_lnT(li, t1, toks[("ln", t1)],
                                                  None, None, lt)
                    role["lt_free"] = t_tr
                return go

            for t1 in range(NT1):
                q.append(([(name, "x", t1), (name, "lnok")], mk_P1(t1)))
                q.append((None, mk_P2(t1)))
            return q

        queues = []
        events = set()

        def drive(n=1):
            done = 0
            for _ in range(n):
                fired = False
                for qq in queues:
                    if not qq:
                        continue
                    need, fn = qq[0]
                    ok = (need is None or
                          (isinstance(need, list) and all(n in events for n in need))
                          or (not isinstance(need, list) and need in events))
                    if ok:
                        qq.pop(0)
                        fn()
                        fired = True
                        done += 1
                        break
                if not fired:
                    break
            return done

        def drain_all():
            import sys
            left = sum(len(qq) for qq in queues)
            if left:
                DRAIN_STATS.append(left)
            while True:
                if drive(1) == 0:
                    live = [qq[0][0] for qq in queues if qq]
                    assert not live, f"undrivable pieces: {live}"
                    break

        def attn(par, on_qkv_done, on_res):
            t_omm = {}
            t_aT_tr = {-1: None}

            def qkv_mms(t1):
                pp = t1
                toks_ev = {}
                t_mm = None
                for wi in range(3):
                    b0 = 2 if wi == 1 else 0
                    r.waits("tensor", [st["lnT_evac"][LNT_A][t1], slot_tok[wi],
                                       t_ones, t_bias,
                                       st["pm_free"][b0], st["pm_free"][b0 + 1]])
                    for cc in range(CC):
                        for half in range(2):
                            r.op("tensor", lambda e, t1=t1, cc=cc, half=half, wi=wi,
                                 b0=b0: e.matmul(
                                     pm[b0 + half][:, :],
                                     lhsT=lnTs[LNT_A][:, cc,
                                                      1 + t1 * 128:1 + (t1 + 1) * 128],
                                     rhs=slots[wi][:, cc * DM + half * 512:
                                                   cc * DM + (half + 1) * 512],
                                     start=(cc == 0), stop=False,
                                     skip_group_check=True))
                    for half in range(2):
                        r.op("tensor", lambda e, wi=wi, half=half, b0=b0: e.matmul(
                            pm[b0 + half][:, :], lhsT=ones_col[:, :],
                            rhs=biasrow[0:1, wi * DM + half * 512:
                                        wi * DM + (half + 1) * 512],
                            start=False, stop=True, skip_group_check=True))
                    t_mm = r.mark("tensor")
                    dst = (qts, kts, vts)[wi][pp]
                    r.waits("scalar", [t_mm, st["qkv_free"][pp]])

                    for half in range(2):
                        r.op("scalar", lambda e, dst=dst, half=half, b0=b0:
                             e.activation(
                                 out=dst[:, half * 512:(half + 1) * 512],
                                 in_=pm[b0 + half][:, :], func=AF.Copy))
                    t_ev = r.mark("scalar")
                    toks_ev[wi] = t_ev
                    st["pm_free"][b0] = t_ev
                    st["pm_free"][b0 + 1] = t_ev
                return (toks_ev[1], toks_ev[2]), t_mm

            def scores(t1, t_qkv):
                t_qk, t_v = t_qkv
                pp = t1
                qt, kt, vt = qts[pp], kts[pp], vts[pp]
                q_h = qt[:].rearrange("p (h q) -> p h q", h=H)
                k_h = kt[:].rearrange("p (h k) -> p h k", h=H)
                v_h = vt[:].rearrange("p (h k) -> p h k", h=H)
                d_h = d_tile[:].rearrange("p (h q) -> p h q", h=H)
                n_h = n_tile[:].rearrange("p (h q) -> p h q", h=H)

                r.waits("gpsimd", [st["E_free"][0], st["E_free"][1]])
                r.op("gpsimd", lambda e: e.memset(e_tiles[0][:].bitcast(I32), 0))
                r.op("gpsimd", lambda e: e.memset(e_tiles[1][:].bitcast(I32), 0))
                t_ez = r.mark("gpsimd")
                st["E_free"][0] = t_ez
                st["E_free"][1] = t_ez

                def s_band(b):
                    kmax = 4 * b + 4
                    sv = s_tiles[b % 2][:, 0:H * 4 * CH].rearrange(
                        "p (h j k) -> p h j k", h=H, j=4)
                    r.waits("vector", [t_qk, st["S_free"][b % 2]])
                    r.op("vector", lambda e, b=b, kmax=kmax, sv=sv, q_h=q_h, k_h=k_h:
                         e.tensor_tensor(
                             out=sv[:, :, :, 0:kmax],
                             in0=q_h[:, :, 4 * b:4 * b + 4].unsqueeze(3).broadcast_to(
                                 [128, H, 4, kmax]),
                             in1=k_h[:, :, 0:kmax].unsqueeze(2).broadcast_to(
                                 [128, H, 4, kmax]), op=ALU.mult))
                    return r.mark("vector")

                t_Sb = {0: s_band(0)}
                t_EN = None
                r.wait("vector", t_v)
                for b in range(16):
                    if DRIVE_IN_SCORES:
                        drive(1)
                    if b + 1 < 16:
                        t_Sb[b + 1] = s_band(b + 1)
                    kmax = 4 * b + 4
                    half = kmax // 2
                    ep = e_tiles[b % 2]
                    ev = ep[:].rearrange("p (h j k) -> p h j k", h=H, j=4)
                    sv = s_tiles[b % 2][:, 0:H * 4 * CH].rearrange(
                        "p (h j k) -> p h j k", h=H, j=4)
                    fo = s_tiles[b % 2][:, 0:H * 4 * half].rearrange(
                        "p (h j k) -> p h j k", h=H, j=4)
                    r.waits("scalar", [t_Sb[b], st["E_free"][b % 2]])
                    for j in range(4):
                        qq = 4 * b + j
                        r.op("scalar", lambda e, j=j, qq=qq, sv=sv, ev=ev: e.activation(
                            out=ev[:, :, j, 0:qq + 1], in_=sv[:, :, j, 0:qq + 1],
                            func=AF.Exp))
                    t_E = r.mark("scalar")
                    st["S_free"][b % 2] = t_E
                    r.wait("vector", t_E)
                    q4 = half // 2
                    r.op("vector", lambda e, ev=ev, fo=fo, half=half, kmax=kmax:
                         e.tensor_tensor(out=fo[:, :, :, 0:half],
                                         in0=ev[:, :, :, 0:half],
                                         in1=ev[:, :, :, half:kmax], op=ALU.add))
                    if q4 > 0:
                        r.op("vector", lambda e, fo=fo, q4=q4, half=half:
                             e.tensor_tensor(out=fo[:, :, :, 0:q4],
                                             in0=fo[:, :, :, 0:q4],
                                             in1=fo[:, :, :, q4:half], op=ALU.add))
                    rw = max(q4, 1)
                    r.op("vector", lambda e, b=b, fo=fo, rw=rw, d_h=d_h:
                         e.tensor_reduce(
                             d_h[:, :, 4 * b:4 * b + 4].unsqueeze(3),
                             fo[:, :, :, 0:rw], axis=AX.X, op=ALU.add))
                    for j in range(4):
                        r.op("vector", lambda e, ev=ev, v_h=v_h, kmax=kmax, j=j:
                             e.tensor_tensor(
                                 out=ev[:, :, j, 0:kmax], in0=ev[:, :, j, 0:kmax],
                                 in1=v_h[:, :, 0:kmax], op=ALU.mult))
                    r.op("vector", lambda e, ev=ev, fo=fo, half=half, kmax=kmax:
                         e.tensor_tensor(out=fo[:, :, :, 0:half],
                                         in0=ev[:, :, :, 0:half],
                                         in1=ev[:, :, :, half:kmax], op=ALU.add))
                    if q4 > 0:
                        r.op("vector", lambda e, fo=fo, q4=q4, half=half:
                             e.tensor_tensor(out=fo[:, :, :, 0:q4],
                                             in0=fo[:, :, :, 0:q4],
                                             in1=fo[:, :, :, q4:half], op=ALU.add))
                    r.op("vector", lambda e, b=b, fo=fo, rw=rw, n_h=n_h:
                         e.tensor_reduce(
                             n_h[:, :, 4 * b:4 * b + 4].unsqueeze(3),
                             fo[:, :, :, 0:rw], axis=AX.X, op=ALU.add))
                    t_EN = r.mark("vector")
                    st["E_free"][b % 2] = t_EN
                r.wait("vector", t_EN)
                d_rcp = big[:, 0:4096].bitcast(F32)
                r.op("vector", lambda e, d_rcp=d_rcp: e.reciprocal(out=d_rcp[:, :],
                                                                   in_=d_tile[:]))
                r.chain("vector")
                r.op("vector", lambda e, d_rcp=d_rcp, qt=qt: e.tensor_tensor(
                    out=qt[:, :], in0=n_tile[:], in1=d_rcp[:, :], op=ALU.mult))
                t_at = r.mark("vector")
                st["S_free"][0] = t_at
                return t_at

            def o_T(t1, t_at):
                pi = st["ptr_i"][0]; st["ptr_i"][0] ^= 1
                ptr = ptrs[pi]
                r.waits("tensor", [t_at, st["ptr_free"][pi]])
                for j in range(8):
                    r.op("tensor", lambda e, j=j, t1=t1, ptr=ptr: e.transpose(
                        ptr[:, j * 128:(j + 1) * 128],
                        qts[t1][:, j * 128:(j + 1) * 128], ident[:]))
                t_tr = r.mark("tensor")
                t_aT_tr[t1] = t_tr
                r.wait("scalar", t_tr)
                r.op("scalar", lambda e, t1=t1, ptr=ptr: e.activation(
                    out=kts[t1][:, :], in_=ptr[:, :], func=AF.Copy))
                t_aT = r.mark("scalar")
                st["ptr_free"][pi] = t_aT
                r.waits("tensor", [t_aT, slot_tok[S_WO],
                                   st["pm_free"][4], st["pm_free"][5]])
                aTv = kts[t1][:].rearrange("p (cc t) -> p cc t", cc=CC)
                for cc in range(CC):
                    for half in range(2):
                        r.op("tensor", lambda e, cc=cc, half=half: e.matmul(
                            pm[4 + half][:, :], lhsT=aTv[:, cc, :],
                            rhs=slots[S_WO][:, cc * DM + half * 512:
                                            cc * DM + (half + 1) * 512],
                            start=(cc == 0), stop=False, skip_group_check=True))
                for half in range(2):
                    r.op("tensor", lambda e, half=half: e.matmul(
                        pm[4 + half][:, :], lhsT=ones_col[:, :],
                        rhs=biasrow[0:1, 3 * DM + half * 512:
                                    3 * DM + (half + 1) * 512],
                        start=False, stop=True, skip_group_check=True))
                t_omm[t1] = r.mark("tensor")

            def o_V(t1):
                r.wait("vector", t_omm[t1])
                r.op("vector", lambda e, t1=t1: e.tensor_tensor(
                    out=xbuf[par][:, t1, 0:512], in0=xbuf[par][:, t1, 0:512],
                    in1=pm[4][:, :], op=ALU.add))
                r.op("vector", lambda e, t1=t1: e.tensor_tensor(
                    out=xbuf[par][:, t1, 512:1024], in0=xbuf[par][:, t1, 512:1024],
                    in1=pm[5][:, :], op=ALU.add))
                tk = r.mark("vector")
                st["x_last"] = tk
                st["pm_free"][4] = tk
                st["pm_free"][5] = tk
                st["qkv_free"][t1] = tk
                on_res(t1, tk)

            t_qkv = {}
            for t1 in range(NT1):
                t_qkv[t1], mm = qkv_mms(t1)
                if t1 == NT1 - 1:
                    on_qkv_done(mm)
            for t1 in range(NT1):
                t_at = scores(t1, t_qkv[t1])
                o_T(t1, t_at)
                if DRIVE_IN_SCORES:
                    drive(1)
                o_V(t1)

        role_f1 = {"lt": ln_t3[0], "xr": xrT2[0], "xk": xkT2[0], "ka": kact2[0],
                   "kT": kaT2[0], "u": xrT2[0], "lt_free": None, "u_free": None}
        role_f2 = {"lt": ln_t3[1], "xr": xrT2[1], "xk": xkT2[1], "ka": kact2[1],
                   "kT": kaT2[1], "u": xrT2[1], "lt_free": None, "u_free": None}
        role_l2 = role_f1

        f1res_prev = [None] * NT1

        def res_store(lst, evset, nm):
            def cb(t1, tok):
                lst[t1] = tok
                evset.add((nm, "x", t1))
            return cb

        events.add(("f1", "slots"))
        events.add(("f1", "lnt"))
        for t1 in range(NT1):
            events.add(("f1", "x", t1))
        q1 = ffn_pieces(0, LNT_A, carry1, carry1, lambda t1: x_tok[0],
                        res_store(f1res_prev, events, "l2"), "f1", role_f1,
                        p2_need=("f1", "lnt"),
                        on_mix_done=lambda: events.add(("l2", "lnok")))
        queues.append(q1)
        ln2_out = [None] * NT1
        q2 = ln_only_pieces(0, LNT_A, lambda t1: f1res_prev[t1], ln2_out, "l2",
                            role_l2)
        queues.append(q2)
        drain_all()
        tok_T0 = ("sT", r.semc["sT"])
        for si, widx in ((0, W_Q), (1, W_K), (2, W_V)):
            r.wait("sync", tok_T0)
            slot_tok[si] = dma4(slots[si], widx)

        for c in range(NCHUNK):
            par = c % 2
            events.clear()
            queues.clear()

            nm_f1 = "f1n"
            nm_f2 = "f2c"
            nm_l2 = "l2n"
            res2 = [None] * NT1
            f1res = [None] * NT1

            if c + 1 < NCHUNK:
                load_x(c + 1)
                for t1 in range(NT1):
                    events.add((nm_f1, "x", t1))
                queues.append(ffn_pieces((c + 1) % 2, LNT_A, carry1, carry1,
                                         lambda t1, p=(c + 1) % 2: x_tok[p],
                                         res_store(f1res, events, nm_l2), nm_f1,
                                         role_f1, p2_need=(nm_f1, "lnt"),
                                         on_mix_done=lambda: events.add(
                                             (nm_l2, "lnok"))))
            f2q = ffn_pieces(par, LNT_F, carry3, carry3,
                             lambda t1: res2[t1], None, nm_f2, role_f2)
            queues.append(f2q)
            ln2n = [None] * NT1
            if c + 1 < NCHUNK:
                queues.append(ln_only_pieces((c + 1) % 2, LNT_A,
                                             lambda t1: f1res[t1], ln2n, nm_l2,
                                             role_l2))

            def on_qkv_done(last_mm, c=c):
                st["lnT_users"][LNT_A] = last_mm
                for si, widx in ((0, W_RT), (1, W_KT), (2, W_VT)):
                    r.wait("sync", last_mm)
                    slot_tok[si] = dma4(slots[si], widx)
                events.add((nm_f1, "slots"))
                events.add((nm_f1, "lnt"))
                events.add((nm_f2, "slots"))

            def on_res(t1, tok):
                res2[t1] = tok
                events.add((nm_f2, "x", t1))

            if TEST_STAGE >= 2:
                attn(par, on_qkv_done, on_res)
            else:
                events.add((nm_f2, "slots"))
                for t1 in range(NT1):
                    events.add((nm_f2, "x", t1))
                    res2[t1] = st["x_last"]
            if TEST_STAGE < 3:
                queues[:] = [q for q in queues if q is not f2q]
            drain_all()

            if c + 1 < NCHUNK:
                slot_deps = ((role_f2["prev_mm_r"], role_f1.get("prev_mm_r")),
                             (role_f2["prev_mm_k"], role_f1.get("prev_mm_k")),
                             (role_f2["prev_mm_v"], role_f1.get("prev_mm_v")))
                for si, widx in ((0, W_Q), (1, W_K), (2, W_V)):
                    r.waits("sync", list(slot_deps[si]))
                    slot_tok[si] = dma4(slots[si], widx)

            r.wait("sync", st["x_last"])
            x_free[par] = dma(out[c * TCH:(c + 1) * TCH, :].rearrange(
                "(t2 p) d -> p t2 d", p=128), xbuf[par][:, :, :])

        r.op("sync", lambda e: e.nop())
        r.wait("sync", ("sD", r.semc["sD"]))
        r.wait("sync", ("sD2", r.semc["sD2"]))

        block = es.enter_context(nc.Block())
        r.emit(block)

    return nc


_cache = {}


def kernel(L0_attn_x, L0_attn_Wq, L0_attn_bq, L0_attn_Wk, L0_attn_bk, L0_attn_Wv,
           L0_attn_bv, L0_attn_Wo, L0_attn_bo, L0_ffn_W_r, L0_ffn_W_k, L0_ffn_W_v,
           L0_ffn_mu_r, L0_ffn_mu_k):
    x = np.ascontiguousarray(np.asarray(L0_attn_x, dtype=np.float32))
    ws = {
        "wq": L0_attn_Wq, "bq": L0_attn_bq, "wk": L0_attn_Wk, "bk": L0_attn_bk,
        "wv": L0_attn_Wv, "bv": L0_attn_bv, "wo": L0_attn_Wo, "bo": L0_attn_bo,
        "wr": L0_ffn_W_r, "wkf": L0_ffn_W_k, "wvf": L0_ffn_W_v,
        "mur": L0_ffn_mu_r, "muk": L0_ffn_mu_k,
    }
    ws = {k: np.ascontiguousarray(np.asarray(v, dtype=np.float32)) for k, v in ws.items()}
    if "nc" not in _cache:
        _cache["nc"] = build_program()
    nc = _cache["nc"]
    in_maps = [dict(ws, x=x[i]) for i in range(8)]
    res = run_bass_kernel_spmd(nc, in_maps, core_ids=list(range(8)))
    return np.stack([np.asarray(res.results[i]["out"]) for i in range(8)], axis=0)


# version 22
# speedup vs baseline: 1.0403x; 1.0070x over previous
import contextlib
import numpy as np

import concourse.bass as bass
import concourse.mybir as mybir
from concourse.bass_utils import run_bass_kernel_spmd

F32 = mybir.dt.float32
BF16 = mybir.dt.bfloat16
I32 = mybir.dt.int32
U8 = mybir.dt.uint8
AF = mybir.ActivationFunctionType
ALU = mybir.AluOpType
AX = mybir.AxisListType

T = 8192
DM = 1024
H = 16
CH = 64
CC = 8
TCH = 512
NCHUNK = T // TCH
NT1 = TCH // 128

ENGS = ("sync", "vector", "scalar", "tensor", "gpsimd")
DRIVE_IN_SCORES = True
DRAIN_STATS = []
TEST_STAGE = 3
QUAKE = 0x5F3759DF


class Rec:

    def __init__(self, nc, es):
        self.nc = nc
        self.items = {e: [] for e in ENGS}
        self.semh = {}
        self.semc = {}
        for name in ("sV", "sA", "sT", "sG", "sS", "sD", "sD2"):
            self.semh[name] = es.enter_context(nc.semaphore(name))
            self.semc[name] = 0
        self.eng_sem = {"vector": "sV", "scalar": "sA", "tensor": "sT",
                        "gpsimd": "sG", "sync": "sS"}

    def op(self, eng, fn):
        self.items[eng].append(["op", fn, None, None])

    def mark(self, eng, sem=None, n=1):
        sem = sem or self.eng_sem[eng]
        items = self.items[eng]
        for i in range(len(items) - 1, -1, -1):
            kind = items[i][0]
            if kind == "op":
                self.semc[sem] += n
                items[i] = ["opinc", items[i][1], self.semh[sem], n]
                return (sem, self.semc[sem])
            if kind == "opinc":
                raise RuntimeError(f"mark({eng}): last op already marked")
        raise RuntimeError(f"mark({eng}): no op to mark")

    def wait(self, eng, tok):
        if tok is None:
            return
        self.items[eng].append(["wait", None, self.semh[tok[0]], tok[1]])

    def waits(self, eng, toks):
        best = {}
        for t in toks:
            if t is None:
                continue
            if t[0] not in best or t[1] > best[t[0]]:
                best[t[0]] = t[1]
        for s, v in best.items():
            self.items[eng].append(["wait", None, self.semh[s], v])

    def chain(self, eng):
        tok = self.mark(eng)
        self.wait(eng, tok)
        return tok

    def emit(self, block):
        for ename in ENGS:
            sec = getattr(block, ename)
            items = self.items[ename]

            def body(e, items=items):
                for kind, fn, sem, val in items:
                    if kind == "op":
                        fn(e)
                    elif kind == "opinc":
                        fn(e).then_inc(sem, val)
                    else:
                        e.wait_ge(sem, val)

            sec(body)


def build_program():
    nc = bass.Bass("TRN2", target_bir_lowering=False)

    x_in = nc.dram_tensor("x", [T, DM], F32, kind="ExternalInput").ap()
    out = nc.dram_tensor("out", [T, DM], F32, kind="ExternalOutput").ap()
    wsrcs = {}
    for name in ("wq", "wk", "wv", "wo", "wr", "wkf", "wvf"):
        wsrcs[name] = nc.dram_tensor(name, [DM, DM], F32, kind="ExternalInput").ap()
    bsrcs = {}
    for name in ("bq", "bk", "bv", "bo", "mur", "muk"):
        bsrcs[name] = nc.dram_tensor(name, [DM], F32, kind="ExternalInput").ap()

    wbf = nc.dram_tensor("wbf", [7, 128, CC * DM], BF16, kind="Internal").ap()
    W_RT, W_KT, W_VT, W_Q, W_K, W_V, W_O = range(7)

    es = contextlib.ExitStack()
    with es:
        sb = lambda name, shape, dt: es.enter_context(nc.sbuf_tensor(name, shape, dt))
        psal = lambda name, shape, dt: es.enter_context(nc.psum_tensor(name, shape, dt))

        slots = [sb(f"wslot{i}", [128, CC * DM], BF16) for i in range(4)]
        S_WO = 3

        big = sb("big", [128, 16384], U8)
        stgf = [big[:, 0:4096].bitcast(F32), big[:, 4096:8192].bitcast(F32)]
        stgbf = [big[:, 8192:10240].bitcast(BF16), big[:, 10240:12288].bitcast(BF16)]
        tstage = big[:, 12288:14336].bitcast(BF16)
        brow_f32 = big[:, 0:4096].bitcast(F32)
        s_tiles = [big[:, 0:8192].bitcast(BF16), big[:, 8192:16384].bitcast(BF16)]

        e_tiles = [sb("e_tile0", [128, H * 4 * CH], BF16),
                   sb("e_tile1", [128, H * 4 * CH], BF16)]
        ident = sb("ident", [128, 128], BF16)
        ones_col = sb("ones_col", [1, 128], BF16)
        biasrow = sb("biasrow", [1, 4 * DM], BF16)
        murT = sb("murT", [128, CC], F32)
        mukT = sb("mukT", [128, CC], F32)
        onemur = sb("onemur", [128, CC], F32)
        onemuk = sb("onemuk", [128, CC], F32)
        xbuf = [sb(f"xbuf{i}", [128, NT1, DM], F32) for i in range(2)]
        lnTs = [sb(f"lnT{i}", [128, CC, 513], BF16) for i in range(2)]
        LNT_A, LNT_F = 0, 1
        ln_t3 = [sb(f"ln_t{i}", [128, DM], BF16) for i in range(2)]
        xrT2 = [sb(f"xrT{i}", [128, DM], BF16) for i in range(2)]
        xkT2 = [sb(f"xkT{i}", [128, DM], BF16) for i in range(2)]
        kact2 = [sb(f"kact{i}", [128, DM], BF16) for i in range(2)]
        kaT2 = [sb(f"kaT{i}", [128, DM], BF16) for i in range(2)]
        carry1 = sb("carry1", [128, CC], BF16)
        carry3 = sb("carry3", [128, CC], BF16)
        qts = [sb(f"qt{i}", [128, DM], BF16) for i in range(4)]
        kts = [sb(f"kt{i}", [128, DM], BF16) for i in range(4)]
        vts = [sb(f"vt{i}", [128, DM], BF16) for i in range(4)]
        d_tile = sb("d_tile", [128, DM], F32)
        n_tile = sb("n_tile", [128, DM], F32)
        bn6 = sb("bn6", [128, 12], F32)
        ag2 = sb("ag2", [128, 2], F32)
        v1 = bn6[:, 0:1]
        y1 = bn6[:, 1:2]
        t1b = bn6[:, 2:3]

        tmpf = sb("tmpf", [128, 512], F32)
        mixtv = sb("mixtv", [128, 512], BF16)
        pm = [psal(f"pm{i}", [128, 512], F32) for i in range(6)]
        ptrs = [psal("ptr0", [128, 1024], BF16), psal("ptr1", [128, 1024], BF16)]

        r = Rec(nc, es)

        st = {
            "ptr_free": [None, None],
            "ptr_i": [0],
            "x_last": None,
            "qkv_free": [None, None, None, None],
            "S_free": [None, None],
            "E_free": [None, None],
            "pm_free": {i: None for i in range(6)},
            "lnT_users": [None, None],
            "lnT_evac": [[None] * NT1, [None] * NT1],
            "lnT_carry": [None, None],
            "mixtv_free": None,
        }

        def dma4(dst_tile, widx):
            tok = None
            for qtr in range(4):
                r.op("sync", lambda e, dst_tile=dst_tile, widx=widx, qtr=qtr:
                     e.dma_start(out=dst_tile[:, qtr * 2048:(qtr + 1) * 2048],
                                 in_=wbf[widx, :, qtr * 2048:(qtr + 1) * 2048]))
                tok = r.mark("sync", "sD", 16)
            return tok

        def dma(dst, src, **kw):
            r.op("sync", lambda e, dst=dst, src=src, kw=kw: e.dma_start(
                out=dst, in_=src, **kw))
            return r.mark("sync", "sD", 16)

        r.op("gpsimd", lambda e: e.iota(ident[:], [[1, 128]], base=0,
                                        channel_multiplier=-1,
                                        allow_small_or_imprecise_dtypes=True))
        t_gi = r.mark("gpsimd")
        r.wait("vector", t_gi)
        r.op("vector", lambda e: e.tensor_scalar(out=ident[:], in0=ident[:], scalar1=0.0,
                                                 scalar2=None, op0=ALU.is_equal))
        t_ident = r.mark("vector")
        r.op("vector", lambda e: e.memset(ones_col[:], 1.0))
        t_ones = r.mark("vector")
        r.op("vector", lambda e: e.memset(carry1[:], 0.0))
        r.op("vector", lambda e: e.memset(carry3[:], 0.0))
        r.mark("vector")

        t_d = dma(murT[:], bsrcs["mur"].rearrange("(cc p) -> p cc", p=128),
                  allow_slow_non_contiguous=True)
        r.wait("vector", t_d)
        t_d = dma(mukT[:], bsrcs["muk"].rearrange("(cc p) -> p cc", p=128),
                  allow_slow_non_contiguous=True)
        r.wait("vector", t_d)
        r.op("vector", lambda e: e.tensor_scalar(out=onemur[:], in0=murT[:], scalar1=-1.0,
                                                 scalar2=1.0, op0=ALU.mult, op1=ALU.add))
        r.op("vector", lambda e: e.tensor_scalar(out=onemuk[:], in0=mukT[:], scalar1=-1.0,
                                                 scalar2=1.0, op0=ALU.mult, op1=ALU.add))
        t_mu = r.chain("vector")

        t_bias = None
        for i, (bname, scale) in enumerate((("bq", 0.125), ("bk", 1.0),
                                            ("bv", 1.0), ("bo", 1.0))):
            t_d = dma(brow_f32[0:1, :], bsrcs[bname].rearrange("(a d) -> a d", a=1))
            r.waits("vector", [t_d])
            r.op("vector", lambda e, i=i, scale=scale: e.tensor_scalar(
                out=biasrow[0:1, i * DM:(i + 1) * DM], in0=brow_f32[0:1, :],
                scalar1=scale, scalar2=None, op0=ALU.mult))
            t_bias = r.mark("vector")
            r.wait("sync", t_bias)

        r.op("scalar", lambda e: e.preload_act_table(AF.Exp))
        r.mark("scalar")

        conv_specs = [
            (W_Q, "wq", 0.125, False), (W_K, "wk", 1.0, False),
            (W_V, "wv", 1.0, False), (W_O, "wo", 1.0, False),
            (W_RT, "wr", 1.0, True), (W_KT, "wkf", 1.0, True),
            (W_VT, "wvf", 1.0, True),
        ]
        def dma_g(dst, srcap):
            r.op("gpsimd", lambda e, dst=dst, srcap=srcap: e.dma_start(
                out=dst, in_=srcap))
            return r.mark("gpsimd", "sD2", 16)

        stg_tok = [t_bias, None]
        bf_tok = [None, None]
        t_tstage_free = None
        for widx, wname, scale, transp in conv_specs:
            wsrc = wsrcs[wname]
            for cc in range(CC):
                par = cc % 2
                r.wait("sync", stg_tok[par])
                t_ld = dma(stgf[par][:, :], wsrc[cc * 128:(cc + 1) * 128, :])
                r.waits("vector", [t_ld, bf_tok[par]])
                r.op("vector", lambda e, par=par, scale=scale: e.tensor_scalar(
                    out=stgbf[par][:, :], in0=stgf[par][:, :], scalar1=scale,
                    scalar2=None, op0=ALU.mult))
                t_cv = r.mark("vector")
                stg_tok[par] = t_cv
                if not transp:
                    r.wait("sync", t_cv)
                    bf_tok[par] = dma(wbf[widx, :, cc * DM:(cc + 1) * DM],
                                      stgbf[par][:, :])
                else:
                    pi = st["ptr_i"][0]; st["ptr_i"][0] ^= 1
                    ptr = ptrs[pi]
                    r.waits("tensor", [t_cv, t_ident, st["ptr_free"][pi]])
                    for j in range(8):
                        r.op("tensor", lambda e, par=par, j=j, ptr=ptr: e.transpose(
                            ptr[:, j * 128:(j + 1) * 128],
                            stgbf[par][:, j * 128:(j + 1) * 128], ident[:]))
                    t_tr = r.mark("tensor")
                    bf_tok[par] = t_tr
                    r.waits("scalar", [t_tr, t_tstage_free])
                    r.op("scalar", lambda e, ptr=ptr: e.activation(out=tstage[:, :],
                                                          in_=ptr[:, :], func=AF.Copy))
                    t_ev = r.mark("scalar")
                    st["ptr_free"][pi] = t_ev
                    r.wait("sync", t_ev)
                    t_tstage_free = dma(
                        wbf[widx][:, :].rearrange("p (j d) -> p j d", j=8)
                        [:, :, cc * 128:(cc + 1) * 128],
                        tstage[:, :].rearrange("p (j d) -> p j d", j=8))

        slot_tok = [None] * 4
        r.waits("sync", [("sD", r.semc["sD"]), ("sD2", r.semc["sD2"])])
        for si, widx in ((0, W_RT), (1, W_KT), (2, W_VT), (S_WO, W_O)):
            slot_tok[si] = dma4(slots[si], widx)

        x_tok = [None, None]
        x_free = [None, None]

        def load_x(c):
            par = c % 2
            r.wait("sync", x_free[par])
            x_tok[par] = dma(xbuf[par][:, :, :],
                             x_in[c * TCH:(c + 1) * TCH, :].rearrange(
                                 "(t2 p) d -> p t2 d", p=128))

        load_x(0)

        def emit_ln_v(par, t1, deps, lt, lt_free):
            xb = xbuf[par]
            r.waits("vector", deps + [lt_free])
            r.op("vector", lambda e: e.bn_stats(bn6[:, 0:6], xb[:, t1, 0:512]))
            r.op("vector", lambda e: e.bn_stats(bn6[:, 6:12], xb[:, t1, 512:1024]))
            r.chain("vector")
            r.op("vector", lambda e: e.bn_aggr(ag2[:], bn6[:]))
            r.chain("vector")
            r.op("vector", lambda e: e.tensor_scalar(
                out=v1, in0=ag2[:, 1:2], scalar1=1e-5, scalar2=None, op0=ALU.add))
            r.chain("vector")
            v1i = v1.bitcast(I32)
            y1i = y1.bitcast(I32)
            r.op("vector", lambda e: e.tensor_copy(out=t1b, in_=v1i))
            r.chain("vector")
            r.op("vector", lambda e: e.tensor_scalar(
                out=t1b, in0=t1b, scalar1=-0.5, scalar2=float(QUAKE),
                op0=ALU.mult, op1=ALU.add))
            r.chain("vector")
            r.op("vector", lambda e: e.tensor_copy(out=y1i, in_=t1b[:]))
            r.chain("vector")
            for _ in range(2):
                r.op("vector", lambda e: e.tensor_tensor(out=t1b, in0=y1,
                                                         in1=y1, op=ALU.mult))
                r.chain("vector")
                r.op("vector", lambda e: e.tensor_tensor(out=t1b, in0=t1b,
                                                         in1=v1, op=ALU.mult))
                r.chain("vector")
                r.op("vector", lambda e: e.tensor_scalar(
                    out=t1b, in0=t1b, scalar1=-0.5, scalar2=1.5,
                    op0=ALU.mult, op1=ALU.add))
                r.chain("vector")
                r.op("vector", lambda e: e.tensor_tensor(out=y1, in0=y1,
                                                         in1=t1b, op=ALU.mult))
                r.chain("vector")
            r.op("vector", lambda e: e.tensor_scalar(
                out=lt[:, :], in0=xb[:, t1, :], scalar1=ag2[:, 0:1],
                scalar2=y1, op0=ALU.subtract, op1=ALU.mult))
            return r.mark("vector")

        def emit_lnT(li, t1, t_ln, carry, save_carry, lt):
            pi = st["ptr_i"][0]
            st["ptr_i"][0] ^= 1
            ptr = ptrs[pi]
            r.waits("tensor", [t_ln, t_ident, st["ptr_free"][pi],
                               st["lnT_users"][li]])
            for j in range(8):
                r.op("tensor", lambda e, j=j, lt=lt, ptr=ptr: e.transpose(
                    ptr[:, j * 128:(j + 1) * 128],
                    lt[:, j * 128:(j + 1) * 128], ident[:]))
            t_tr = r.mark("tensor")
            r.wait("scalar", t_tr)
            r.op("scalar", lambda e, li=li, t1=t1, ptr=ptr: e.activation(
                out=lnTs[li][:, :, 1 + t1 * 128:1 + (t1 + 1) * 128],
                in_=ptr[:].rearrange("p (j t) -> p j t", j=8), func=AF.Copy))
            tk = r.mark("scalar")
            st["lnT_evac"][li][t1] = tk
            st["ptr_free"][pi] = tk
            if t1 == 0 and carry is not None:
                r.waits("vector", [st["lnT_users"][li]])
                r.op("vector", lambda e, li=li, carry=carry: e.tensor_copy(
                    out=lnTs[li][:, :, 0:1], in_=carry[:].unsqueeze(2)))
                st["lnT_carry"][li] = r.mark("vector")
            if t1 == NT1 - 1 and save_carry is not None:
                r.waits("vector", [tk])
                r.op("vector", lambda e, li=li, save_carry=save_carry: e.tensor_copy(
                    out=save_carry[:].unsqueeze(2), in_=lnTs[li][:, :, 512:513]))
                r.mark("vector")
            return tk, t_tr

        def ffn_pieces(par, li, carry, save_carry, x_dep, res_cb, name,
                       role, p2_need=None, on_mix_done=None):
            toks = {}
            lt, xr_b, xk_b, ka_b, kT_b, u_b = (role["lt"], role["xr"], role["xk"],
                                               role["ka"], role["kT"], role["u"])
            q = []

            def piece(need, fn):
                q.append((need, fn))

            def mk_P1(t1):
                def go():
                    toks[("ln", t1)] = emit_ln_v(par, t1, [x_dep(t1)],
                                                 lt, role["lt_free"])
                return go

            def mk_P2(t1):
                def go():
                    _, t_tr = emit_lnT(li, t1, toks[("ln", t1)], carry,
                                       save_carry, lt)
                    role["lt_free"] = t_tr
                return go

            def mk_P3(t1):
                def go():
                    deps = [st["lnT_evac"][li][t1], t_mu,
                            toks.get(("mm_r", t1 - 1)), toks.get(("mm_k", t1 - 1)),
                            toks.get(("v_done", t1 - 1))]
                    if t1 > 0:
                        deps.append(st["lnT_evac"][li][t1 - 1])
                    else:
                        deps.append(st["lnT_carry"][li])
                        deps.append(role.get("prev_mm_r"))
                        deps.append(role.get("prev_mm_k"))
                        deps.append(role.get("prev_v_done"))
                    r.waits("vector", deps)
                    xr = xr_b[:].rearrange("p (cc t) -> p cc t", cc=CC)
                    xk = xk_b[:].rearrange("p (cc t) -> p cc t", cc=CC)
                    lo = 1 + t1 * 128
                    tv = mixtv[:].rearrange("p (a t) -> p a t", a=4)
                    for dst, muT, onemu in ((xr, murT, onemur), (xk, mukT, onemuk)):
                        for g in range(2):
                            r.waits("scalar", [st["mixtv_free"]])
                            for cc in range(g * 4, g * 4 + 4):
                                r.op("scalar", lambda e, muT=muT, cc=cc, lo=lo,
                                     li=li, tv=tv: e.activation(
                                         out=tv[:, cc % 4, :],
                                         in_=lnTs[li][:, cc, lo:lo + 128],
                                         func=AF.Copy,
                                         scale=muT[:, cc:cc + 1]))
                            t_p1 = r.mark("scalar")
                            r.waits("vector", [t_p1])
                            for cc in range(g * 4, g * 4 + 4):
                                r.op("vector", lambda e, dst=dst, onemu=onemu, cc=cc,
                                     lo=lo, li=li, tv=tv: e.scalar_tensor_tensor(
                                         out=dst[:, cc, :],
                                         in0=lnTs[li][:, cc, lo - 1:lo + 127],
                                         scalar=onemu[:, cc:cc + 1],
                                         in1=tv[:, cc % 4, :], op0=ALU.mult,
                                         op1=ALU.add))
                            tok_mix = r.chain("vector")
                            st["mixtv_free"] = tok_mix
                    toks[("mix", t1)] = tok_mix
                    if t1 == NT1 - 1:
                        st["lnT_users"][li] = toks[("mix", t1)]
                        if on_mix_done is not None:
                            on_mix_done()
                return go

            def evac_r(t1, half, bank, t_mm):
                r.waits("scalar", [t_mm, role.get("u_free")])
                r.op("scalar", lambda e, bank=bank, half=half: e.activation(
                    out=u_b[:, half * 512:(half + 1) * 512], in_=bank[:, :],
                    func=AF.Tanh, scale=0.5))
                return r.mark("scalar")

            def evac_k(t1, half, bank, t_mm):
                r.waits("scalar", [t_mm, toks.get(("kaT_tr", t1 - 1)),
                                   role.get("prev_kaT_tr") if t1 == 0 else None])
                r.op("scalar", lambda e, bank=bank, half=half: e.activation(
                    out=ka_b[:, half * 512:(half + 1) * 512], in_=bank[:, :],
                    func=AF.Relu))
                t_relu = r.mark("scalar")
                r.waits("vector", [t_relu])
                r.op("vector", lambda e, half=half: e.tensor_tensor(
                    out=ka_b[:, half * 512:(half + 1) * 512],
                    in0=ka_b[:, half * 512:(half + 1) * 512],
                    in1=ka_b[:, half * 512:(half + 1) * 512], op=ALU.mult))
                return r.mark("vector")

            def evac_v(t1, half, bank, t_mm):
                r.waits("vector", [t_mm])
                r.op("vector", lambda e, bank=bank, half=half:
                     e.scalar_tensor_tensor(
                         out=tmpf[:, :], in0=u_b[:, half * 512:(half + 1) * 512],
                         scalar=1.0, in1=bank[:, :], op0=ALU.add, op1=ALU.mult))
                r.chain("vector")
                r.op("vector", lambda e, t1=t1, half=half: e.scalar_tensor_tensor(
                    out=xbuf[par][:, t1, half * 512:(half + 1) * 512],
                    in0=tmpf[:, :], scalar=0.25,
                    in1=xbuf[par][:, t1, half * 512:(half + 1) * 512],
                    op0=ALU.mult, op1=ALU.add))
                tk = r.mark("vector")
                if half == 1:
                    role["u_free"] = tk
                    st["x_last"] = tk
                    if res_cb is not None:
                        res_cb(t1, tk)
                return tk

            def mk_mm(t1, phase, stat_of, slot_i, evac):
                def go():
                    stat, t_stat = stat_of()
                    sv = stat[:].rearrange("p (cc t) -> p cc t", cc=CC)
                    r.waits("tensor", [t_stat, slot_tok[slot_i],
                                       st["pm_free"][2], st["pm_free"][3]])
                    for cc in range(CC):
                        for half in range(2):
                            r.op("tensor", lambda e, cc=cc, half=half, sv=sv,
                                 slot_i=slot_i: e.matmul(
                                     pm[2 + half][:, :], lhsT=sv[:, cc, :],
                                     rhs=slots[slot_i][:, cc * DM + half * 512:
                                                       cc * DM + (half + 1) * 512],
                                     start=(cc == 0), stop=(cc == CC - 1),
                                     skip_group_check=True))
                    t_mm = r.mark("tensor")
                    toks[("mm_" + phase, t1)] = t_mm
                    tks = [evac(t1, 0, pm[2], t_mm), evac(t1, 1, pm[3], t_mm)]
                    st["pm_free"][2] = tks[0]
                    st["pm_free"][3] = tks[1]
                    if phase == "v":
                        toks[("v_done", t1)] = tks[1]
                return go

            def mk_P6(t1):
                def go():
                    t_k = toks[("mm_k_done", t1)]
                    pi = st["ptr_i"][0]; st["ptr_i"][0] ^= 1
                    ptr = ptrs[pi]
                    r.waits("tensor", [t_k, st["ptr_free"][pi],
                                       toks.get(("mm_v", t1 - 1)),
                                       role.get("prev_mm_v") if t1 == 0 else None])
                    for j in range(8):
                        r.op("tensor", lambda e, j=j, ptr=ptr: e.transpose(
                            ptr[:, j * 128:(j + 1) * 128],
                            ka_b[:, j * 128:(j + 1) * 128], ident[:]))
                    t_tr = r.mark("tensor")
                    toks[("kaT_tr", t1)] = t_tr
                    r.wait("scalar", t_tr)
                    r.op("scalar", lambda e, ptr=ptr: e.activation(
                        out=kT_b[:, :], in_=ptr[:, :], func=AF.Copy))
                    tk = r.mark("scalar")
                    st["ptr_free"][pi] = tk
                    toks[("kaT", t1)] = tk
                return go

            for t1 in range(NT1):
                piece((name, "x", t1), mk_P1(t1))
                piece(p2_need, mk_P2(t1))
                piece(None, mk_P3(t1))
                piece((name, "slots"),
                      mk_mm(t1, "r", (lambda t1=t1: (xr_b, toks[("mix", t1)])),
                            0, evac_r))
                piece(None,
                      mk_mm(t1, "k", (lambda t1=t1: (xk_b, toks[("mix", t1)])),
                            1, evac_k))

                def fin_k(t1=t1):
                    toks[("mm_k_done", t1)] = ("sV", r.semc["sV"])
                piece(None, fin_k)
                piece(None, mk_P6(t1))
                piece(None,
                      mk_mm(t1, "v", (lambda t1=t1: (kT_b, toks[("kaT", t1)])),
                            2, evac_v))
            def fin_all():
                role["prev_mm_r"] = toks[("mm_r", NT1 - 1)]
                role["prev_mm_k"] = toks[("mm_k", NT1 - 1)]
                role["prev_mm_v"] = toks[("mm_v", NT1 - 1)]
                role["prev_kaT_tr"] = toks[("kaT_tr", NT1 - 1)]
                role["prev_v_done"] = toks[("v_done", NT1 - 1)]
            piece(None, fin_all)
            return q

        def ln_only_pieces(par, li, x_dep, out_toks, name, role):
            q = []
            toks = {}
            lt = role["lt"]

            def mk_P1(t1):
                def go():
                    toks[("ln", t1)] = emit_ln_v(par, t1, [x_dep(t1)],
                                                 lt, role["lt_free"])
                return go

            def mk_P2(t1):
                def go():
                    out_toks[t1], t_tr = emit_lnT(li, t1, toks[("ln", t1)],
                                                  None, None, lt)
                    role["lt_free"] = t_tr
                return go

            for t1 in range(NT1):
                q.append(([(name, "x", t1), (name, "lnok")], mk_P1(t1)))
                q.append((None, mk_P2(t1)))
            return q

        queues = []
        events = set()

        def drive(n=1):
            done = 0
            for _ in range(n):
                fired = False
                for qq in queues:
                    if not qq:
                        continue
                    need, fn = qq[0]
                    ok = (need is None or
                          (isinstance(need, list) and all(n in events for n in need))
                          or (not isinstance(need, list) and need in events))
                    if ok:
                        qq.pop(0)
                        fn()
                        fired = True
                        done += 1
                        break
                if not fired:
                    break
            return done

        def drain_all():
            import sys
            left = sum(len(qq) for qq in queues)
            if left:
                DRAIN_STATS.append(left)
            while True:
                if drive(1) == 0:
                    live = [qq[0][0] for qq in queues if qq]
                    assert not live, f"undrivable pieces: {live}"
                    break

        def attn(par, on_qkv_done, on_res):
            t_omm = {}
            t_aT_tr = {-1: None}

            def qkv_mms(t1):
                pp = t1
                toks_ev = {}
                t_mm = None
                for wi in range(3):
                    b0 = 2 if wi == 1 else 0
                    r.waits("tensor", [st["lnT_evac"][LNT_A][t1], slot_tok[wi],
                                       t_ones, t_bias,
                                       st["pm_free"][b0], st["pm_free"][b0 + 1]])
                    for cc in range(CC):
                        for half in range(2):
                            r.op("tensor", lambda e, t1=t1, cc=cc, half=half, wi=wi,
                                 b0=b0: e.matmul(
                                     pm[b0 + half][:, :],
                                     lhsT=lnTs[LNT_A][:, cc,
                                                      1 + t1 * 128:1 + (t1 + 1) * 128],
                                     rhs=slots[wi][:, cc * DM + half * 512:
                                                   cc * DM + (half + 1) * 512],
                                     start=(cc == 0), stop=False,
                                     skip_group_check=True))
                    for half in range(2):
                        r.op("tensor", lambda e, wi=wi, half=half, b0=b0: e.matmul(
                            pm[b0 + half][:, :], lhsT=ones_col[:, :],
                            rhs=biasrow[0:1, wi * DM + half * 512:
                                        wi * DM + (half + 1) * 512],
                            start=False, stop=True, skip_group_check=True))
                    t_mm = r.mark("tensor")
                    dst = (qts, kts, vts)[wi][pp]
                    r.waits("scalar", [t_mm, st["qkv_free"][pp]])

                    for half in range(2):
                        r.op("scalar", lambda e, dst=dst, half=half, b0=b0:
                             e.activation(
                                 out=dst[:, half * 512:(half + 1) * 512],
                                 in_=pm[b0 + half][:, :], func=AF.Copy))
                    t_ev = r.mark("scalar")
                    toks_ev[wi] = t_ev
                    st["pm_free"][b0] = t_ev
                    st["pm_free"][b0 + 1] = t_ev
                return (toks_ev[1], toks_ev[2]), t_mm

            def scores(t1, t_qkv):
                t_qk, t_v = t_qkv
                pp = t1
                qt, kt, vt = qts[pp], kts[pp], vts[pp]
                q_h = qt[:].rearrange("p (h q) -> p h q", h=H)
                k_h = kt[:].rearrange("p (h k) -> p h k", h=H)
                v_h = vt[:].rearrange("p (h k) -> p h k", h=H)
                d_h = d_tile[:].rearrange("p (h q) -> p h q", h=H)
                n_h = n_tile[:].rearrange("p (h q) -> p h q", h=H)

                r.waits("gpsimd", [st["E_free"][0], st["E_free"][1]])
                r.op("gpsimd", lambda e: e.memset(e_tiles[0][:].bitcast(I32), 0))
                r.op("gpsimd", lambda e: e.memset(e_tiles[1][:].bitcast(I32), 0))
                t_ez = r.mark("gpsimd")
                st["E_free"][0] = t_ez
                st["E_free"][1] = t_ez

                def s_band(b):
                    kmax = 4 * b + 4
                    sv = s_tiles[b % 2][:, 0:H * 4 * CH].rearrange(
                        "p (h j k) -> p h j k", h=H, j=4)
                    r.waits("vector", [t_qk, st["S_free"][b % 2]])
                    r.op("vector", lambda e, b=b, kmax=kmax, sv=sv, q_h=q_h, k_h=k_h:
                         e.tensor_tensor(
                             out=sv[:, :, :, 0:kmax],
                             in0=q_h[:, :, 4 * b:4 * b + 4].unsqueeze(3).broadcast_to(
                                 [128, H, 4, kmax]),
                             in1=k_h[:, :, 0:kmax].unsqueeze(2).broadcast_to(
                                 [128, H, 4, kmax]), op=ALU.mult))
                    return r.mark("vector")

                t_Sb = {0: s_band(0)}
                t_EN = None
                r.wait("vector", t_v)
                for b in range(16):
                    if DRIVE_IN_SCORES:
                        drive(1)
                    if b + 1 < 16:
                        t_Sb[b + 1] = s_band(b + 1)
                    kmax = 4 * b + 4
                    half = kmax // 2
                    ep = e_tiles[b % 2]
                    ev = ep[:].rearrange("p (h j k) -> p h j k", h=H, j=4)
                    sv = s_tiles[b % 2][:, 0:H * 4 * CH].rearrange(
                        "p (h j k) -> p h j k", h=H, j=4)
                    fo = s_tiles[b % 2][:, 0:H * 4 * half].rearrange(
                        "p (h j k) -> p h j k", h=H, j=4)
                    r.waits("scalar", [t_Sb[b], st["E_free"][b % 2]])
                    for j in range(4):
                        qq = 4 * b + j
                        r.op("scalar", lambda e, j=j, qq=qq, sv=sv, ev=ev: e.activation(
                            out=ev[:, :, j, 0:qq + 1], in_=sv[:, :, j, 0:qq + 1],
                            func=AF.Exp))
                    t_E = r.mark("scalar")
                    st["S_free"][b % 2] = t_E
                    r.wait("vector", t_E)
                    q4 = half // 2
                    r.op("vector", lambda e, ev=ev, fo=fo, half=half, kmax=kmax:
                         e.tensor_tensor(out=fo[:, :, :, 0:half],
                                         in0=ev[:, :, :, 0:half],
                                         in1=ev[:, :, :, half:kmax], op=ALU.add))
                    if q4 > 0:
                        r.op("vector", lambda e, fo=fo, q4=q4, half=half:
                             e.tensor_tensor(out=fo[:, :, :, 0:q4],
                                             in0=fo[:, :, :, 0:q4],
                                             in1=fo[:, :, :, q4:half], op=ALU.add))
                    q8 = 0
                    if q8 > 0:
                        r.op("vector", lambda e, fo=fo, q8=q8, q4=q4:
                             e.tensor_tensor(out=fo[:, :, :, 0:q8],
                                             in0=fo[:, :, :, 0:q8],
                                             in1=fo[:, :, :, q8:q4], op=ALU.add))
                    rw = q8 if q8 > 0 else q4
                    r.op("vector", lambda e, b=b, fo=fo, rw=rw, d_h=d_h:
                         e.tensor_reduce(
                             d_h[:, :, 4 * b:4 * b + 4].unsqueeze(3),
                             fo[:, :, :, 0:rw], axis=AX.X, op=ALU.add))
                    r.op("vector", lambda e, ev=ev, v_h=v_h, kmax=kmax:
                         e.tensor_tensor(
                             out=ev[:, :, :, 0:kmax], in0=ev[:, :, :, 0:kmax],
                             in1=v_h[:, :, 0:kmax].unsqueeze(2).broadcast_to(
                                 [128, H, 4, kmax]), op=ALU.mult))
                    r.op("vector", lambda e, ev=ev, fo=fo, half=half, kmax=kmax:
                         e.tensor_tensor(out=fo[:, :, :, 0:half],
                                         in0=ev[:, :, :, 0:half],
                                         in1=ev[:, :, :, half:kmax], op=ALU.add))
                    if q4 > 0:
                        r.op("vector", lambda e, fo=fo, q4=q4, half=half:
                             e.tensor_tensor(out=fo[:, :, :, 0:q4],
                                             in0=fo[:, :, :, 0:q4],
                                             in1=fo[:, :, :, q4:half], op=ALU.add))
                    if q8 > 0:
                        r.op("vector", lambda e, fo=fo, q8=q8, q4=q4:
                             e.tensor_tensor(out=fo[:, :, :, 0:q8],
                                             in0=fo[:, :, :, 0:q8],
                                             in1=fo[:, :, :, q8:q4], op=ALU.add))
                    r.op("vector", lambda e, b=b, fo=fo, rw=rw, n_h=n_h:
                         e.tensor_reduce(
                             n_h[:, :, 4 * b:4 * b + 4].unsqueeze(3),
                             fo[:, :, :, 0:rw], axis=AX.X, op=ALU.add))
                    t_EN = r.mark("vector")
                    st["E_free"][b % 2] = t_EN
                r.wait("vector", t_EN)
                d_rcp = big[:, 0:4096].bitcast(F32)
                r.op("vector", lambda e, d_rcp=d_rcp: e.reciprocal(out=d_rcp[:, :],
                                                                   in_=d_tile[:]))
                r.chain("vector")
                r.op("vector", lambda e, d_rcp=d_rcp, qt=qt: e.tensor_tensor(
                    out=qt[:, :], in0=n_tile[:], in1=d_rcp[:, :], op=ALU.mult))
                t_at = r.mark("vector")
                st["S_free"][0] = t_at
                return t_at

            def o_T(t1, t_at):
                pi = st["ptr_i"][0]; st["ptr_i"][0] ^= 1
                ptr = ptrs[pi]
                r.waits("tensor", [t_at, st["ptr_free"][pi]])
                for j in range(8):
                    r.op("tensor", lambda e, j=j, t1=t1, ptr=ptr: e.transpose(
                        ptr[:, j * 128:(j + 1) * 128],
                        qts[t1][:, j * 128:(j + 1) * 128], ident[:]))
                t_tr = r.mark("tensor")
                t_aT_tr[t1] = t_tr
                r.wait("scalar", t_tr)
                r.op("scalar", lambda e, t1=t1, ptr=ptr: e.activation(
                    out=kts[t1][:, :], in_=ptr[:, :], func=AF.Copy))
                t_aT = r.mark("scalar")
                st["ptr_free"][pi] = t_aT
                r.waits("tensor", [t_aT, slot_tok[S_WO],
                                   st["pm_free"][4], st["pm_free"][5]])
                aTv = kts[t1][:].rearrange("p (cc t) -> p cc t", cc=CC)
                for cc in range(CC):
                    for half in range(2):
                        r.op("tensor", lambda e, cc=cc, half=half: e.matmul(
                            pm[4 + half][:, :], lhsT=aTv[:, cc, :],
                            rhs=slots[S_WO][:, cc * DM + half * 512:
                                            cc * DM + (half + 1) * 512],
                            start=(cc == 0), stop=False, skip_group_check=True))
                for half in range(2):
                    r.op("tensor", lambda e, half=half: e.matmul(
                        pm[4 + half][:, :], lhsT=ones_col[:, :],
                        rhs=biasrow[0:1, 3 * DM + half * 512:
                                    3 * DM + (half + 1) * 512],
                        start=False, stop=True, skip_group_check=True))
                t_omm[t1] = r.mark("tensor")

            def o_V(t1):
                r.wait("vector", t_omm[t1])
                r.op("vector", lambda e, t1=t1: e.tensor_tensor(
                    out=xbuf[par][:, t1, 0:512], in0=xbuf[par][:, t1, 0:512],
                    in1=pm[4][:, :], op=ALU.add))
                r.op("vector", lambda e, t1=t1: e.tensor_tensor(
                    out=xbuf[par][:, t1, 512:1024], in0=xbuf[par][:, t1, 512:1024],
                    in1=pm[5][:, :], op=ALU.add))
                tk = r.mark("vector")
                st["x_last"] = tk
                st["pm_free"][4] = tk
                st["pm_free"][5] = tk
                st["qkv_free"][t1] = tk
                on_res(t1, tk)

            t_qkv = {}
            for t1 in range(NT1):
                t_qkv[t1], mm = qkv_mms(t1)
                if t1 == NT1 - 1:
                    on_qkv_done(mm)
            for t1 in range(NT1):
                t_at = scores(t1, t_qkv[t1])
                o_T(t1, t_at)
                if DRIVE_IN_SCORES:
                    drive(1)
                o_V(t1)

        role_f1 = {"lt": ln_t3[0], "xr": xrT2[0], "xk": xkT2[0], "ka": kact2[0],
                   "kT": kaT2[0], "u": xrT2[0], "lt_free": None, "u_free": None}
        role_f2 = {"lt": ln_t3[1], "xr": xrT2[1], "xk": xkT2[1], "ka": kact2[1],
                   "kT": kaT2[1], "u": xrT2[1], "lt_free": None, "u_free": None}
        role_l2 = role_f1

        f1res_prev = [None] * NT1

        def res_store(lst, evset, nm):
            def cb(t1, tok):
                lst[t1] = tok
                evset.add((nm, "x", t1))
            return cb

        events.add(("f1", "slots"))
        events.add(("f1", "lnt"))
        for t1 in range(NT1):
            events.add(("f1", "x", t1))
        q1 = ffn_pieces(0, LNT_A, carry1, carry1, lambda t1: x_tok[0],
                        res_store(f1res_prev, events, "l2"), "f1", role_f1,
                        p2_need=("f1", "lnt"),
                        on_mix_done=lambda: events.add(("l2", "lnok")))
        queues.append(q1)
        ln2_out = [None] * NT1
        q2 = ln_only_pieces(0, LNT_A, lambda t1: f1res_prev[t1], ln2_out, "l2",
                            role_l2)
        queues.append(q2)
        drain_all()
        tok_T0 = ("sT", r.semc["sT"])
        for si, widx in ((0, W_Q), (1, W_K), (2, W_V)):
            r.wait("sync", tok_T0)
            slot_tok[si] = dma4(slots[si], widx)

        for c in range(NCHUNK):
            par = c % 2
            events.clear()
            queues.clear()

            nm_f1 = "f1n"
            nm_f2 = "f2c"
            nm_l2 = "l2n"
            res2 = [None] * NT1
            f1res = [None] * NT1

            if c + 1 < NCHUNK:
                load_x(c + 1)
                for t1 in range(NT1):
                    events.add((nm_f1, "x", t1))
                queues.append(ffn_pieces((c + 1) % 2, LNT_A, carry1, carry1,
                                         lambda t1, p=(c + 1) % 2: x_tok[p],
                                         res_store(f1res, events, nm_l2), nm_f1,
                                         role_f1, p2_need=(nm_f1, "lnt"),
                                         on_mix_done=lambda: events.add(
                                             (nm_l2, "lnok"))))
            f2q = ffn_pieces(par, LNT_F, carry3, carry3,
                             lambda t1: res2[t1], None, nm_f2, role_f2)
            queues.append(f2q)
            ln2n = [None] * NT1
            if c + 1 < NCHUNK:
                queues.append(ln_only_pieces((c + 1) % 2, LNT_A,
                                             lambda t1: f1res[t1], ln2n, nm_l2,
                                             role_l2))

            def on_qkv_done(last_mm, c=c):
                st["lnT_users"][LNT_A] = last_mm
                for si, widx in ((0, W_RT), (1, W_KT), (2, W_VT)):
                    r.wait("sync", last_mm)
                    slot_tok[si] = dma4(slots[si], widx)
                events.add((nm_f1, "slots"))
                events.add((nm_f1, "lnt"))
                events.add((nm_f2, "slots"))

            def on_res(t1, tok):
                res2[t1] = tok
                events.add((nm_f2, "x", t1))

            if TEST_STAGE >= 2:
                attn(par, on_qkv_done, on_res)
            else:
                events.add((nm_f2, "slots"))
                for t1 in range(NT1):
                    events.add((nm_f2, "x", t1))
                    res2[t1] = st["x_last"]
            if TEST_STAGE < 3:
                queues[:] = [q for q in queues if q is not f2q]
            drain_all()

            if c + 1 < NCHUNK:
                slot_deps = ((role_f2["prev_mm_r"], role_f1.get("prev_mm_r")),
                             (role_f2["prev_mm_k"], role_f1.get("prev_mm_k")),
                             (role_f2["prev_mm_v"], role_f1.get("prev_mm_v")))
                for si, widx in ((0, W_Q), (1, W_K), (2, W_V)):
                    r.waits("sync", list(slot_deps[si]))
                    slot_tok[si] = dma4(slots[si], widx)

            r.wait("sync", st["x_last"])
            x_free[par] = dma(out[c * TCH:(c + 1) * TCH, :].rearrange(
                "(t2 p) d -> p t2 d", p=128), xbuf[par][:, :, :])

        r.op("sync", lambda e: e.nop())
        r.wait("sync", ("sD", r.semc["sD"]))
        r.wait("sync", ("sD2", r.semc["sD2"]))

        block = es.enter_context(nc.Block())
        r.emit(block)

    return nc


_cache = {}


def kernel(L0_attn_x, L0_attn_Wq, L0_attn_bq, L0_attn_Wk, L0_attn_bk, L0_attn_Wv,
           L0_attn_bv, L0_attn_Wo, L0_attn_bo, L0_ffn_W_r, L0_ffn_W_k, L0_ffn_W_v,
           L0_ffn_mu_r, L0_ffn_mu_k):
    x = np.ascontiguousarray(np.asarray(L0_attn_x, dtype=np.float32))
    ws = {
        "wq": L0_attn_Wq, "bq": L0_attn_bq, "wk": L0_attn_Wk, "bk": L0_attn_bk,
        "wv": L0_attn_Wv, "bv": L0_attn_bv, "wo": L0_attn_Wo, "bo": L0_attn_bo,
        "wr": L0_ffn_W_r, "wkf": L0_ffn_W_k, "wvf": L0_ffn_W_v,
        "mur": L0_ffn_mu_r, "muk": L0_ffn_mu_k,
    }
    ws = {k: np.ascontiguousarray(np.asarray(v, dtype=np.float32)) for k, v in ws.items()}
    if "nc" not in _cache:
        _cache["nc"] = build_program()
    nc = _cache["nc"]
    in_maps = [dict(ws, x=x[i]) for i in range(8)]
    res = run_bass_kernel_spmd(nc, in_maps, core_ids=list(range(8)))
    return np.stack([np.asarray(res.results[i]["out"]) for i in range(8)], axis=0)


# version 23 (final)
# speedup vs baseline: 1.0438x; 1.0034x over previous
import contextlib
import numpy as np

import concourse.bass as bass
import concourse.mybir as mybir
from concourse.bass_utils import run_bass_kernel_spmd

F32 = mybir.dt.float32
BF16 = mybir.dt.bfloat16
I32 = mybir.dt.int32
U8 = mybir.dt.uint8
AF = mybir.ActivationFunctionType
ALU = mybir.AluOpType
AX = mybir.AxisListType

T = 8192
DM = 1024
H = 16
CH = 64
CC = 8
TCH = 512
NCHUNK = T // TCH
NT1 = TCH // 128

ENGS = ("sync", "vector", "scalar", "tensor", "gpsimd")
DRIVE_IN_SCORES = True
DRAIN_STATS = []
TEST_STAGE = 3
QUAKE = 0x5F3759DF


class Rec:

    def __init__(self, nc, es):
        self.nc = nc
        self.items = {e: [] for e in ENGS}
        self.semh = {}
        self.semc = {}
        for name in ("sV", "sA", "sT", "sG", "sS", "sD", "sD2"):
            self.semh[name] = es.enter_context(nc.semaphore(name))
            self.semc[name] = 0
        self.eng_sem = {"vector": "sV", "scalar": "sA", "tensor": "sT",
                        "gpsimd": "sG", "sync": "sS"}

    def op(self, eng, fn):
        self.items[eng].append(["op", fn, None, None])

    def mark(self, eng, sem=None, n=1):
        sem = sem or self.eng_sem[eng]
        items = self.items[eng]
        for i in range(len(items) - 1, -1, -1):
            kind = items[i][0]
            if kind == "op":
                self.semc[sem] += n
                items[i] = ["opinc", items[i][1], self.semh[sem], n]
                return (sem, self.semc[sem])
            if kind == "opinc":
                raise RuntimeError(f"mark({eng}): last op already marked")
        raise RuntimeError(f"mark({eng}): no op to mark")

    def wait(self, eng, tok):
        if tok is None:
            return
        self.items[eng].append(["wait", None, self.semh[tok[0]], tok[1]])

    def waits(self, eng, toks):
        best = {}
        for t in toks:
            if t is None:
                continue
            if t[0] not in best or t[1] > best[t[0]]:
                best[t[0]] = t[1]
        for s, v in best.items():
            self.items[eng].append(["wait", None, self.semh[s], v])

    def chain(self, eng):
        tok = self.mark(eng)
        self.wait(eng, tok)
        return tok

    def emit(self, block):
        for ename in ENGS:
            sec = getattr(block, ename)
            items = self.items[ename]

            def body(e, items=items):
                for kind, fn, sem, val in items:
                    if kind == "op":
                        fn(e)
                    elif kind == "opinc":
                        fn(e).then_inc(sem, val)
                    else:
                        e.wait_ge(sem, val)

            sec(body)


def build_program():
    nc = bass.Bass("TRN2", target_bir_lowering=False)

    x_in = nc.dram_tensor("x", [T, DM], F32, kind="ExternalInput").ap()
    out = nc.dram_tensor("out", [T, DM], F32, kind="ExternalOutput").ap()
    wsrcs = {}
    for name in ("wq", "wk", "wv", "wo", "wr", "wkf", "wvf"):
        wsrcs[name] = nc.dram_tensor(name, [DM, DM], F32, kind="ExternalInput").ap()
    bsrcs = {}
    for name in ("bq", "bk", "bv", "bo", "mur", "muk"):
        bsrcs[name] = nc.dram_tensor(name, [DM], F32, kind="ExternalInput").ap()

    wbf = nc.dram_tensor("wbf", [7, 128, CC * DM], BF16, kind="Internal").ap()
    W_RT, W_KT, W_VT, W_Q, W_K, W_V, W_O = range(7)

    es = contextlib.ExitStack()
    with es:
        sb = lambda name, shape, dt: es.enter_context(nc.sbuf_tensor(name, shape, dt))
        psal = lambda name, shape, dt: es.enter_context(nc.psum_tensor(name, shape, dt))

        slots = [sb(f"wslot{i}", [128, CC * DM], BF16) for i in range(4)]
        S_WO = 3

        big = sb("big", [128, 16384], U8)
        stgf = [big[:, 0:4096].bitcast(F32), big[:, 4096:8192].bitcast(F32)]
        stgbf = [big[:, 8192:10240].bitcast(BF16), big[:, 10240:12288].bitcast(BF16)]
        tstage = big[:, 12288:14336].bitcast(BF16)
        brow_f32 = big[:, 0:4096].bitcast(F32)
        s_tiles = [big[:, 0:8192].bitcast(BF16), big[:, 8192:16384].bitcast(BF16)]

        e_tiles = [sb("e_tile0", [128, H * 4 * CH], BF16),
                   sb("e_tile1", [128, H * 4 * CH], BF16)]
        ident = sb("ident", [128, 128], BF16)
        ones_col = sb("ones_col", [1, 128], BF16)
        biasrow = sb("biasrow", [1, 4 * DM], BF16)
        murT = sb("murT", [128, CC], F32)
        mukT = sb("mukT", [128, CC], F32)
        onemur = sb("onemur", [128, CC], F32)
        onemuk = sb("onemuk", [128, CC], F32)
        xbuf = [sb(f"xbuf{i}", [128, NT1, DM], F32) for i in range(2)]
        lnTs = [sb(f"lnT{i}", [128, CC, 513], BF16) for i in range(2)]
        LNT_A, LNT_F = 0, 1
        ln_t3 = [sb(f"ln_t{i}", [128, DM], BF16) for i in range(2)]
        xrT2 = [sb(f"xrT{i}", [128, DM], BF16) for i in range(2)]
        xkT2 = [sb(f"xkT{i}", [128, DM], BF16) for i in range(2)]
        kact2 = [sb(f"kact{i}", [128, DM], BF16) for i in range(2)]
        kaT2 = [sb(f"kaT{i}", [128, DM], BF16) for i in range(2)]
        carry1 = sb("carry1", [128, CC], BF16)
        carry3 = sb("carry3", [128, CC], BF16)
        qts = [sb(f"qt{i}", [128, DM], BF16) for i in range(4)]
        kts = [sb(f"kt{i}", [128, DM], BF16) for i in range(4)]
        vts = [sb(f"vt{i}", [128, DM], BF16) for i in range(4)]
        d_tile = sb("d_tile", [128, DM], F32)
        n_tile = sb("n_tile", [128, DM], F32)
        bn6 = sb("bn6", [128, 12], F32)
        ag2 = sb("ag2", [128, 2], F32)
        v1 = bn6[:, 0:1]
        y1 = bn6[:, 1:2]
        t1b = bn6[:, 2:3]

        tmpf = sb("tmpf", [128, 512], F32)
        mixtv = sb("mixtv", [128, 512], BF16)
        pm = [psal(f"pm{i}", [128, 512], F32) for i in range(6)]
        ptrs = [psal("ptr0", [128, 1024], BF16), psal("ptr1", [128, 1024], BF16)]

        r = Rec(nc, es)

        st = {
            "ptr_free": [None, None],
            "ptr_i": [0],
            "x_last": None,
            "qkv_free": [None, None, None, None],
            "S_free": [None, None],
            "E_free": [None, None],
            "pm_free": {i: None for i in range(6)},
            "lnT_users": [None, None],
            "lnT_evac": [[None] * NT1, [None] * NT1],
            "lnT_carry": [None, None],
            "mixtv_free": None,
        }

        def dma4(dst_tile, widx):
            tok = None
            for qtr in range(4):
                r.op("sync", lambda e, dst_tile=dst_tile, widx=widx, qtr=qtr:
                     e.dma_start(out=dst_tile[:, qtr * 2048:(qtr + 1) * 2048],
                                 in_=wbf[widx, :, qtr * 2048:(qtr + 1) * 2048]))
                tok = r.mark("sync", "sD", 16)
            return tok

        def dma(dst, src, **kw):
            r.op("sync", lambda e, dst=dst, src=src, kw=kw: e.dma_start(
                out=dst, in_=src, **kw))
            return r.mark("sync", "sD", 16)

        r.op("gpsimd", lambda e: e.iota(ident[:], [[1, 128]], base=0,
                                        channel_multiplier=-1,
                                        allow_small_or_imprecise_dtypes=True))
        t_gi = r.mark("gpsimd")
        r.wait("vector", t_gi)
        r.op("vector", lambda e: e.tensor_scalar(out=ident[:], in0=ident[:], scalar1=0.0,
                                                 scalar2=None, op0=ALU.is_equal))
        t_ident = r.mark("vector")
        r.op("vector", lambda e: e.memset(ones_col[:], 1.0))
        t_ones = r.mark("vector")
        r.op("vector", lambda e: e.memset(carry1[:], 0.0))
        r.op("vector", lambda e: e.memset(carry3[:], 0.0))
        r.mark("vector")

        t_d = dma(murT[:], bsrcs["mur"].rearrange("(cc p) -> p cc", p=128),
                  allow_slow_non_contiguous=True)
        r.wait("vector", t_d)
        t_d = dma(mukT[:], bsrcs["muk"].rearrange("(cc p) -> p cc", p=128),
                  allow_slow_non_contiguous=True)
        r.wait("vector", t_d)
        r.op("vector", lambda e: e.tensor_scalar(out=onemur[:], in0=murT[:], scalar1=-1.0,
                                                 scalar2=1.0, op0=ALU.mult, op1=ALU.add))
        r.op("vector", lambda e: e.tensor_scalar(out=onemuk[:], in0=mukT[:], scalar1=-1.0,
                                                 scalar2=1.0, op0=ALU.mult, op1=ALU.add))
        t_mu = r.chain("vector")

        t_bias = None
        for i, (bname, scale) in enumerate((("bq", 0.125), ("bk", 1.0),
                                            ("bv", 1.0), ("bo", 1.0))):
            t_d = dma(brow_f32[0:1, :], bsrcs[bname].rearrange("(a d) -> a d", a=1))
            r.waits("vector", [t_d])
            r.op("vector", lambda e, i=i, scale=scale: e.tensor_scalar(
                out=biasrow[0:1, i * DM:(i + 1) * DM], in0=brow_f32[0:1, :],
                scalar1=scale, scalar2=None, op0=ALU.mult))
            t_bias = r.mark("vector")
            r.wait("sync", t_bias)

        r.op("scalar", lambda e: e.preload_act_table(AF.Exp))
        r.mark("scalar")

        conv_specs = [
            (W_Q, "wq", 0.125, False), (W_K, "wk", 1.0, False),
            (W_V, "wv", 1.0, False), (W_O, "wo", 1.0, False),
            (W_RT, "wr", 1.0, True), (W_KT, "wkf", 1.0, True),
            (W_VT, "wvf", 1.0, True),
        ]
        def dma_g(dst, srcap):
            r.op("gpsimd", lambda e, dst=dst, srcap=srcap: e.dma_start(
                out=dst, in_=srcap))
            return r.mark("gpsimd", "sD2", 16)

        stg_tok = [t_bias, None]
        bf_tok = [None, None]
        t_tstage_free = None
        for widx, wname, scale, transp in conv_specs:
            wsrc = wsrcs[wname]
            for cc in range(CC):
                par = cc % 2
                r.wait("sync", stg_tok[par])
                t_ld = dma(stgf[par][:, :], wsrc[cc * 128:(cc + 1) * 128, :])
                r.waits("vector", [t_ld, bf_tok[par]])
                r.op("vector", lambda e, par=par, scale=scale: e.tensor_scalar(
                    out=stgbf[par][:, :], in0=stgf[par][:, :], scalar1=scale,
                    scalar2=None, op0=ALU.mult))
                t_cv = r.mark("vector")
                stg_tok[par] = t_cv
                if not transp:
                    r.wait("sync", t_cv)
                    bf_tok[par] = dma(wbf[widx, :, cc * DM:(cc + 1) * DM],
                                      stgbf[par][:, :])
                else:
                    pi = st["ptr_i"][0]; st["ptr_i"][0] ^= 1
                    ptr = ptrs[pi]
                    r.waits("tensor", [t_cv, t_ident, st["ptr_free"][pi]])
                    for j in range(8):
                        r.op("tensor", lambda e, par=par, j=j, ptr=ptr: e.transpose(
                            ptr[:, j * 128:(j + 1) * 128],
                            stgbf[par][:, j * 128:(j + 1) * 128], ident[:]))
                    t_tr = r.mark("tensor")
                    bf_tok[par] = t_tr
                    r.waits("scalar", [t_tr, t_tstage_free])
                    r.op("scalar", lambda e, ptr=ptr: e.activation(out=tstage[:, :],
                                                          in_=ptr[:, :], func=AF.Copy))
                    t_ev = r.mark("scalar")
                    st["ptr_free"][pi] = t_ev
                    r.wait("sync", t_ev)
                    t_tstage_free = dma(
                        wbf[widx][:, :].rearrange("p (j d) -> p j d", j=8)
                        [:, :, cc * 128:(cc + 1) * 128],
                        tstage[:, :].rearrange("p (j d) -> p j d", j=8))

        slot_tok = [None] * 4
        r.waits("sync", [("sD", r.semc["sD"]), ("sD2", r.semc["sD2"])])
        for si, widx in ((0, W_RT), (1, W_KT), (2, W_VT), (S_WO, W_O)):
            slot_tok[si] = dma4(slots[si], widx)

        x_tok = [None, None]
        x_free = [None, None]

        def load_x(c):
            par = c % 2
            r.wait("sync", x_free[par])
            x_tok[par] = dma(xbuf[par][:, :, :],
                             x_in[c * TCH:(c + 1) * TCH, :].rearrange(
                                 "(t2 p) d -> p t2 d", p=128))

        load_x(0)

        def emit_ln_v(par, t1, deps, lt, lt_free):
            xb = xbuf[par]
            r.waits("vector", deps + [lt_free])
            r.op("vector", lambda e: e.bn_stats(bn6[:, 0:6], xb[:, t1, 0:512]))
            r.op("vector", lambda e: e.bn_stats(bn6[:, 6:12], xb[:, t1, 512:1024]))
            r.chain("vector")
            r.op("vector", lambda e: e.bn_aggr(ag2[:], bn6[:]))
            r.chain("vector")
            r.op("vector", lambda e: e.tensor_scalar(
                out=v1, in0=ag2[:, 1:2], scalar1=1e-5, scalar2=None, op0=ALU.add))
            r.chain("vector")
            v1i = v1.bitcast(I32)
            y1i = y1.bitcast(I32)
            r.op("vector", lambda e: e.tensor_copy(out=t1b, in_=v1i))
            r.chain("vector")
            r.op("vector", lambda e: e.tensor_scalar(
                out=t1b, in0=t1b, scalar1=-0.5, scalar2=float(QUAKE),
                op0=ALU.mult, op1=ALU.add))
            r.chain("vector")
            r.op("vector", lambda e: e.tensor_copy(out=y1i, in_=t1b[:]))
            r.chain("vector")
            for _ in range(2):
                r.op("vector", lambda e: e.tensor_tensor(out=t1b, in0=y1,
                                                         in1=y1, op=ALU.mult))
                r.chain("vector")
                r.op("vector", lambda e: e.tensor_tensor(out=t1b, in0=t1b,
                                                         in1=v1, op=ALU.mult))
                r.chain("vector")
                r.op("vector", lambda e: e.tensor_scalar(
                    out=t1b, in0=t1b, scalar1=-0.5, scalar2=1.5,
                    op0=ALU.mult, op1=ALU.add))
                r.chain("vector")
                r.op("vector", lambda e: e.tensor_tensor(out=y1, in0=y1,
                                                         in1=t1b, op=ALU.mult))
                r.chain("vector")
            r.op("vector", lambda e: e.tensor_scalar(
                out=lt[:, :], in0=xb[:, t1, :], scalar1=ag2[:, 0:1],
                scalar2=y1, op0=ALU.subtract, op1=ALU.mult))
            return r.mark("vector")

        def emit_lnT(li, t1, t_ln, carry, save_carry, lt):
            pi = st["ptr_i"][0]
            st["ptr_i"][0] ^= 1
            ptr = ptrs[pi]
            r.waits("tensor", [t_ln, t_ident, st["ptr_free"][pi],
                               st["lnT_users"][li]])
            for j in range(8):
                r.op("tensor", lambda e, j=j, lt=lt, ptr=ptr: e.transpose(
                    ptr[:, j * 128:(j + 1) * 128],
                    lt[:, j * 128:(j + 1) * 128], ident[:]))
            t_tr = r.mark("tensor")
            r.wait("scalar", t_tr)
            r.op("scalar", lambda e, li=li, t1=t1, ptr=ptr: e.activation(
                out=lnTs[li][:, :, 1 + t1 * 128:1 + (t1 + 1) * 128],
                in_=ptr[:].rearrange("p (j t) -> p j t", j=8), func=AF.Copy))
            tk = r.mark("scalar")
            st["lnT_evac"][li][t1] = tk
            st["ptr_free"][pi] = tk
            if t1 == 0 and carry is not None:
                r.waits("vector", [st["lnT_users"][li]])
                r.op("vector", lambda e, li=li, carry=carry: e.tensor_copy(
                    out=lnTs[li][:, :, 0:1], in_=carry[:].unsqueeze(2)))
                st["lnT_carry"][li] = r.mark("vector")
            if t1 == NT1 - 1 and save_carry is not None:
                r.waits("vector", [tk])
                r.op("vector", lambda e, li=li, save_carry=save_carry: e.tensor_copy(
                    out=save_carry[:].unsqueeze(2), in_=lnTs[li][:, :, 512:513]))
                r.mark("vector")
            return tk, t_tr

        def ffn_pieces(par, li, carry, save_carry, x_dep, res_cb, name,
                       role, p2_need=None, on_mix_done=None):
            toks = {}
            lt, xr_b, xk_b, ka_b, kT_b, u_b = (role["lt"], role["xr"], role["xk"],
                                               role["ka"], role["kT"], role["u"])
            q = []

            def piece(need, fn):
                q.append((need, fn))

            def mk_P1(t1):
                def go():
                    toks[("ln", t1)] = emit_ln_v(par, t1, [x_dep(t1)],
                                                 lt, role["lt_free"])
                return go

            def mk_P2(t1):
                def go():
                    _, t_tr = emit_lnT(li, t1, toks[("ln", t1)], carry,
                                       save_carry, lt)
                    role["lt_free"] = t_tr
                return go

            def mk_P3(t1):
                def go():
                    deps = [st["lnT_evac"][li][t1], t_mu,
                            toks.get(("mm_r", t1 - 1)), toks.get(("mm_k", t1 - 1)),
                            toks.get(("v_done", t1 - 1))]
                    if t1 > 0:
                        deps.append(st["lnT_evac"][li][t1 - 1])
                    else:
                        deps.append(st["lnT_carry"][li])
                        deps.append(role.get("prev_mm_r"))
                        deps.append(role.get("prev_mm_k"))
                        deps.append(role.get("prev_v_done"))
                    r.waits("vector", deps)
                    xr = xr_b[:].rearrange("p (cc t) -> p cc t", cc=CC)
                    xk = xk_b[:].rearrange("p (cc t) -> p cc t", cc=CC)
                    lo = 1 + t1 * 128
                    tv = mixtv[:].rearrange("p (a t) -> p a t", a=4)
                    for dst, muT, onemu in ((xr, murT, onemur), (xk, mukT, onemuk)):
                        for g in range(2):
                            r.waits("scalar", [st["mixtv_free"]])
                            for cc in range(g * 4, g * 4 + 4):
                                r.op("scalar", lambda e, muT=muT, cc=cc, lo=lo,
                                     li=li, tv=tv: e.activation(
                                         out=tv[:, cc % 4, :],
                                         in_=lnTs[li][:, cc, lo:lo + 128],
                                         func=AF.Copy,
                                         scale=muT[:, cc:cc + 1]))
                            t_p1 = r.mark("scalar")
                            r.waits("vector", [t_p1])
                            for cc in range(g * 4, g * 4 + 4):
                                r.op("vector", lambda e, dst=dst, onemu=onemu, cc=cc,
                                     lo=lo, li=li, tv=tv: e.scalar_tensor_tensor(
                                         out=dst[:, cc, :],
                                         in0=lnTs[li][:, cc, lo - 1:lo + 127],
                                         scalar=onemu[:, cc:cc + 1],
                                         in1=tv[:, cc % 4, :], op0=ALU.mult,
                                         op1=ALU.add))
                            tok_mix = r.chain("vector")
                            st["mixtv_free"] = tok_mix
                    toks[("mix", t1)] = tok_mix
                    if t1 == NT1 - 1:
                        st["lnT_users"][li] = toks[("mix", t1)]
                        if on_mix_done is not None:
                            on_mix_done()
                return go

            def evac_r(t1, half, bank, t_mm):
                r.waits("scalar", [t_mm, role.get("u_free")])
                r.op("scalar", lambda e, bank=bank, half=half: e.activation(
                    out=u_b[:, half * 512:(half + 1) * 512], in_=bank[:, :],
                    func=AF.Tanh, scale=0.5))
                return r.mark("scalar")

            def evac_k(t1, half, bank, t_mm):
                r.waits("scalar", [t_mm, toks.get(("kaT_tr", t1 - 1)),
                                   role.get("prev_kaT_tr") if t1 == 0 else None])
                r.op("scalar", lambda e, bank=bank, half=half: e.activation(
                    out=ka_b[:, half * 512:(half + 1) * 512], in_=bank[:, :],
                    func=AF.Relu))
                t_relu = r.mark("scalar")
                r.waits("vector", [t_relu])
                r.op("vector", lambda e, half=half: e.tensor_tensor(
                    out=ka_b[:, half * 512:(half + 1) * 512],
                    in0=ka_b[:, half * 512:(half + 1) * 512],
                    in1=ka_b[:, half * 512:(half + 1) * 512], op=ALU.mult))
                return r.mark("vector")

            def evac_v(t1, half, bank, t_mm):
                r.waits("vector", [t_mm])
                r.op("vector", lambda e, bank=bank, half=half:
                     e.scalar_tensor_tensor(
                         out=tmpf[:, :], in0=u_b[:, half * 512:(half + 1) * 512],
                         scalar=1.0, in1=bank[:, :], op0=ALU.add, op1=ALU.mult))
                r.chain("vector")
                r.op("vector", lambda e, t1=t1, half=half: e.scalar_tensor_tensor(
                    out=xbuf[par][:, t1, half * 512:(half + 1) * 512],
                    in0=tmpf[:, :], scalar=0.25,
                    in1=xbuf[par][:, t1, half * 512:(half + 1) * 512],
                    op0=ALU.mult, op1=ALU.add))
                tk = r.mark("vector")
                if half == 1:
                    role["u_free"] = tk
                    st["x_last"] = tk
                    if res_cb is not None:
                        res_cb(t1, tk)
                return tk

            def mk_mm(t1, phase, stat_of, slot_i, evac):
                def go():
                    stat, t_stat = stat_of()
                    sv = stat[:].rearrange("p (cc t) -> p cc t", cc=CC)
                    r.waits("tensor", [t_stat, slot_tok[slot_i],
                                       st["pm_free"][2], st["pm_free"][3]])
                    for cc in range(CC):
                        for half in range(2):
                            r.op("tensor", lambda e, cc=cc, half=half, sv=sv,
                                 slot_i=slot_i: e.matmul(
                                     pm[2 + half][:, :], lhsT=sv[:, cc, :],
                                     rhs=slots[slot_i][:, cc * DM + half * 512:
                                                       cc * DM + (half + 1) * 512],
                                     start=(cc == 0), stop=(cc == CC - 1),
                                     skip_group_check=True))
                    t_mm = r.mark("tensor")
                    toks[("mm_" + phase, t1)] = t_mm
                    tks = [evac(t1, 0, pm[2], t_mm), evac(t1, 1, pm[3], t_mm)]
                    st["pm_free"][2] = tks[0]
                    st["pm_free"][3] = tks[1]
                    if phase == "v":
                        toks[("v_done", t1)] = tks[1]
                return go

            def mk_P6(t1):
                def go():
                    t_k = toks[("mm_k_done", t1)]
                    pi = st["ptr_i"][0]; st["ptr_i"][0] ^= 1
                    ptr = ptrs[pi]
                    r.waits("tensor", [t_k, st["ptr_free"][pi],
                                       toks.get(("mm_v", t1 - 1)),
                                       role.get("prev_mm_v") if t1 == 0 else None])
                    for j in range(8):
                        r.op("tensor", lambda e, j=j, ptr=ptr: e.transpose(
                            ptr[:, j * 128:(j + 1) * 128],
                            ka_b[:, j * 128:(j + 1) * 128], ident[:]))
                    t_tr = r.mark("tensor")
                    toks[("kaT_tr", t1)] = t_tr
                    r.wait("scalar", t_tr)
                    r.op("scalar", lambda e, ptr=ptr: e.activation(
                        out=kT_b[:, :], in_=ptr[:, :], func=AF.Copy))
                    tk = r.mark("scalar")
                    st["ptr_free"][pi] = tk
                    toks[("kaT", t1)] = tk
                return go

            for t1 in range(NT1):
                piece((name, "x", t1), mk_P1(t1))
                piece(p2_need, mk_P2(t1))
                piece(None, mk_P3(t1))
                piece((name, "slots"),
                      mk_mm(t1, "r", (lambda t1=t1: (xr_b, toks[("mix", t1)])),
                            0, evac_r))
                piece(None,
                      mk_mm(t1, "k", (lambda t1=t1: (xk_b, toks[("mix", t1)])),
                            1, evac_k))

                def fin_k(t1=t1):
                    toks[("mm_k_done", t1)] = ("sV", r.semc["sV"])
                piece(None, fin_k)
                piece(None, mk_P6(t1))
                piece(None,
                      mk_mm(t1, "v", (lambda t1=t1: (kT_b, toks[("kaT", t1)])),
                            2, evac_v))
            def fin_all():
                role["prev_mm_r"] = toks[("mm_r", NT1 - 1)]
                role["prev_mm_k"] = toks[("mm_k", NT1 - 1)]
                role["prev_mm_v"] = toks[("mm_v", NT1 - 1)]
                role["prev_kaT_tr"] = toks[("kaT_tr", NT1 - 1)]
                role["prev_v_done"] = toks[("v_done", NT1 - 1)]
            piece(None, fin_all)
            return q

        def ln_only_pieces(par, li, x_dep, out_toks, name, role):
            q = []
            toks = {}
            lt = role["lt"]

            def mk_P1(t1):
                def go():
                    toks[("ln", t1)] = emit_ln_v(par, t1, [x_dep(t1)],
                                                 lt, role["lt_free"])
                return go

            def mk_P2(t1):
                def go():
                    out_toks[t1], t_tr = emit_lnT(li, t1, toks[("ln", t1)],
                                                  None, None, lt)
                    role["lt_free"] = t_tr
                return go

            for t1 in range(NT1):
                q.append(([(name, "x", t1), (name, "lnok")], mk_P1(t1)))
                q.append((None, mk_P2(t1)))
            return q

        queues = []
        events = set()

        def drive(n=1):
            done = 0
            for _ in range(n):
                fired = False
                for qq in queues:
                    if not qq:
                        continue
                    need, fn = qq[0]
                    ok = (need is None or
                          (isinstance(need, list) and all(n in events for n in need))
                          or (not isinstance(need, list) and need in events))
                    if ok:
                        qq.pop(0)
                        fn()
                        fired = True
                        done += 1
                        break
                if not fired:
                    break
            return done

        def drain_all():
            import sys
            left = sum(len(qq) for qq in queues)
            if left:
                DRAIN_STATS.append(left)
            while True:
                if drive(1) == 0:
                    live = [qq[0][0] for qq in queues if qq]
                    assert not live, f"undrivable pieces: {live}"
                    break

        def attn(par, on_qkv_done, on_res):
            t_omm = {}
            t_aT_tr = {-1: None}

            def qkv_mms(t1):
                pp = t1
                toks_ev = {}
                t_mm = None
                for wi in range(3):
                    b0 = 2 if wi == 1 else 0
                    r.waits("tensor", [st["lnT_evac"][LNT_A][t1], slot_tok[wi],
                                       t_ones, t_bias,
                                       st["pm_free"][b0], st["pm_free"][b0 + 1]])
                    for cc in range(CC):
                        for half in range(2):
                            r.op("tensor", lambda e, t1=t1, cc=cc, half=half, wi=wi,
                                 b0=b0: e.matmul(
                                     pm[b0 + half][:, :],
                                     lhsT=lnTs[LNT_A][:, cc,
                                                      1 + t1 * 128:1 + (t1 + 1) * 128],
                                     rhs=slots[wi][:, cc * DM + half * 512:
                                                   cc * DM + (half + 1) * 512],
                                     start=(cc == 0), stop=False,
                                     skip_group_check=True))
                    for half in range(2):
                        r.op("tensor", lambda e, wi=wi, half=half, b0=b0: e.matmul(
                            pm[b0 + half][:, :], lhsT=ones_col[:, :],
                            rhs=biasrow[0:1, wi * DM + half * 512:
                                        wi * DM + (half + 1) * 512],
                            start=False, stop=True, skip_group_check=True))
                    t_mm = r.mark("tensor")
                    dst = (qts, kts, vts)[wi][pp]
                    r.waits("scalar", [t_mm, st["qkv_free"][pp]])

                    for half in range(2):
                        r.op("scalar", lambda e, dst=dst, half=half, b0=b0:
                             e.activation(
                                 out=dst[:, half * 512:(half + 1) * 512],
                                 in_=pm[b0 + half][:, :], func=AF.Copy))
                    t_ev = r.mark("scalar")
                    toks_ev[wi] = t_ev
                    st["pm_free"][b0] = t_ev
                    st["pm_free"][b0 + 1] = t_ev
                return (toks_ev[1], toks_ev[2]), t_mm

            def scores(t1, t_qkv):
                t_qk, t_v = t_qkv
                pp = t1
                qt, kt, vt = qts[pp], kts[pp], vts[pp]
                q_h = qt[:].rearrange("p (h q) -> p h q", h=H)
                k_h = kt[:].rearrange("p (h k) -> p h k", h=H)
                v_h = vt[:].rearrange("p (h k) -> p h k", h=H)
                d_h = d_tile[:].rearrange("p (h q) -> p h q", h=H)
                n_h = n_tile[:].rearrange("p (h q) -> p h q", h=H)

                r.waits("gpsimd", [st["E_free"][0], st["E_free"][1]])
                r.op("gpsimd", lambda e: e.memset(e_tiles[0][:].bitcast(I32), 0))
                r.op("gpsimd", lambda e: e.memset(e_tiles[1][:].bitcast(I32), 0))
                t_ez = r.mark("gpsimd")
                st["E_free"][0] = t_ez
                st["E_free"][1] = t_ez

                def s_band(b):
                    kmax = 4 * b + 4
                    sv = s_tiles[b % 2][:, 0:H * 4 * CH].rearrange(
                        "p (h j k) -> p h j k", h=H, j=4)
                    r.waits("vector", [t_qk, st["S_free"][b % 2]])
                    r.op("vector", lambda e, b=b, kmax=kmax, sv=sv, q_h=q_h, k_h=k_h:
                         e.tensor_tensor(
                             out=sv[:, :, :, 0:kmax],
                             in0=k_h[:, :, 0:kmax].unsqueeze(2).broadcast_to(
                                 [128, H, 4, kmax]),
                             in1=q_h[:, :, 4 * b:4 * b + 4].unsqueeze(3).broadcast_to(
                                 [128, H, 4, kmax]), op=ALU.mult))
                    return r.mark("vector")

                t_Sb = {0: s_band(0)}
                t_EN = None
                r.wait("vector", t_v)
                for b in range(16):
                    if DRIVE_IN_SCORES:
                        drive(1)
                    if b + 1 < 16:
                        t_Sb[b + 1] = s_band(b + 1)
                    kmax = 4 * b + 4
                    half = kmax // 2
                    ep = e_tiles[b % 2]
                    ev = ep[:].rearrange("p (h j k) -> p h j k", h=H, j=4)
                    sv = s_tiles[b % 2][:, 0:H * 4 * CH].rearrange(
                        "p (h j k) -> p h j k", h=H, j=4)
                    fo = s_tiles[b % 2][:, 0:H * 4 * half].rearrange(
                        "p (h j k) -> p h j k", h=H, j=4)
                    r.waits("scalar", [t_Sb[b], st["E_free"][b % 2]])
                    for j in range(4):
                        qq = 4 * b + j
                        r.op("scalar", lambda e, j=j, qq=qq, sv=sv, ev=ev: e.activation(
                            out=ev[:, :, j, 0:qq + 1], in_=sv[:, :, j, 0:qq + 1],
                            func=AF.Exp))
                    t_E = r.mark("scalar")
                    st["S_free"][b % 2] = t_E
                    r.wait("vector", t_E)
                    q4 = half // 2
                    r.op("vector", lambda e, ev=ev, fo=fo, half=half, kmax=kmax:
                         e.tensor_tensor(out=fo[:, :, :, 0:half],
                                         in0=ev[:, :, :, 0:half],
                                         in1=ev[:, :, :, half:kmax], op=ALU.add))
                    if q4 > 0:
                        r.op("vector", lambda e, fo=fo, q4=q4, half=half:
                             e.tensor_tensor(out=fo[:, :, :, 0:q4],
                                             in0=fo[:, :, :, 0:q4],
                                             in1=fo[:, :, :, q4:half], op=ALU.add))
                    q8 = 0
                    if q8 > 0:
                        r.op("vector", lambda e, fo=fo, q8=q8, q4=q4:
                             e.tensor_tensor(out=fo[:, :, :, 0:q8],
                                             in0=fo[:, :, :, 0:q8],
                                             in1=fo[:, :, :, q8:q4], op=ALU.add))
                    rw = q8 if q8 > 0 else q4
                    r.op("vector", lambda e, b=b, fo=fo, rw=rw, d_h=d_h:
                         e.tensor_reduce(
                             d_h[:, :, 4 * b:4 * b + 4].unsqueeze(3),
                             fo[:, :, :, 0:rw], axis=AX.X, op=ALU.add))
                    r.op("vector", lambda e, ev=ev, v_h=v_h, kmax=kmax:
                         e.tensor_tensor(
                             out=ev[:, :, :, 0:kmax], in0=ev[:, :, :, 0:kmax],
                             in1=v_h[:, :, 0:kmax].unsqueeze(2).broadcast_to(
                                 [128, H, 4, kmax]), op=ALU.mult))
                    r.op("vector", lambda e, ev=ev, fo=fo, half=half, kmax=kmax:
                         e.tensor_tensor(out=fo[:, :, :, 0:half],
                                         in0=ev[:, :, :, 0:half],
                                         in1=ev[:, :, :, half:kmax], op=ALU.add))
                    if q4 > 0:
                        r.op("vector", lambda e, fo=fo, q4=q4, half=half:
                             e.tensor_tensor(out=fo[:, :, :, 0:q4],
                                             in0=fo[:, :, :, 0:q4],
                                             in1=fo[:, :, :, q4:half], op=ALU.add))
                    if q8 > 0:
                        r.op("vector", lambda e, fo=fo, q8=q8, q4=q4:
                             e.tensor_tensor(out=fo[:, :, :, 0:q8],
                                             in0=fo[:, :, :, 0:q8],
                                             in1=fo[:, :, :, q8:q4], op=ALU.add))
                    r.op("vector", lambda e, b=b, fo=fo, rw=rw, n_h=n_h:
                         e.tensor_reduce(
                             n_h[:, :, 4 * b:4 * b + 4].unsqueeze(3),
                             fo[:, :, :, 0:rw], axis=AX.X, op=ALU.add))
                    t_EN = r.mark("vector")
                    st["E_free"][b % 2] = t_EN
                r.wait("vector", t_EN)
                d_rcp = big[:, 0:4096].bitcast(F32)
                r.op("vector", lambda e, d_rcp=d_rcp: e.reciprocal(out=d_rcp[:, :],
                                                                   in_=d_tile[:]))
                r.chain("vector")
                r.op("vector", lambda e, d_rcp=d_rcp, qt=qt: e.tensor_tensor(
                    out=qt[:, :], in0=n_tile[:], in1=d_rcp[:, :], op=ALU.mult))
                t_at = r.mark("vector")
                st["S_free"][0] = t_at
                return t_at

            def o_T(t1, t_at):
                pi = st["ptr_i"][0]; st["ptr_i"][0] ^= 1
                ptr = ptrs[pi]
                r.waits("tensor", [t_at, st["ptr_free"][pi]])
                for j in range(8):
                    r.op("tensor", lambda e, j=j, t1=t1, ptr=ptr: e.transpose(
                        ptr[:, j * 128:(j + 1) * 128],
                        qts[t1][:, j * 128:(j + 1) * 128], ident[:]))
                t_tr = r.mark("tensor")
                t_aT_tr[t1] = t_tr
                r.wait("scalar", t_tr)
                r.op("scalar", lambda e, t1=t1, ptr=ptr: e.activation(
                    out=kts[t1][:, :], in_=ptr[:, :], func=AF.Copy))
                t_aT = r.mark("scalar")
                st["ptr_free"][pi] = t_aT
                r.waits("tensor", [t_aT, slot_tok[S_WO],
                                   st["pm_free"][4], st["pm_free"][5]])
                aTv = kts[t1][:].rearrange("p (cc t) -> p cc t", cc=CC)
                for cc in range(CC):
                    for half in range(2):
                        r.op("tensor", lambda e, cc=cc, half=half: e.matmul(
                            pm[4 + half][:, :], lhsT=aTv[:, cc, :],
                            rhs=slots[S_WO][:, cc * DM + half * 512:
                                            cc * DM + (half + 1) * 512],
                            start=(cc == 0), stop=False, skip_group_check=True))
                for half in range(2):
                    r.op("tensor", lambda e, half=half: e.matmul(
                        pm[4 + half][:, :], lhsT=ones_col[:, :],
                        rhs=biasrow[0:1, 3 * DM + half * 512:
                                    3 * DM + (half + 1) * 512],
                        start=False, stop=True, skip_group_check=True))
                t_omm[t1] = r.mark("tensor")

            def o_V(t1):
                r.wait("vector", t_omm[t1])
                r.op("vector", lambda e, t1=t1: e.tensor_tensor(
                    out=xbuf[par][:, t1, 0:512], in0=xbuf[par][:, t1, 0:512],
                    in1=pm[4][:, :], op=ALU.add))
                r.op("vector", lambda e, t1=t1: e.tensor_tensor(
                    out=xbuf[par][:, t1, 512:1024], in0=xbuf[par][:, t1, 512:1024],
                    in1=pm[5][:, :], op=ALU.add))
                tk = r.mark("vector")
                st["x_last"] = tk
                st["pm_free"][4] = tk
                st["pm_free"][5] = tk
                st["qkv_free"][t1] = tk
                on_res(t1, tk)

            t_qkv = {}
            for t1 in range(NT1):
                t_qkv[t1], mm = qkv_mms(t1)
                if t1 == NT1 - 1:
                    on_qkv_done(mm)
            for t1 in range(NT1):
                t_at = scores(t1, t_qkv[t1])
                o_T(t1, t_at)
                if DRIVE_IN_SCORES:
                    drive(1)
                o_V(t1)

        role_f1 = {"lt": ln_t3[0], "xr": xrT2[0], "xk": xkT2[0], "ka": kact2[0],
                   "kT": kaT2[0], "u": xrT2[0], "lt_free": None, "u_free": None}
        role_f2 = {"lt": ln_t3[1], "xr": xrT2[1], "xk": xkT2[1], "ka": kact2[1],
                   "kT": kaT2[1], "u": xrT2[1], "lt_free": None, "u_free": None}
        role_l2 = role_f1

        f1res_prev = [None] * NT1

        def res_store(lst, evset, nm):
            def cb(t1, tok):
                lst[t1] = tok
                evset.add((nm, "x", t1))
            return cb

        events.add(("f1", "slots"))
        events.add(("f1", "lnt"))
        for t1 in range(NT1):
            events.add(("f1", "x", t1))
        q1 = ffn_pieces(0, LNT_A, carry1, carry1, lambda t1: x_tok[0],
                        res_store(f1res_prev, events, "l2"), "f1", role_f1,
                        p2_need=("f1", "lnt"),
                        on_mix_done=lambda: events.add(("l2", "lnok")))
        queues.append(q1)
        ln2_out = [None] * NT1
        q2 = ln_only_pieces(0, LNT_A, lambda t1: f1res_prev[t1], ln2_out, "l2",
                            role_l2)
        queues.append(q2)
        drain_all()
        tok_T0 = ("sT", r.semc["sT"])
        for si, widx in ((0, W_Q), (1, W_K), (2, W_V)):
            r.wait("sync", tok_T0)
            slot_tok[si] = dma4(slots[si], widx)

        for c in range(NCHUNK):
            par = c % 2
            events.clear()
            queues.clear()

            nm_f1 = "f1n"
            nm_f2 = "f2c"
            nm_l2 = "l2n"
            res2 = [None] * NT1
            f1res = [None] * NT1

            if c + 1 < NCHUNK:
                load_x(c + 1)
                for t1 in range(NT1):
                    events.add((nm_f1, "x", t1))
                queues.append(ffn_pieces((c + 1) % 2, LNT_A, carry1, carry1,
                                         lambda t1, p=(c + 1) % 2: x_tok[p],
                                         res_store(f1res, events, nm_l2), nm_f1,
                                         role_f1, p2_need=(nm_f1, "lnt"),
                                         on_mix_done=lambda: events.add(
                                             (nm_l2, "lnok"))))
            f2q = ffn_pieces(par, LNT_F, carry3, carry3,
                             lambda t1: res2[t1], None, nm_f2, role_f2)
            queues.append(f2q)
            ln2n = [None] * NT1
            if c + 1 < NCHUNK:
                queues.append(ln_only_pieces((c + 1) % 2, LNT_A,
                                             lambda t1: f1res[t1], ln2n, nm_l2,
                                             role_l2))

            def on_qkv_done(last_mm, c=c):
                st["lnT_users"][LNT_A] = last_mm
                for si, widx in ((0, W_RT), (1, W_KT), (2, W_VT)):
                    r.wait("sync", last_mm)
                    slot_tok[si] = dma4(slots[si], widx)
                events.add((nm_f1, "slots"))
                events.add((nm_f1, "lnt"))
                events.add((nm_f2, "slots"))

            def on_res(t1, tok):
                res2[t1] = tok
                events.add((nm_f2, "x", t1))

            if TEST_STAGE >= 2:
                attn(par, on_qkv_done, on_res)
            else:
                events.add((nm_f2, "slots"))
                for t1 in range(NT1):
                    events.add((nm_f2, "x", t1))
                    res2[t1] = st["x_last"]
            if TEST_STAGE < 3:
                queues[:] = [q for q in queues if q is not f2q]
            drain_all()

            if c + 1 < NCHUNK:
                slot_deps = ((role_f2["prev_mm_r"], role_f1.get("prev_mm_r")),
                             (role_f2["prev_mm_k"], role_f1.get("prev_mm_k")),
                             (role_f2["prev_mm_v"], role_f1.get("prev_mm_v")))
                for si, widx in ((0, W_Q), (1, W_K), (2, W_V)):
                    r.waits("sync", list(slot_deps[si]))
                    slot_tok[si] = dma4(slots[si], widx)

            r.wait("sync", st["x_last"])
            x_free[par] = dma(out[c * TCH:(c + 1) * TCH, :].rearrange(
                "(t2 p) d -> p t2 d", p=128), xbuf[par][:, :, :])

        r.op("sync", lambda e: e.nop())
        r.wait("sync", ("sD", r.semc["sD"]))
        r.wait("sync", ("sD2", r.semc["sD2"]))

        block = es.enter_context(nc.Block())
        r.emit(block)

    return nc


_cache = {}


def kernel(L0_attn_x, L0_attn_Wq, L0_attn_bq, L0_attn_Wk, L0_attn_bk, L0_attn_Wv,
           L0_attn_bv, L0_attn_Wo, L0_attn_bo, L0_ffn_W_r, L0_ffn_W_k, L0_ffn_W_v,
           L0_ffn_mu_r, L0_ffn_mu_k):
    x = np.ascontiguousarray(np.asarray(L0_attn_x, dtype=np.float32))
    ws = {
        "wq": L0_attn_Wq, "bq": L0_attn_bq, "wk": L0_attn_Wk, "bk": L0_attn_bk,
        "wv": L0_attn_Wv, "bv": L0_attn_bv, "wo": L0_attn_Wo, "bo": L0_attn_bo,
        "wr": L0_ffn_W_r, "wkf": L0_ffn_W_k, "wvf": L0_ffn_W_v,
        "mur": L0_ffn_mu_r, "muk": L0_ffn_mu_k,
    }
    ws = {k: np.ascontiguousarray(np.asarray(v, dtype=np.float32)) for k, v in ws.items()}
    if "nc" not in _cache:
        _cache["nc"] = build_program()
    nc = _cache["nc"]
    in_maps = [dict(ws, x=x[i]) for i in range(8)]
    res = run_bass_kernel_spmd(nc, in_maps, core_ids=list(range(8)))
    return np.stack([np.asarray(res.results[i]["out"]) for i in range(8)], axis=0)


# version 25
# speedup vs baseline: 1.0453x; 1.0014x over previous
import contextlib
import numpy as np

import concourse.bass as bass
import concourse.mybir as mybir
from concourse.bass_utils import run_bass_kernel_spmd

F32 = mybir.dt.float32
BF16 = mybir.dt.bfloat16
I32 = mybir.dt.int32
U8 = mybir.dt.uint8
AF = mybir.ActivationFunctionType
ALU = mybir.AluOpType
AX = mybir.AxisListType

T = 8192
DM = 1024
H = 16
CH = 64
CC = 8
TCH = 512
NCHUNK = T // TCH
NT1 = TCH // 128

ENGS = ("sync", "vector", "scalar", "tensor", "gpsimd")
DRIVE_IN_SCORES = True
DRAIN_STATS = []
TEST_STAGE = 3
QUAKE = 0x5F3759DF


class Rec:

    def __init__(self, nc, es):
        self.nc = nc
        self.items = {e: [] for e in ENGS}
        self.semh = {}
        self.semc = {}
        for name in ("sV", "sA", "sT", "sG", "sS", "sD", "sD2"):
            self.semh[name] = es.enter_context(nc.semaphore(name))
            self.semc[name] = 0
        self.eng_sem = {"vector": "sV", "scalar": "sA", "tensor": "sT",
                        "gpsimd": "sG", "sync": "sS"}

    def op(self, eng, fn):
        self.items[eng].append(["op", fn, None, None])

    def mark(self, eng, sem=None, n=1):
        sem = sem or self.eng_sem[eng]
        items = self.items[eng]
        for i in range(len(items) - 1, -1, -1):
            kind = items[i][0]
            if kind == "op":
                self.semc[sem] += n
                items[i] = ["opinc", items[i][1], self.semh[sem], n]
                return (sem, self.semc[sem])
            if kind == "opinc":
                raise RuntimeError(f"mark({eng}): last op already marked")
        raise RuntimeError(f"mark({eng}): no op to mark")

    def wait(self, eng, tok):
        if tok is None:
            return
        self.items[eng].append(["wait", None, self.semh[tok[0]], tok[1]])

    def waits(self, eng, toks):
        best = {}
        for t in toks:
            if t is None:
                continue
            if t[0] not in best or t[1] > best[t[0]]:
                best[t[0]] = t[1]
        for s, v in best.items():
            self.items[eng].append(["wait", None, self.semh[s], v])

    def chain(self, eng):
        tok = self.mark(eng)
        self.wait(eng, tok)
        return tok

    def emit(self, block):
        for ename in ENGS:
            sec = getattr(block, ename)
            items = self.items[ename]

            def body(e, items=items):
                for kind, fn, sem, val in items:
                    if kind == "op":
                        fn(e)
                    elif kind == "opinc":
                        fn(e).then_inc(sem, val)
                    else:
                        e.wait_ge(sem, val)

            sec(body)


def build_program():
    nc = bass.Bass("TRN2", target_bir_lowering=False)

    x_in = nc.dram_tensor("x", [T, DM], F32, kind="ExternalInput").ap()
    out = nc.dram_tensor("out", [T, DM], F32, kind="ExternalOutput").ap()
    wsrcs = {}
    for name in ("wq", "wk", "wv", "wo", "wr", "wkf", "wvf"):
        wsrcs[name] = nc.dram_tensor(name, [DM, DM], F32, kind="ExternalInput").ap()
    bsrcs = {}
    for name in ("bq", "bk", "bv", "bo", "mur", "muk"):
        bsrcs[name] = nc.dram_tensor(name, [DM], F32, kind="ExternalInput").ap()

    wbf = nc.dram_tensor("wbf", [7, 128, CC * DM], BF16, kind="Internal").ap()
    W_RT, W_KT, W_VT, W_Q, W_K, W_V, W_O = range(7)

    es = contextlib.ExitStack()
    with es:
        sb = lambda name, shape, dt: es.enter_context(nc.sbuf_tensor(name, shape, dt))
        psal = lambda name, shape, dt: es.enter_context(nc.psum_tensor(name, shape, dt))

        slots = [sb(f"wslot{i}", [128, CC * DM], BF16) for i in range(4)]
        S_WO = 3

        big = sb("big", [128, 16384], U8)
        stgf = [big[:, 0:4096].bitcast(F32), big[:, 4096:8192].bitcast(F32)]
        stgbf = [big[:, 8192:10240].bitcast(BF16), big[:, 10240:12288].bitcast(BF16)]
        tstage = big[:, 12288:14336].bitcast(BF16)
        brow_f32 = big[:, 0:4096].bitcast(F32)
        s_tiles = [big[:, 0:8192].bitcast(BF16), big[:, 8192:16384].bitcast(BF16)]

        e_tiles = [sb("e_tile0", [128, H * 4 * CH], BF16),
                   sb("e_tile1", [128, H * 4 * CH], BF16)]
        ident = sb("ident", [128, 128], BF16)
        ones_col = sb("ones_col", [1, 128], BF16)
        biasrow = sb("biasrow", [1, 4 * DM], BF16)
        murT = sb("murT", [128, CC], F32)
        mukT = sb("mukT", [128, CC], F32)
        onemur = sb("onemur", [128, CC], F32)
        onemuk = sb("onemuk", [128, CC], F32)
        xbuf = [sb(f"xbuf{i}", [128, NT1, DM], F32) for i in range(2)]
        lnTs = [sb(f"lnT{i}", [128, CC, 513], BF16) for i in range(2)]
        LNT_A, LNT_F = 0, 1
        ln_t3 = [sb(f"ln_t{i}", [128, DM], BF16) for i in range(2)]
        xrT2 = [sb(f"xrT{i}", [128, DM], BF16) for i in range(2)]
        xkT2 = [sb(f"xkT{i}", [128, DM], BF16) for i in range(2)]
        kact2 = [sb(f"kact{i}", [128, DM], BF16) for i in range(2)]
        kaT2 = [sb(f"kaT{i}", [128, DM], BF16) for i in range(2)]
        carry1 = sb("carry1", [128, CC], BF16)
        carry3 = sb("carry3", [128, CC], BF16)
        qts = [sb(f"qt{i}", [128, DM], BF16) for i in range(4)]
        kts = [sb(f"kt{i}", [128, DM], BF16) for i in range(4)]
        vts = [sb(f"vt{i}", [128, DM], BF16) for i in range(4)]
        d_tile = sb("d_tile", [128, DM], F32)
        n_tile = sb("n_tile", [128, DM], F32)
        bn6 = sb("bn6", [128, 12], F32)
        ag2 = sb("ag2", [128, 2], F32)
        v1 = bn6[:, 0:1]
        y1 = bn6[:, 1:2]
        t1b = bn6[:, 2:3]

        tmpf = sb("tmpf", [128, 512], F32)
        mixtv = sb("mixtv", [128, 512], BF16)
        pm = [psal(f"pm{i}", [128, 512], F32) for i in range(6)]
        ptrs = [psal("ptr0", [128, 1024], BF16), psal("ptr1", [128, 1024], BF16)]

        r = Rec(nc, es)

        st = {
            "ptr_free": [None, None],
            "ptr_i": [0],
            "x_last": None,
            "qkv_free": [None, None, None, None],
            "S_free": [None, None],
            "E_free": [None, None],
            "pm_free": {i: None for i in range(6)},
            "lnT_users": [None, None],
            "lnT_evac": [[None] * NT1, [None] * NT1],
            "lnT_carry": [None, None],
            "mixtv_free": None,
        }

        def dma4(dst_tile, widx):
            tok = None
            for qtr in range(4):
                r.op("sync", lambda e, dst_tile=dst_tile, widx=widx, qtr=qtr:
                     e.dma_start(out=dst_tile[:, qtr * 2048:(qtr + 1) * 2048],
                                 in_=wbf[widx, :, qtr * 2048:(qtr + 1) * 2048]))
                tok = r.mark("sync", "sD", 16)
            return tok

        def dma(dst, src, **kw):
            r.op("sync", lambda e, dst=dst, src=src, kw=kw: e.dma_start(
                out=dst, in_=src, **kw))
            return r.mark("sync", "sD", 16)

        r.op("gpsimd", lambda e: e.iota(ident[:], [[1, 128]], base=0,
                                        channel_multiplier=-1,
                                        allow_small_or_imprecise_dtypes=True))
        t_gi = r.mark("gpsimd")
        r.wait("vector", t_gi)
        r.op("vector", lambda e: e.tensor_scalar(out=ident[:], in0=ident[:], scalar1=0.0,
                                                 scalar2=None, op0=ALU.is_equal))
        t_ident = r.mark("vector")
        r.op("vector", lambda e: e.memset(ones_col[:], 1.0))
        t_ones = r.mark("vector")
        r.op("vector", lambda e: e.memset(carry1[:], 0.0))
        r.op("vector", lambda e: e.memset(carry3[:], 0.0))
        r.mark("vector")

        t_d = dma(murT[:], bsrcs["mur"].rearrange("(cc p) -> p cc", p=128),
                  allow_slow_non_contiguous=True)
        r.wait("vector", t_d)
        t_d = dma(mukT[:], bsrcs["muk"].rearrange("(cc p) -> p cc", p=128),
                  allow_slow_non_contiguous=True)
        r.wait("vector", t_d)
        r.op("vector", lambda e: e.tensor_scalar(out=onemur[:], in0=murT[:], scalar1=-1.0,
                                                 scalar2=1.0, op0=ALU.mult, op1=ALU.add))
        r.op("vector", lambda e: e.tensor_scalar(out=onemuk[:], in0=mukT[:], scalar1=-1.0,
                                                 scalar2=1.0, op0=ALU.mult, op1=ALU.add))
        t_mu = r.chain("vector")

        t_bias = None
        for i, (bname, scale) in enumerate((("bq", 0.125), ("bk", 1.0),
                                            ("bv", 1.0), ("bo", 1.0))):
            t_d = dma(brow_f32[0:1, :], bsrcs[bname].rearrange("(a d) -> a d", a=1))
            r.waits("vector", [t_d])
            r.op("vector", lambda e, i=i, scale=scale: e.tensor_scalar(
                out=biasrow[0:1, i * DM:(i + 1) * DM], in0=brow_f32[0:1, :],
                scalar1=scale, scalar2=None, op0=ALU.mult))
            t_bias = r.mark("vector")
            r.wait("sync", t_bias)

        r.op("scalar", lambda e: e.preload_act_table(AF.Exp))
        r.mark("scalar")

        conv_specs = [
            (W_Q, "wq", 0.125, False), (W_K, "wk", 1.0, False),
            (W_V, "wv", 1.0, False), (W_O, "wo", 1.0, False),
            (W_RT, "wr", 1.0, True), (W_KT, "wkf", 1.0, True),
            (W_VT, "wvf", 1.0, True),
        ]
        def dma_g(dst, srcap):
            r.op("gpsimd", lambda e, dst=dst, srcap=srcap: e.dma_start(
                out=dst, in_=srcap))
            return r.mark("gpsimd", "sD2", 16)

        stg_tok = [t_bias, None]
        bf_tok = [None, None]
        t_tstage_free = None
        for widx, wname, scale, transp in conv_specs:
            wsrc = wsrcs[wname]
            for cc in range(CC):
                par = cc % 2
                r.wait("sync", stg_tok[par])
                t_ld = dma(stgf[par][:, :], wsrc[cc * 128:(cc + 1) * 128, :])
                r.waits("vector", [t_ld, bf_tok[par]])
                r.op("vector", lambda e, par=par, scale=scale: e.tensor_scalar(
                    out=stgbf[par][:, :], in0=stgf[par][:, :], scalar1=scale,
                    scalar2=None, op0=ALU.mult))
                t_cv = r.mark("vector")
                stg_tok[par] = t_cv
                if not transp:
                    r.wait("sync", t_cv)
                    bf_tok[par] = dma(wbf[widx, :, cc * DM:(cc + 1) * DM],
                                      stgbf[par][:, :])
                else:
                    pi = st["ptr_i"][0]; st["ptr_i"][0] ^= 1
                    ptr = ptrs[pi]
                    r.waits("tensor", [t_cv, t_ident, st["ptr_free"][pi]])
                    for j in range(8):
                        r.op("tensor", lambda e, par=par, j=j, ptr=ptr: e.transpose(
                            ptr[:, j * 128:(j + 1) * 128],
                            stgbf[par][:, j * 128:(j + 1) * 128], ident[:]))
                    t_tr = r.mark("tensor")
                    bf_tok[par] = t_tr
                    r.waits("scalar", [t_tr, t_tstage_free])
                    r.op("scalar", lambda e, ptr=ptr: e.activation(out=tstage[:, :],
                                                          in_=ptr[:, :], func=AF.Copy))
                    t_ev = r.mark("scalar")
                    st["ptr_free"][pi] = t_ev
                    r.wait("sync", t_ev)
                    t_tstage_free = dma(
                        wbf[widx][:, :].rearrange("p (j d) -> p j d", j=8)
                        [:, :, cc * 128:(cc + 1) * 128],
                        tstage[:, :].rearrange("p (j d) -> p j d", j=8))

        slot_tok = [None] * 4
        r.waits("sync", [("sD", r.semc["sD"]), ("sD2", r.semc["sD2"])])
        for si, widx in ((0, W_RT), (1, W_KT), (2, W_VT), (S_WO, W_O)):
            slot_tok[si] = dma4(slots[si], widx)

        x_tok = [None, None]
        x_free = [None, None]

        def load_x(c):
            par = c % 2
            r.wait("sync", x_free[par])
            x_tok[par] = dma(xbuf[par][:, :, :],
                             x_in[c * TCH:(c + 1) * TCH, :].rearrange(
                                 "(t2 p) d -> p t2 d", p=128))

        load_x(0)

        def emit_ln_v(par, t1, deps, lt, lt_free):
            xb = xbuf[par]
            r.waits("vector", deps + [lt_free])
            r.op("vector", lambda e: e.bn_stats(bn6[:, 0:6], xb[:, t1, 0:512]))
            r.op("vector", lambda e: e.bn_stats(bn6[:, 6:12], xb[:, t1, 512:1024]))
            r.chain("vector")
            r.op("vector", lambda e: e.bn_aggr(ag2[:], bn6[:]))
            r.chain("vector")
            r.op("vector", lambda e: e.tensor_scalar(
                out=v1, in0=ag2[:, 1:2], scalar1=1e-5, scalar2=None, op0=ALU.add))
            r.chain("vector")
            v1i = v1.bitcast(I32)
            y1i = y1.bitcast(I32)
            r.op("vector", lambda e: e.tensor_copy(out=t1b, in_=v1i))
            r.chain("vector")
            r.op("vector", lambda e: e.tensor_scalar(
                out=t1b, in0=t1b, scalar1=-0.5, scalar2=float(QUAKE),
                op0=ALU.mult, op1=ALU.add))
            r.chain("vector")
            r.op("vector", lambda e: e.tensor_copy(out=y1i, in_=t1b[:]))
            r.chain("vector")
            for _ in range(2):
                r.op("vector", lambda e: e.tensor_tensor(out=t1b, in0=y1,
                                                         in1=y1, op=ALU.mult))
                r.chain("vector")
                r.op("vector", lambda e: e.tensor_tensor(out=t1b, in0=t1b,
                                                         in1=v1, op=ALU.mult))
                r.chain("vector")
                r.op("vector", lambda e: e.tensor_scalar(
                    out=t1b, in0=t1b, scalar1=-0.5, scalar2=1.5,
                    op0=ALU.mult, op1=ALU.add))
                r.chain("vector")
                r.op("vector", lambda e: e.tensor_tensor(out=y1, in0=y1,
                                                         in1=t1b, op=ALU.mult))
                r.chain("vector")
            r.op("vector", lambda e: e.tensor_scalar(
                out=lt[:, :], in0=xb[:, t1, :], scalar1=ag2[:, 0:1],
                scalar2=y1, op0=ALU.subtract, op1=ALU.mult))
            return r.mark("vector")

        def emit_lnT(li, t1, t_ln, carry, save_carry, lt):
            pi = st["ptr_i"][0]
            st["ptr_i"][0] ^= 1
            ptr = ptrs[pi]
            r.waits("tensor", [t_ln, t_ident, st["ptr_free"][pi],
                               st["lnT_users"][li]])
            for j in range(8):
                r.op("tensor", lambda e, j=j, lt=lt, ptr=ptr: e.transpose(
                    ptr[:, j * 128:(j + 1) * 128],
                    lt[:, j * 128:(j + 1) * 128], ident[:]))
            t_tr = r.mark("tensor")
            r.wait("scalar", t_tr)
            r.op("scalar", lambda e, li=li, t1=t1, ptr=ptr: e.activation(
                out=lnTs[li][:, :, 1 + t1 * 128:1 + (t1 + 1) * 128],
                in_=ptr[:].rearrange("p (j t) -> p j t", j=8), func=AF.Copy))
            tk = r.mark("scalar")
            st["lnT_evac"][li][t1] = tk
            st["ptr_free"][pi] = tk
            if t1 == 0 and carry is not None:
                r.waits("vector", [st["lnT_users"][li]])
                r.op("vector", lambda e, li=li, carry=carry: e.tensor_copy(
                    out=lnTs[li][:, :, 0:1], in_=carry[:].unsqueeze(2)))
                st["lnT_carry"][li] = r.mark("vector")
            if t1 == NT1 - 1 and save_carry is not None:
                r.waits("vector", [tk])
                r.op("vector", lambda e, li=li, save_carry=save_carry: e.tensor_copy(
                    out=save_carry[:].unsqueeze(2), in_=lnTs[li][:, :, 512:513]))
                r.mark("vector")
            return tk, t_tr

        def ffn_pieces(par, li, carry, save_carry, x_dep, res_cb, name,
                       role, p2_need=None, on_mix_done=None):
            toks = {}
            lt, xr_b, xk_b, ka_b, kT_b, u_b = (role["lt"], role["xr"], role["xk"],
                                               role["ka"], role["kT"], role["u"])
            q = []

            def piece(need, fn):
                q.append((need, fn))

            def mk_P1(t1):
                def go():
                    toks[("ln", t1)] = emit_ln_v(par, t1, [x_dep(t1)],
                                                 lt, role["lt_free"])
                return go

            def mk_P2(t1):
                def go():
                    _, t_tr = emit_lnT(li, t1, toks[("ln", t1)], carry,
                                       save_carry, lt)
                    role["lt_free"] = t_tr
                return go

            def mk_P3(t1):
                def go():
                    deps = [st["lnT_evac"][li][t1], t_mu,
                            toks.get(("mm_r", t1 - 1)), toks.get(("mm_k", t1 - 1)),
                            toks.get(("v_done", t1 - 1))]
                    if t1 > 0:
                        deps.append(st["lnT_evac"][li][t1 - 1])
                    else:
                        deps.append(st["lnT_carry"][li])
                        deps.append(role.get("prev_mm_r"))
                        deps.append(role.get("prev_mm_k"))
                        deps.append(role.get("prev_v_done"))
                    r.waits("vector", deps)
                    xr = xr_b[:].rearrange("p (cc t) -> p cc t", cc=CC)
                    xk = xk_b[:].rearrange("p (cc t) -> p cc t", cc=CC)
                    lo = 1 + t1 * 128
                    tv = mixtv[:].rearrange("p (a t) -> p a t", a=4)
                    for dst, muT, onemu in ((xr, murT, onemur), (xk, mukT, onemuk)):
                        for g in range(2):
                            r.waits("scalar", [st["mixtv_free"]])
                            for cc in range(g * 4, g * 4 + 4):
                                r.op("scalar", lambda e, muT=muT, cc=cc, lo=lo,
                                     li=li, tv=tv: e.activation(
                                         out=tv[:, cc % 4, :],
                                         in_=lnTs[li][:, cc, lo:lo + 128],
                                         func=AF.Copy,
                                         scale=muT[:, cc:cc + 1]))
                            t_p1 = r.mark("scalar")
                            r.waits("vector", [t_p1])
                            for cc in range(g * 4, g * 4 + 4):
                                r.op("vector", lambda e, dst=dst, onemu=onemu, cc=cc,
                                     lo=lo, li=li, tv=tv: e.scalar_tensor_tensor(
                                         out=dst[:, cc, :],
                                         in0=lnTs[li][:, cc, lo - 1:lo + 127],
                                         scalar=onemu[:, cc:cc + 1],
                                         in1=tv[:, cc % 4, :], op0=ALU.mult,
                                         op1=ALU.add))
                            tok_mix = r.chain("vector")
                            st["mixtv_free"] = tok_mix
                    toks[("mix", t1)] = tok_mix
                    if t1 == NT1 - 1:
                        st["lnT_users"][li] = toks[("mix", t1)]
                        if on_mix_done is not None:
                            on_mix_done()
                return go

            def evac_r(t1, half, bank, t_mm):
                r.waits("scalar", [t_mm, role.get("u_free")])
                r.op("scalar", lambda e, bank=bank, half=half: e.activation(
                    out=u_b[:, half * 512:(half + 1) * 512], in_=bank[:, :],
                    func=AF.Tanh, scale=0.5))
                return r.mark("scalar")

            def evac_k(t1, half, bank, t_mm):
                r.waits("scalar", [t_mm, toks.get(("kaT_tr", t1 - 1)),
                                   role.get("prev_kaT_tr") if t1 == 0 else None])
                r.op("scalar", lambda e, bank=bank, half=half: e.activation(
                    out=ka_b[:, half * 512:(half + 1) * 512], in_=bank[:, :],
                    func=AF.Relu))
                t_relu = r.mark("scalar")
                r.waits("vector", [t_relu])
                r.op("vector", lambda e, half=half: e.tensor_tensor(
                    out=ka_b[:, half * 512:(half + 1) * 512],
                    in0=ka_b[:, half * 512:(half + 1) * 512],
                    in1=ka_b[:, half * 512:(half + 1) * 512], op=ALU.mult))
                return r.mark("vector")

            def evac_v(t1, half, bank, t_mm):
                r.waits("vector", [t_mm])
                r.op("vector", lambda e, bank=bank, half=half:
                     e.scalar_tensor_tensor(
                         out=tmpf[:, :], in0=u_b[:, half * 512:(half + 1) * 512],
                         scalar=1.0, in1=bank[:, :], op0=ALU.add, op1=ALU.mult))
                r.chain("vector")
                r.op("vector", lambda e, t1=t1, half=half: e.scalar_tensor_tensor(
                    out=xbuf[par][:, t1, half * 512:(half + 1) * 512],
                    in0=tmpf[:, :], scalar=0.25,
                    in1=xbuf[par][:, t1, half * 512:(half + 1) * 512],
                    op0=ALU.mult, op1=ALU.add))
                tk = r.mark("vector")
                if half == 1:
                    role["u_free"] = tk
                    st["x_last"] = tk
                    if res_cb is not None:
                        res_cb(t1, tk)
                return tk

            def mk_mm(t1, phase, stat_of, slot_i, evac):
                def go():
                    stat, t_stat = stat_of()
                    sv = stat[:].rearrange("p (cc t) -> p cc t", cc=CC)
                    r.waits("tensor", [t_stat, slot_tok[slot_i],
                                       st["pm_free"][2], st["pm_free"][3]])
                    for cc in range(CC):
                        for half in range(2):
                            r.op("tensor", lambda e, cc=cc, half=half, sv=sv,
                                 slot_i=slot_i: e.matmul(
                                     pm[2 + half][:, :], lhsT=sv[:, cc, :],
                                     rhs=slots[slot_i][:, cc * DM + half * 512:
                                                       cc * DM + (half + 1) * 512],
                                     start=(cc == 0), stop=(cc == CC - 1),
                                     skip_group_check=True))
                    t_mm = r.mark("tensor")
                    toks[("mm_" + phase, t1)] = t_mm
                    tks = [evac(t1, 0, pm[2], t_mm), evac(t1, 1, pm[3], t_mm)]
                    st["pm_free"][2] = tks[0]
                    st["pm_free"][3] = tks[1]
                    if phase == "v":
                        toks[("v_done", t1)] = tks[1]
                return go

            def mk_P6(t1):
                def go():
                    t_k = toks[("mm_k_done", t1)]
                    pi = st["ptr_i"][0]; st["ptr_i"][0] ^= 1
                    ptr = ptrs[pi]
                    r.waits("tensor", [t_k, st["ptr_free"][pi],
                                       toks.get(("mm_v", t1 - 1)),
                                       role.get("prev_mm_v") if t1 == 0 else None])
                    for j in range(8):
                        r.op("tensor", lambda e, j=j, ptr=ptr: e.transpose(
                            ptr[:, j * 128:(j + 1) * 128],
                            ka_b[:, j * 128:(j + 1) * 128], ident[:]))
                    t_tr = r.mark("tensor")
                    toks[("kaT_tr", t1)] = t_tr
                    r.wait("scalar", t_tr)
                    r.op("scalar", lambda e, ptr=ptr: e.activation(
                        out=kT_b[:, :], in_=ptr[:, :], func=AF.Copy))
                    tk = r.mark("scalar")
                    st["ptr_free"][pi] = tk
                    toks[("kaT", t1)] = tk
                return go

            for t1 in range(NT1):
                piece((name, "x", t1), mk_P1(t1))
                piece(p2_need, mk_P2(t1))
                piece(None, mk_P3(t1))
                piece((name, "slots"),
                      mk_mm(t1, "r", (lambda t1=t1: (xr_b, toks[("mix", t1)])),
                            0, evac_r))
                piece(None,
                      mk_mm(t1, "k", (lambda t1=t1: (xk_b, toks[("mix", t1)])),
                            1, evac_k))

                def fin_k(t1=t1):
                    toks[("mm_k_done", t1)] = ("sV", r.semc["sV"])
                piece(None, fin_k)
                piece(None, mk_P6(t1))
                piece(None,
                      mk_mm(t1, "v", (lambda t1=t1: (kT_b, toks[("kaT", t1)])),
                            2, evac_v))
            def fin_all():
                role["prev_mm_r"] = toks[("mm_r", NT1 - 1)]
                role["prev_mm_k"] = toks[("mm_k", NT1 - 1)]
                role["prev_mm_v"] = toks[("mm_v", NT1 - 1)]
                role["prev_kaT_tr"] = toks[("kaT_tr", NT1 - 1)]
                role["prev_v_done"] = toks[("v_done", NT1 - 1)]
            piece(None, fin_all)
            return q

        def ln_only_pieces(par, li, x_dep, out_toks, name, role):
            q = []
            toks = {}
            lt = role["lt"]

            def mk_P1(t1):
                def go():
                    toks[("ln", t1)] = emit_ln_v(par, t1, [x_dep(t1)],
                                                 lt, role["lt_free"])
                return go

            def mk_P2(t1):
                def go():
                    out_toks[t1], t_tr = emit_lnT(li, t1, toks[("ln", t1)],
                                                  None, None, lt)
                    role["lt_free"] = t_tr
                return go

            for t1 in range(NT1):
                q.append(([(name, "x", t1), (name, "lnok")], mk_P1(t1)))
                q.append((None, mk_P2(t1)))
            return q

        queues = []
        events = set()

        def drive(n=1):
            done = 0
            for _ in range(n):
                fired = False
                for qq in queues:
                    if not qq:
                        continue
                    need, fn = qq[0]
                    ok = (need is None or
                          (isinstance(need, list) and all(n in events for n in need))
                          or (not isinstance(need, list) and need in events))
                    if ok:
                        qq.pop(0)
                        fn()
                        fired = True
                        done += 1
                        break
                if not fired:
                    break
            return done

        def drain_all():
            import sys
            left = sum(len(qq) for qq in queues)
            if left:
                DRAIN_STATS.append(left)
            while True:
                if drive(1) == 0:
                    live = [qq[0][0] for qq in queues if qq]
                    assert not live, f"undrivable pieces: {live}"
                    break

        def attn(par, on_qkv_done, on_res):
            t_omm = {}
            t_aT_tr = {-1: None}

            def qkv_mms(t1):
                pp = t1
                toks_ev = {}
                t_mm = None
                for wi in range(3):
                    b0 = 2 if wi == 1 else 0
                    r.waits("tensor", [st["lnT_evac"][LNT_A][t1], slot_tok[wi],
                                       t_ones, t_bias,
                                       st["pm_free"][b0], st["pm_free"][b0 + 1]])
                    for cc in range(CC):
                        for half in range(2):
                            r.op("tensor", lambda e, t1=t1, cc=cc, half=half, wi=wi,
                                 b0=b0: e.matmul(
                                     pm[b0 + half][:, :],
                                     lhsT=lnTs[LNT_A][:, cc,
                                                      1 + t1 * 128:1 + (t1 + 1) * 128],
                                     rhs=slots[wi][:, cc * DM + half * 512:
                                                   cc * DM + (half + 1) * 512],
                                     start=(cc == 0), stop=False,
                                     skip_group_check=True))
                    for half in range(2):
                        r.op("tensor", lambda e, wi=wi, half=half, b0=b0: e.matmul(
                            pm[b0 + half][:, :], lhsT=ones_col[:, :],
                            rhs=biasrow[0:1, wi * DM + half * 512:
                                        wi * DM + (half + 1) * 512],
                            start=False, stop=True, skip_group_check=True))
                    t_mm = r.mark("tensor")
                    dst = (qts, kts, vts)[wi][pp]
                    r.waits("scalar", [t_mm, st["qkv_free"][pp]])

                    for half in range(2):
                        r.op("scalar", lambda e, dst=dst, half=half, b0=b0:
                             e.activation(
                                 out=dst[:, half * 512:(half + 1) * 512],
                                 in_=pm[b0 + half][:, :], func=AF.Copy))
                    t_ev = r.mark("scalar")
                    toks_ev[wi] = t_ev
                    st["pm_free"][b0] = t_ev
                    st["pm_free"][b0 + 1] = t_ev
                return (toks_ev[1], toks_ev[2]), t_mm

            def scores(t1, t_qkv):
                t_qk, t_v = t_qkv
                pp = t1
                qt, kt, vt = qts[pp], kts[pp], vts[pp]
                q_h = qt[:].rearrange("p (h q) -> p h q", h=H)
                k_h = kt[:].rearrange("p (h k) -> p h k", h=H)
                v_h = vt[:].rearrange("p (h k) -> p h k", h=H)
                d_h = d_tile[:].rearrange("p (h q) -> p h q", h=H)
                n_h = n_tile[:].rearrange("p (h q) -> p h q", h=H)

                r.waits("gpsimd", [st["E_free"][0], st["E_free"][1]])
                r.op("gpsimd", lambda e: e.memset(e_tiles[0][:].bitcast(I32), 0))
                r.op("gpsimd", lambda e: e.memset(e_tiles[1][:].bitcast(I32), 0))
                t_ez = r.mark("gpsimd")
                st["E_free"][0] = t_ez
                st["E_free"][1] = t_ez

                def s_band(b):
                    kmax = 4 * b + 4
                    sv = s_tiles[b % 2][:, 0:H * 4 * CH].rearrange(
                        "p (h j k) -> p h j k", h=H, j=4)
                    r.waits("vector", [t_qk, st["S_free"][b % 2]])
                    r.op("vector", lambda e, b=b, kmax=kmax, sv=sv, q_h=q_h, k_h=k_h:
                         e.tensor_tensor(
                             out=sv[:, :, :, 0:kmax],
                             in0=k_h[:, :, 0:kmax].unsqueeze(2).broadcast_to(
                                 [128, H, 4, kmax]),
                             in1=q_h[:, :, 4 * b:4 * b + 4].unsqueeze(3).broadcast_to(
                                 [128, H, 4, kmax]), op=ALU.mult))
                    return r.mark("vector")

                t_Sb = {0: s_band(0)}
                t_EN = None
                r.wait("vector", t_v)
                for b in range(16):
                    if DRIVE_IN_SCORES:
                        drive(1)
                    if b + 1 < 16:
                        t_Sb[b + 1] = s_band(b + 1)
                    kmax = 4 * b + 4
                    half = kmax // 2
                    ep = e_tiles[b % 2]
                    ev = ep[:].rearrange("p (h j k) -> p h j k", h=H, j=4)
                    sv = s_tiles[b % 2][:, 0:H * 4 * CH].rearrange(
                        "p (h j k) -> p h j k", h=H, j=4)
                    fo = s_tiles[b % 2][:, 0:H * 4 * half].rearrange(
                        "p (h j k) -> p h j k", h=H, j=4)
                    r.waits("scalar", [t_Sb[b], st["E_free"][b % 2]])
                    for j in range(4):
                        qq = 4 * b + j
                        r.op("scalar", lambda e, j=j, qq=qq, sv=sv, ev=ev: e.activation(
                            out=ev[:, :, j, 0:qq + 1], in_=sv[:, :, j, 0:qq + 1],
                            func=AF.Exp))
                    t_E = r.mark("scalar")
                    st["S_free"][b % 2] = t_E
                    r.wait("vector", t_E)
                    q4 = half // 2
                    r.op("vector", lambda e, ev=ev, fo=fo, half=half, kmax=kmax:
                         e.tensor_tensor(out=fo[:, :, :, 0:half],
                                         in0=ev[:, :, :, 0:half],
                                         in1=ev[:, :, :, half:kmax], op=ALU.add))
                    if q4 > 0:
                        r.op("vector", lambda e, fo=fo, q4=q4, half=half:
                             e.tensor_tensor(out=fo[:, :, :, 0:q4],
                                             in0=fo[:, :, :, 0:q4],
                                             in1=fo[:, :, :, q4:half], op=ALU.add))
                    q8 = 0
                    if q8 > 0:
                        r.op("vector", lambda e, fo=fo, q8=q8, q4=q4:
                             e.tensor_tensor(out=fo[:, :, :, 0:q8],
                                             in0=fo[:, :, :, 0:q8],
                                             in1=fo[:, :, :, q8:q4], op=ALU.add))
                    rw = q8 if q8 > 0 else q4
                    r.op("vector", lambda e, b=b, fo=fo, rw=rw, d_h=d_h:
                         e.tensor_reduce(
                             d_h[:, :, 4 * b:4 * b + 4].unsqueeze(3),
                             fo[:, :, :, 0:rw], axis=AX.X, op=ALU.add))
                    r.op("vector", lambda e, ev=ev, v_h=v_h, kmax=kmax:
                         e.tensor_tensor(
                             out=ev[:, :, :, 0:kmax], in0=ev[:, :, :, 0:kmax],
                             in1=v_h[:, :, 0:kmax].unsqueeze(2).broadcast_to(
                                 [128, H, 4, kmax]), op=ALU.mult))
                    r.op("vector", lambda e, ev=ev, fo=fo, half=half, kmax=kmax:
                         e.tensor_tensor(out=fo[:, :, :, 0:half],
                                         in0=ev[:, :, :, 0:half],
                                         in1=ev[:, :, :, half:kmax], op=ALU.add))
                    if q4 > 0:
                        r.op("vector", lambda e, fo=fo, q4=q4, half=half:
                             e.tensor_tensor(out=fo[:, :, :, 0:q4],
                                             in0=fo[:, :, :, 0:q4],
                                             in1=fo[:, :, :, q4:half], op=ALU.add))
                    if q8 > 0:
                        r.op("vector", lambda e, fo=fo, q8=q8, q4=q4:
                             e.tensor_tensor(out=fo[:, :, :, 0:q8],
                                             in0=fo[:, :, :, 0:q8],
                                             in1=fo[:, :, :, q8:q4], op=ALU.add))
                    r.op("vector", lambda e, b=b, fo=fo, rw=rw, n_h=n_h:
                         e.tensor_reduce(
                             n_h[:, :, 4 * b:4 * b + 4].unsqueeze(3),
                             fo[:, :, :, 0:rw], axis=AX.X, op=ALU.add))
                    t_EN = r.mark("vector")
                    st["E_free"][b % 2] = t_EN
                r.wait("vector", t_EN)
                d_rcp = big[:, 0:4096].bitcast(F32)
                r.op("vector", lambda e, d_rcp=d_rcp: e.reciprocal(out=d_rcp[:, :],
                                                                   in_=d_tile[:]))
                r.chain("vector")
                r.op("vector", lambda e, d_rcp=d_rcp, qt=qt: e.tensor_tensor(
                    out=qt[:, :], in0=n_tile[:], in1=d_rcp[:, :], op=ALU.mult))
                t_at = r.mark("vector")
                st["S_free"][0] = t_at
                return t_at

            def o_T(t1, t_at):
                pi = st["ptr_i"][0]; st["ptr_i"][0] ^= 1
                ptr = ptrs[pi]
                r.waits("tensor", [t_at, st["ptr_free"][pi]])
                for j in range(8):
                    r.op("tensor", lambda e, j=j, t1=t1, ptr=ptr: e.transpose(
                        ptr[:, j * 128:(j + 1) * 128],
                        qts[t1][:, j * 128:(j + 1) * 128], ident[:]))
                t_tr = r.mark("tensor")
                t_aT_tr[t1] = t_tr
                r.wait("scalar", t_tr)
                r.op("scalar", lambda e, t1=t1, ptr=ptr: e.activation(
                    out=kts[t1][:, :], in_=ptr[:, :], func=AF.Copy))
                t_aT = r.mark("scalar")
                st["ptr_free"][pi] = t_aT
                r.waits("tensor", [t_aT, slot_tok[S_WO],
                                   st["pm_free"][4], st["pm_free"][5]])
                aTv = kts[t1][:].rearrange("p (cc t) -> p cc t", cc=CC)
                for cc in range(CC):
                    for half in range(2):
                        r.op("tensor", lambda e, cc=cc, half=half: e.matmul(
                            pm[4 + half][:, :], lhsT=aTv[:, cc, :],
                            rhs=slots[S_WO][:, cc * DM + half * 512:
                                            cc * DM + (half + 1) * 512],
                            start=(cc == 0), stop=False, skip_group_check=True))
                for half in range(2):
                    r.op("tensor", lambda e, half=half: e.matmul(
                        pm[4 + half][:, :], lhsT=ones_col[:, :],
                        rhs=biasrow[0:1, 3 * DM + half * 512:
                                    3 * DM + (half + 1) * 512],
                        start=False, stop=True, skip_group_check=True))
                t_omm[t1] = r.mark("tensor")

            def o_V(t1):
                r.wait("vector", t_omm[t1])
                r.op("vector", lambda e, t1=t1: e.tensor_tensor(
                    out=xbuf[par][:, t1, 0:512], in0=pm[4][:, :],
                    in1=xbuf[par][:, t1, 0:512], op=ALU.add))
                r.op("vector", lambda e, t1=t1: e.tensor_tensor(
                    out=xbuf[par][:, t1, 512:1024], in0=pm[5][:, :],
                    in1=xbuf[par][:, t1, 512:1024], op=ALU.add))
                tk = r.mark("vector")
                st["x_last"] = tk
                st["pm_free"][4] = tk
                st["pm_free"][5] = tk
                st["qkv_free"][t1] = tk
                on_res(t1, tk)

            t_qkv = {}
            for t1 in range(NT1):
                t_qkv[t1], mm = qkv_mms(t1)
                if t1 == NT1 - 1:
                    on_qkv_done(mm)
            for t1 in range(NT1):
                t_at = scores(t1, t_qkv[t1])
                o_T(t1, t_at)
                if DRIVE_IN_SCORES:
                    drive(1)
                o_V(t1)

        role_f1 = {"lt": ln_t3[0], "xr": xrT2[0], "xk": xkT2[0], "ka": kact2[0],
                   "kT": kaT2[0], "u": xrT2[0], "lt_free": None, "u_free": None}
        role_f2 = {"lt": ln_t3[1], "xr": xrT2[1], "xk": xkT2[1], "ka": kact2[1],
                   "kT": kaT2[1], "u": xrT2[1], "lt_free": None, "u_free": None}
        role_l2 = role_f1

        f1res_prev = [None] * NT1

        def res_store(lst, evset, nm):
            def cb(t1, tok):
                lst[t1] = tok
                evset.add((nm, "x", t1))
            return cb

        events.add(("f1", "slots"))
        events.add(("f1", "lnt"))
        for t1 in range(NT1):
            events.add(("f1", "x", t1))
        q1 = ffn_pieces(0, LNT_A, carry1, carry1, lambda t1: x_tok[0],
                        res_store(f1res_prev, events, "l2"), "f1", role_f1,
                        p2_need=("f1", "lnt"),
                        on_mix_done=lambda: events.add(("l2", "lnok")))
        queues.append(q1)
        ln2_out = [None] * NT1
        q2 = ln_only_pieces(0, LNT_A, lambda t1: f1res_prev[t1], ln2_out, "l2",
                            role_l2)
        queues.append(q2)
        drain_all()
        tok_T0 = ("sT", r.semc["sT"])
        for si, widx in ((0, W_Q), (1, W_K), (2, W_V)):
            r.wait("sync", tok_T0)
            slot_tok[si] = dma4(slots[si], widx)

        for c in range(NCHUNK):
            par = c % 2
            events.clear()
            queues.clear()

            nm_f1 = "f1n"
            nm_f2 = "f2c"
            nm_l2 = "l2n"
            res2 = [None] * NT1
            f1res = [None] * NT1

            if c + 1 < NCHUNK:
                load_x(c + 1)
                for t1 in range(NT1):
                    events.add((nm_f1, "x", t1))
                queues.append(ffn_pieces((c + 1) % 2, LNT_A, carry1, carry1,
                                         lambda t1, p=(c + 1) % 2: x_tok[p],
                                         res_store(f1res, events, nm_l2), nm_f1,
                                         role_f1, p2_need=(nm_f1, "lnt"),
                                         on_mix_done=lambda: events.add(
                                             (nm_l2, "lnok"))))
            f2q = ffn_pieces(par, LNT_F, carry3, carry3,
                             lambda t1: res2[t1], None, nm_f2, role_f2)
            queues.append(f2q)
            ln2n = [None] * NT1
            if c + 1 < NCHUNK:
                queues.append(ln_only_pieces((c + 1) % 2, LNT_A,
                                             lambda t1: f1res[t1], ln2n, nm_l2,
                                             role_l2))

            def on_qkv_done(last_mm, c=c):
                st["lnT_users"][LNT_A] = last_mm
                for si, widx in ((0, W_RT), (1, W_KT), (2, W_VT)):
                    r.wait("sync", last_mm)
                    slot_tok[si] = dma4(slots[si], widx)
                events.add((nm_f1, "slots"))
                events.add((nm_f1, "lnt"))
                events.add((nm_f2, "slots"))

            def on_res(t1, tok):
                res2[t1] = tok
                events.add((nm_f2, "x", t1))

            if TEST_STAGE >= 2:
                attn(par, on_qkv_done, on_res)
            else:
                events.add((nm_f2, "slots"))
                for t1 in range(NT1):
                    events.add((nm_f2, "x", t1))
                    res2[t1] = st["x_last"]
            if TEST_STAGE < 3:
                queues[:] = [q for q in queues if q is not f2q]
            drain_all()

            if c + 1 < NCHUNK:
                slot_deps = ((role_f2["prev_mm_r"], role_f1.get("prev_mm_r")),
                             (role_f2["prev_mm_k"], role_f1.get("prev_mm_k")),
                             (role_f2["prev_mm_v"], role_f1.get("prev_mm_v")))
                for si, widx in ((0, W_Q), (1, W_K), (2, W_V)):
                    r.waits("sync", list(slot_deps[si]))
                    slot_tok[si] = dma4(slots[si], widx)

            r.wait("sync", st["x_last"])
            x_free[par] = dma(out[c * TCH:(c + 1) * TCH, :].rearrange(
                "(t2 p) d -> p t2 d", p=128), xbuf[par][:, :, :])

        r.op("sync", lambda e: e.nop())
        r.wait("sync", ("sD", r.semc["sD"]))
        r.wait("sync", ("sD2", r.semc["sD2"]))

        block = es.enter_context(nc.Block())
        r.emit(block)

    return nc


_cache = {}


def kernel(L0_attn_x, L0_attn_Wq, L0_attn_bq, L0_attn_Wk, L0_attn_bk, L0_attn_Wv,
           L0_attn_bv, L0_attn_Wo, L0_attn_bo, L0_ffn_W_r, L0_ffn_W_k, L0_ffn_W_v,
           L0_ffn_mu_r, L0_ffn_mu_k):
    x = np.ascontiguousarray(np.asarray(L0_attn_x, dtype=np.float32))
    ws = {
        "wq": L0_attn_Wq, "bq": L0_attn_bq, "wk": L0_attn_Wk, "bk": L0_attn_bk,
        "wv": L0_attn_Wv, "bv": L0_attn_bv, "wo": L0_attn_Wo, "bo": L0_attn_bo,
        "wr": L0_ffn_W_r, "wkf": L0_ffn_W_k, "wvf": L0_ffn_W_v,
        "mur": L0_ffn_mu_r, "muk": L0_ffn_mu_k,
    }
    ws = {k: np.ascontiguousarray(np.asarray(v, dtype=np.float32)) for k, v in ws.items()}
    if "nc" not in _cache:
        _cache["nc"] = build_program()
    nc = _cache["nc"]
    in_maps = [dict(ws, x=x[i]) for i in range(8)]
    res = run_bass_kernel_spmd(nc, in_maps, core_ids=list(range(8)))
    return np.stack([np.asarray(res.results[i]["out"]) for i in range(8)], axis=0)
